# version 02
# baseline (speedup 1.0000x reference)
import numpy as np
import ml_dtypes
from contextlib import ExitStack
import concourse.bass as bass
import concourse.mybir as mybir
from concourse.bass_utils import run_bass_kernel_spmd

F32 = mybir.dt.float32
BF16 = mybir.dt.bfloat16
FP16 = mybir.dt.float16
I32 = mybir.dt.int32
AF = mybir.ActivationFunctionType
ALU = mybir.AluOpType
AX = mybir.AxisListType

B, S, D, H, HD = 2, 4096, 2048, 16, 128
TPC = 1024
NT = TPC // 128
ND = D // 128
F1, F2 = 16384, 8192
C32 = 12582912.0
C64 = 6755399441055744.0
SM_SCALE = float(1.0 / np.sqrt(np.float64(HD)))
BLK1 = 2 * D * TPC
V_OFF = D * TPC

STAGE = 99
TAPS = []


class KB:

    def __init__(self, nc, block, ctx, sems):
        self.nc, self.block, self.ctx, self.sems = nc, block, ctx, sems

    def inc(self, sem, instr, k=1):
        instr.then_inc(self.sems[sem][0], k)
        self.sems[sem][1] += k

    def incd(self, sem, instr):
        self.inc(sem, instr, 16)

    def n(self, sem):
        return self.sems[sem][1]

    def wait(self, eng, sem, val=None):
        eng.wait_ge(self.sems[sem][0], self.sems[sem][1] if val is None else val)

    def on(self, engname, fn):
        getattr(self.block, engname)(fn)

    def barrier(self):
        engs = ("sync", "vector", "scalar", "tensor", "gpsimd")
        for en in engs:
            def f(eng):
                eng.drain()
                self.inc("ph", eng.nop())
            self.on(en, f)
        t = self.n("ph")
        for en in engs:
            def f(eng, t=t):
                eng.wait_ge(self.sems["ph"][0], t)
            self.on(en, f)


def build_program(taps=()):
    nc = bass.Bass("TRN2", target_bir_lowering=False, debug=False)

    def di(name, shape, dt=F32):
        return nc.dram_tensor(name, shape, dt, kind="ExternalInput").ap()

    x_t = di("x_t", [D, TPC])
    cst = di("cst", [128, 12])
    vecs = di("vecs", [128, 5 * ND])
    negd = di("negd", [128, 64])
    ofs = di("ofs", [1, 96], I32)
    wq0 = di("wq0", [D, D]); wk0 = di("wk0", [D, D]); wv0 = di("wv0", [D, D])
    wo0 = di("wo0", [D, D])
    wgvT = di("wgvT", [D, F1])
    woutP = di("woutP", [16 * F2, 128])
    wq1 = di("wq1", [D, D]); wk1 = di("wk1", [D, D]); wv1 = di("wv1", [D, D])
    wo1 = di("wo1", [D, D])
    wgT = di("wgT", [D, F2]); wuT = di("wuT", [D, F2])
    wdP = di("wdP", [16 * F2, 128])
    WQ = [wq0, wq1]; WK = [wk0, wk1]; WV = [wv0, wv1]; WO = [wo0, wo1]

    y_t = nc.dram_tensor("y_t", [D, TPC], F32, kind="ExternalOutput").ap()
    tap_out = {nm: nc.dram_tensor("tap_" + nm, sh, dt, kind="ExternalOutput").ap()
               for nm, sh, dt in taps}

    def dd(name, shape, dt):
        return nc.dram_tensor(name, shape, dt, kind="Internal").ap()

    ag1_in = [dd(f"ag1_in{L}", [2 * D, TPC], BF16) for L in range(2)]
    ag1_out = [dd(f"ag1_out{L}", [8 * D, TPC], BF16) for L in range(2)]
    ag2_in = [dd(f"ag2_in{L}", [4 * HD, S], BF16) for L in range(2)]
    ag2_out = [dd(f"ag2_out{L}", [D, S], BF16) for L in range(2)]
    ar_in = dd("ar_in", [1, 2], F32)
    ar_out = dd("ar_out", [1, 2], F32)
    t16_hbm = dd("t16_hbm", [NT * D * 128], FP16)
    qi_hbm = dd("qi_hbm", [ND * TPC * 128], BF16)
    u_hbm = dd("u_hbm", [64 * TPC * 128], FP16)
    p_hbm = dd("p_hbm", [64 * TPC * 128], BF16)
    x_hbm = dd("x_hbm", [D * TPC], F32)
    tern_hbm = dd("tern_hbm", [16 * F2, 128], BF16)
    row_hbm = dd("row_hbm", [4 * TPC], F32)

    with ExitStack() as ctx:
        E = ctx.enter_context
        X = E(nc.sbuf_tensor("X", [128, ND * TPC], F32))
        CS = E(nc.sbuf_tensor("CS", [128, 12], F32))
        VEC = E(nc.sbuf_tensor("VEC", [128, 5 * ND], F32))
        OFS = E(nc.sbuf_tensor("OFS", [1, 96], I32))
        IND = E(nc.sbuf_tensor("IND", [128, 2], F32))
        ONE1 = E(nc.sbuf_tensor("ONE1", [1, 128], F32))
        ONEF = E(nc.sbuf_tensor("ONEF", [128, 1], F32))
        ONEB = E(nc.sbuf_tensor("ONEB", [128, 2], BF16))
        TAU = E(nc.sbuf_tensor("TAU", [128, 8], F32))
        STT = E(nc.sbuf_tensor("STT", [128, 64], F32))
        ROW = E(nc.sbuf_tensor("ROW", [1, 4 * TPC], F32))
        NEGD = E(nc.sbuf_tensor("NEGD", [128, 64], F32))
        XN = None
        QT = None
        IOT0 = E(nc.sbuf_tensor("IOT0", [128, 512], F32))

        sems = {}
        for s in ("ph", "xld", "wld", "stat", "stA", "stB", "sgA", "sgB",
                  "shA", "shB", "ag", "mm", "dve", "act", "gp", "st", "tp",
                  "tpA", "tpB", "aux", "ev", "pq", "pr"):
            sems[s] = [E(nc.semaphore("sem_" + s)), 0]
        block = E(nc.Block())
        kb = KB(nc, block, ctx, sems)
        inc, incd, wait, on, n_ = kb.inc, kb.incd, kb.wait, kb.on, kb.n
        _uq = [0]
        def sbname(base):
            _uq[0] += 1
            return f"{base}_{_uq[0]}"

        def tap(name, src_ap):
            if name not in tap_out:
                return
            def f(sync):
                incd("st", sync.dma_start(out=tap_out[name][:, :], in_=src_ap))
                wait(sync, "st")
            on("sync", f)
            kb.barrier()

        def f(sync):
            incd("xld", sync.dma_start(out=CS[:, :], in_=cst[:, :]))
            incd("xld", sync.dma_start(out=VEC[:, :], in_=vecs[:, :]))
            incd("xld", sync.dma_start(out=OFS[:, :], in_=ofs[:, :]))
            incd("xld", sync.dma_start(out=NEGD[:, :], in_=negd[:, :]))
            incd("xld", sync.dma_start(
                out=bass.AP(X, 0, [[ND * TPC, 128], [TPC, ND], [1, TPC]]),
                in_=bass.AP(x_t.tensor, 0, [[TPC, 128], [128 * TPC, ND], [1, TPC]])))
        on("sync", f)

        def f(vector):
            wait(vector, "xld")
            vector.tensor_scalar(IND[:, 1:2], CS[:, 2:3], CS[:, 1:2], None, ALU.add)
            vector.drain()
            vector.tensor_scalar(IND[:, 0:1], IND[:, 1:2], CS[:, 1:2], None, ALU.subtract)
            inc("dve", vector.memset(ONE1[:, :], 1.0))
            inc("dve", vector.memset(ONEF[:, :], 1.0))
            inc("dve", vector.memset(ONEB[:, :], 1.0))
            inc("dve", vector.memset(STT[:, :], 0.0))
            inc("dve", vector.memset(ROW[0:1, :], 0.0))
        on("vector", f)

        def f(gpsimd):
            inc("gp", gpsimd.iota(IOT0[:, :], [[1, 512]], base=0,
                                  channel_multiplier=-1,
                                  allow_small_or_imprecise_dtypes=True))
        on("gpsimd", f)



        def stats_phase2():
            with ExitStack() as c2:
                WS = c2.enter_context(nc.sbuf_tensor(sbname("WS"), [128, 2 * 4096], F32))
                ACC = c2.enter_context(nc.sbuf_tensor(sbname("ACC"), [128, 8], F32))
                PSA = c2.enter_context(nc.psum_tensor(sbname("PSA"), [128, 8], F32))
                pieces = []
                for j in range(8):
                    pieces.append((wgvT, 3, (j // 4) * 128 * F1 + (j % 4) * 4096,
                                   [[F1, 128], [1, 4096]], 0))
                for j in range(4):
                    pieces.append((woutP, 4, j * 32 * 128 * 128,
                                   [[128, 128], [128 * 128, 32], [1, 128]], 1))

                def g(gpsimd):
                    wait(gpsimd, "xld")
                    for i, (src, col, off, dims, _w) in enumerate(pieces):
                        if i >= 2:
                            wait(gpsimd, "dve", n_("dve") - 0)
                red_base = n_("dve")
                for i, (src, col, off, dims, w_id) in enumerate(pieces):
                    def gl(gpsimd, i=i, src=src, dims=dims):
                        if i == 0:
                            wait(gpsimd, "xld")
                        if i >= 2:
                            wait(gpsimd, "dve", red_base + (i - 1))
                        with gpsimd.register() as r1:
                            gpsimd.reg_load(r1, OFS[0:1, 72 + i:73 + i])
                            incd("stA" if i % 2 == 0 else "stB", gpsimd.dma_start(
                                out=WS[:, (i % 2) * 4096:(i % 2 + 1) * 4096],
                                in_=bass.AP(src.tensor, r1, dims)))
                    on("gpsimd", gl)

                    def vr(vector, i=i, w_id=w_id):
                        wait(vector, "stA" if i % 2 == 0 else "stB",
                             16 * (i // 2 + 1))
                        if i == 0:
                            vector.memset(ACC[:, 0:2], 0.0)
                        vector.reduce_sum(ACC[:, 4 + (i % 2):5 + (i % 2)],
                                          WS[:, (i % 2) * 4096:(i % 2 + 1) * 4096],
                                          axis=AX.X, apply_absolute_value=True)
                        vector.drain()
                        inc("dve", vector.tensor_tensor(
                            ACC[:, w_id:w_id + 1], ACC[:, w_id:w_id + 1],
                            ACC[:, 4 + (i % 2):5 + (i % 2)], ALU.add))
                        vector.drain()
                    on("vector", vr)

                def vv(vector):
                    vector.drain()
                    inc("dve", vector.tensor_copy(ACC[:, 2:4], ACC[:, 0:2]))
                on("vector", vv)

                def t(tensor):
                    wait(tensor, "dve")
                    inc("mm", tensor.matmul(PSA[0:1, 0:2], ONEF[:, 0:1], ACC[:, 2:4],
                                            start=True, stop=True))
                on("tensor", t)

                def v3(vector):
                    wait(vector, "mm")
                    inc("dve", vector.tensor_copy(ACC[0:1, 0:2], PSA[0:1, 0:2]))
                on("vector", v3)

                def s(sync):
                    wait(sync, "dve")
                    incd("stat", sync.dma_start(out=ar_in[0:1, :], in_=ACC[0:1, 0:2]))
                on("sync", s)

                def g3(gpsimd):
                    wait(gpsimd, "stat")
                    inc("ag", gpsimd.collective_compute(
                        "AllReduce", ALU.add,
                        replica_groups=[[0, 1, 2, 3, 4, 5, 6, 7]],
                        ins=[ar_in[:, :]], outs=[ar_out[:, :]]))
                on("gpsimd", g3)

                def s2(sync):
                    wait(sync, "ag")
                    incd("stat", sync.dma_start(out=ACC[0:1, 0:2], in_=ar_out[0:1, :]))
                on("sync", s2)

                def v4(vector):
                    wait(vector, "stat")
                    vector.tensor_scalar(ACC[0:1, 0:1], ACC[0:1, 0:1],
                                         1.0 / (D * F1), None, ALU.mult)
                    vector.tensor_scalar(ACC[0:1, 1:2], ACC[0:1, 1:2],
                                         1.0 / (D * F2), None, ALU.mult)
                    vector.drain()
                    vector.tensor_scalar(ACC[0:1, 0:2], ACC[0:1, 0:2],
                                         CS[0:1, 3:4], None, ALU.max)
                    vector.drain()
                    inc("dve", vector.tensor_scalar(ACC[0:1, 2:4], ACC[0:1, 0:2],
                                                    0.5, None, ALU.mult))
                on("vector", v4)

                def t3(tensor):
                    wait(tensor, "dve")
                    inc("mm", tensor.matmul(PSA[:, 4:8], ONE1[0:1, :], ACC[0:1, 0:4],
                                            start=True, stop=True))
                on("tensor", t3)

                def v5(vector):
                    wait(vector, "mm")
                    vector.tensor_copy(TAU[:, 4:6], PSA[:, 4:6])
                    vector.tensor_copy(TAU[:, 0:1], PSA[:, 6:7])
                    vector.tensor_copy(TAU[:, 2:3], PSA[:, 7:8])
                    vector.drain()
                    inc("dve", vector.tensor_scalar(TAU[:, 1:2], TAU[:, 0:1],
                                                    -1.0, None, ALU.mult))
                    inc("dve", vector.tensor_scalar(TAU[:, 3:4], TAU[:, 2:3],
                                                    -1.0, None, ALU.mult))
                on("vector", v5)
            kb.barrier()
        stats_phase2()

        def emit_ln(src, src_bf16, eps_col, fold_R, h_out):
            with ExitStack() as c2:
                CB = c2.enter_context(nc.sbuf_tensor(sbname("lnCB"), [128, 2 * TPC], BF16))
                SQ = c2.enter_context(nc.sbuf_tensor(sbname("lnSQ"), [128, 2 * TPC], BF16))
                PS1 = c2.enter_context(nc.psum_tensor(sbname("lnPS1"), [1, TPC], F32))
                PS2 = c2.enter_context(nc.psum_tensor(sbname("lnPS2"), [1, TPC], F32))
                PSB = c2.enter_context(nc.psum_tensor(sbname("lnPSB"), [128, 2 * TPC], F32))
                HW16 = c2.enter_context(nc.sbuf_tensor(sbname("lnHW16"), [128, 2 * TPC], FP16))
                TMPA = c2.enter_context(nc.sbuf_tensor(sbname("lnTA"), [128, TPC], F32))
                TMPB = c2.enter_context(nc.sbuf_tensor(sbname("lnTB"), [128, TPC], F32))
                a0, d0, m0 = n_("act"), n_("dve"), n_("mm")
                tpa0, tpb0 = n_("tpA"), n_("tpB")
                for c in range(ND):
                    sl = slice(c * TPC, (c + 1) * TPC)
                    bsl = slice((c % 2) * TPC, (c % 2 + 1) * TPC)
                    if not src_bf16:
                        def vc(vector, c=c, sl=sl, bsl=bsl):
                            if c >= 2:
                                wait(vector, "mm", m0 + 4 * (c - 1))
                            inc("dve", vector.tensor_copy(CB[:, bsl], src[:, sl]))
                        on("vector", vc)
                    def sc(scalar, c=c, sl=sl, bsl=bsl):
                        if c == 0:
                            wait(scalar, "xld")
                        if c >= 2:
                            wait(scalar, "mm", m0 + 4 * (c - 1))
                        inc("act", scalar.activation(SQ[:, bsl], src[:, sl], AF.Square))
                    on("scalar", sc)
                    def tc(tensor, c=c, bsl=bsl):
                        if not src_bf16:
                            wait(tensor, "dve", d0 + (c + 1))
                        wait(tensor, "act", a0 + (c + 1))
                        for h in range(2):
                            hs = slice((c % 2) * TPC + 512 * h, (c % 2) * TPC + 512 * (h + 1))
                            os_ = slice(512 * h, 512 * (h + 1))
                            src_sl = (CB[:, hs] if not src_bf16 else
                                      src[:, c * TPC + 512 * h: c * TPC + 512 * (h + 1)])
                            inc("mm", tensor.matmul(PS1[0:1, os_], ONEB[:, 0:1], src_sl,
                                                    start=(c == 0), stop=(c == ND - 1)))
                            inc("mm", tensor.matmul(PS2[0:1, os_], ONEB[:, 0:1], SQ[:, hs],
                                                    start=(c == 0), stop=(c == ND - 1)))
                    on("tensor", tc)

                def ve(vector):
                    wait(vector, "mm")
                    vector.tensor_copy(ROW[0:1, 0:TPC], PS1[0:1, :])
                    vector.tensor_copy(ROW[0:1, TPC:2 * TPC], PS2[0:1, :])
                    vector.drain()
                    vector.tensor_scalar(ROW[0:1, 0:TPC], ROW[0:1, 0:TPC],
                                         CS[0:1, 5:6], None, ALU.mult)
                    vector.tensor_scalar(ROW[0:1, TPC:2 * TPC], ROW[0:1, TPC:2 * TPC],
                                         CS[0:1, 5:6], None, ALU.mult)
                    vector.drain()
                    vector.tensor_tensor(ROW[0:1, 2 * TPC:3 * TPC], ROW[0:1, 0:TPC],
                                         ROW[0:1, 0:TPC], ALU.mult)
                    vector.drain()
                    vector.tensor_tensor(ROW[0:1, TPC:2 * TPC], ROW[0:1, TPC:2 * TPC],
                                         ROW[0:1, 2 * TPC:3 * TPC], ALU.subtract)
                    vector.drain()
                    inc("dve", vector.tensor_scalar(
                        ROW[0:1, TPC:2 * TPC], ROW[0:1, TPC:2 * TPC],
                        CS[0:1, eps_col:eps_col + 1], None, ALU.add))
                on("vector", ve)

                def se(scalar):
                    wait(scalar, "dve")
                    scalar.activation(ROW[0:1, 2 * TPC:3 * TPC], ROW[0:1, TPC:2 * TPC],
                                      AF.Ln)
                    scalar.drain()
                    inc("act", scalar.activation(
                        ROW[0:1, TPC:2 * TPC], ROW[0:1, 2 * TPC:3 * TPC],
                        AF.Exp, scale=-0.5))
                on("scalar", se)

                def vf(vector):
                    wait(vector, "act")
                    inc("dve", vector.tensor_tensor(
                        ROW[0:1, 2 * TPC:3 * TPC], ROW[0:1, 0:TPC],
                        ROW[0:1, TPC:2 * TPC], ALU.mult))
                on("vector", vf)

                if h_out == 'h':
                    def vq(vector):
                        wait(vector, "act")
                        vector.tensor_tensor(ROW[0:1, 3 * TPC:4 * TPC],
                                             ROW[0:1, TPC:2 * TPC],
                                             ROW[0:1, TPC:2 * TPC], ALU.mult)
                        vector.drain()
                        inc("dve", vector.tensor_scalar(
                            ROW[0:1, 3 * TPC:4 * TPC], ROW[0:1, 3 * TPC:4 * TPC],
                            -1e-5, 1.0 + 1e-5, ALU.mult, ALU.add))
                    on("vector", vq)

                    def sq2(scalar):
                        wait(scalar, "dve")
                        scalar.activation(ROW[0:1, 3 * TPC:4 * TPC],
                                          ROW[0:1, 3 * TPC:4 * TPC], AF.Ln)
                        scalar.drain()
                        inc("act", scalar.activation(
                            ROW[0:1, 3 * TPC:4 * TPC], ROW[0:1, 3 * TPC:4 * TPC],
                            AF.Exp, scale=-0.5))
                    on("scalar", sq2)

                    def sq3(sync):
                        wait(sync, "act")
                        incd("tp", sync.dma_start(
                            out=bass.AP(row_hbm.tensor, 0, [[1, TPC]]),
                            in_=ROW[0:1, 3 * TPC:4 * TPC]))
                        wait(sync, "tp")
                        incd("tp", sync.dma_start(
                            out=STT[:, 32:40],
                            in_=bass.AP(row_hbm.tensor, 0, [[1, 128], [128, 8]]),
                            allow_slow_non_contiguous=True))
                        wait(sync, "tp")
                        inc("aux", sync.nop())
                    on("sync", sq3)

                def tb(tensor):
                    wait(tensor, "dve")
                    for h in range(2):
                        inc("mm", tensor.matmul(
                            PSB[:, 512 * h:512 * (h + 1)], ONE1[0:1, :],
                            ROW[0:1, TPC + 512 * h:TPC + 512 * (h + 1)],
                            start=True, stop=True))
                        inc("mm", tensor.matmul(
                            PSB[:, TPC + 512 * h:TPC + 512 * (h + 1)], ONE1[0:1, :],
                            ROW[0:1, 2 * TPC + 512 * h:2 * TPC + 512 * (h + 1)],
                            start=True, stop=True))
                on("tensor", tb)

                ms0 = n_("mm")
                for c in range(ND):
                    sl = slice(c * TPC, (c + 1) * TPC)
                    def va(vector, c=c, sl=sl):
                        if c == 0:
                            wait(vector, "mm", ms0)
                        vector.drain()
                        vector.tensor_tensor(TMPA[:, :], src[:, sl], PSB[:, 0:TPC],
                                             ALU.mult)
                        vector.drain()
                        if h_out == 'xq':
                            inc("dve", vector.tensor_tensor(TMPB[:, :], TMPA[:, :],
                                                            PSB[:, TPC:2 * TPC],
                                                            ALU.subtract))
                        elif fold_R:
                            vector.tensor_tensor(TMPB[:, :], TMPA[:, :],
                                                 PSB[:, TPC:2 * TPC], ALU.subtract)
                            vector.drain()
                            inc("dve", vector.tensor_tensor(X[:, sl], X[:, sl],
                                                            TMPB[:, :], ALU.add))
                        else:
                            inc("dve", vector.tensor_tensor(XN[:, sl], TMPA[:, :],
                                                            PSB[:, TPC:2 * TPC],
                                                            ALU.subtract))
                        if h_out == 'h':
                            vector.drain()
                            inc("dve", vector.tensor_scalar(
                                TMPA[:, :], TMPB[:, :],
                                VEC[:, 4 * ND + c:4 * ND + c + 1], None, ALU.mult))
                    on("vector", va)
                    if h_out is not None:
                        def vh(vector, c=c):
                            if c >= 2:
                                wait(vector, "tpA" if c % 2 == 0 else "tpB",
                                     tpa0 + 16 * ((c - 2) // 2 + 1) if c % 2 == 0
                                     else tpb0 + 16 * ((c - 2) // 2 + 1))
                            vector.drain()
                            inc("dve", vector.tensor_copy(
                                HW16[:, (c % 2) * TPC:(c % 2 + 1) * TPC],
                                TMPA[:, :] if h_out == 'h' else TMPB[:, :]))
                        on("vector", vh)
                        def sh(sync, c=c):
                            wait(sync, "dve")
                            incd("tpA" if c % 2 == 0 else "tpB", sync.dma_start(
                                out=bass.AP(t16_hbm.tensor, c * 128 * 128,
                                            [[128, 128], [D * 128, NT], [1, 128]]),
                                in_=HW16[:, (c % 2) * TPC:(c % 2 + 1) * TPC]))
                        on("sync", sh)
                if h_out is not None:
                    def sw(sync):
                        wait(sync, "tpA")
                        wait(sync, "tpB")
                        inc("aux", sync.nop())
                    on("sync", sw)
            kb.barrier()

        def emit_proj(wdram, mode, li, bias_col=None, dest=None, dest_off=0):
            with ExitStack() as c2:
                WPN = c2.enter_context(nc.sbuf_tensor(sbname("pjW"), [128, 2 * ND * 512], BF16))
                STG = c2.enter_context(nc.sbuf_tensor(sbname("pjS"), [128, 2 * TPC], BF16))
                PSQa = c2.enter_context(nc.psum_tensor(sbname("pjP"), [128, TPC], F32))
                PSQb = c2.enter_context(nc.psum_tensor(sbname("pjP"), [128, TPC], F32))
                PSQs = (PSQa, PSQb)
                wlsem = ("stA", "stB")
                wl0 = (n_("stA"), n_("stB"))
                sgsem = ("sgA", "sgB")
                sg0 = (n_("sgA"), n_("sgB"))
                mm0, ev0 = n_("mm"), n_("ev")

                for p in range(4):
                    def gw(gpsimd, p=p):
                        if p == 0:
                            wait(gpsimd, "ph")
                        if p >= 2:
                            wait(gpsimd, "mm", mm0 + 4 * (p - 1) *
                                 (32 if mode != 'tm' else 32))
                        incd(wlsem[p % 2], gpsimd.dma_start(
                            out=bass.AP(WPN, (p % 2) * ND * 512,
                                        [[2 * ND * 512, 128], [512, ND], [1, 512]]),
                            in_=bass.AP(wdram.tensor, 512 * p,
                                        [[D, 128], [128 * D, ND], [1, 512]])))
                    on("gpsimd", gw)

                if mode in ('fm', 'fm_sb', 'res'):
                    for p in range(4):
                        for ocr in range(4):
                            oc = 4 * p + ocr
                            def tm(tensor, p=p, ocr=ocr, oc=oc):
                                if ocr == 0:
                                    wait(tensor, wlsem[p % 2], 16 * (wl0[p % 2] // 16 + p // 2 + 1))
                                if oc >= 2:
                                    wait(tensor, "ev", ev0 + oc - 1)
                                for th in range(2):
                                    for dc in range(ND):
                                        tensor.matmul(
                                            PSQs[oc % 2][:, th * 512:(th + 1) * 512],
                                            WPN[:, (p % 2) * ND * 512 + dc * 512 +
                                                ocr * 128: (p % 2) * ND * 512 +
                                                dc * 512 + (ocr + 1) * 128],
                                            XN[:, dc * TPC + th * 512:
                                               dc * TPC + (th + 1) * 512],
                                            start=(dc == 0), stop=(dc == ND - 1))
                                tensor.drain()
                                inc("mm", tensor.nop(), 32)
                            on("tensor", tm)

                            if mode == 'fm_sb':
                                def se_(scalar, oc=oc):
                                    wait(scalar, "mm", mm0 + 32 * (oc + 1))
                                    scalar.drain()
                                    inc("ev", scalar.activation(
                                        QT[:, oc * TPC:(oc + 1) * TPC],
                                        PSQs[oc % 2][:, :],
                                        AF.Identity,
                                        bias=(VEC[:, bias_col + oc:bias_col + oc + 1]
                                              if bias_col is not None else CS[:, 7:8])))
                                on("scalar", se_)
                            elif mode == 'fm':
                                def se_(scalar, oc=oc):
                                    wait(scalar, "mm", mm0 + 32 * (oc + 1))
                                    if oc >= 2:
                                        wait(scalar, sgsem[oc % 2],
                                             sg0[oc % 2] + 16 * ((oc - 2) // 2 + 1))
                                    scalar.drain()
                                    inc("ev", scalar.activation(
                                        STG[:, (oc % 2) * TPC:(oc % 2 + 1) * TPC],
                                        PSQs[oc % 2][:, :],
                                        AF.Identity,
                                        bias=(VEC[:, bias_col + oc:bias_col + oc + 1]
                                              if bias_col is not None else CS[:, 7:8])))
                                on("scalar", se_)
                                def sd(sync, oc=oc):
                                    wait(sync, "ev", ev0 + oc + 1)
                                    incd(sgsem[oc % 2], sync.dma_start(
                                        out=bass.AP(dest.tensor, dest_off + oc * 128 * TPC,
                                                    [[TPC, 128], [1, TPC]]),
                                        in_=STG[:, (oc % 2) * TPC:(oc % 2 + 1) * TPC]))
                                on("sync", sd)
                            else:
                                def ve_(vector, oc=oc, li=li):
                                    wait(vector, "mm", mm0 + 32 * (oc + 1))
                                    vector.drain()
                                    if bias_col is not None:
                                        inc("ev", vector.scalar_tensor_tensor(
                                            X[:, oc * TPC:(oc + 1) * TPC],
                                            PSQs[oc % 2][:, :],
                                            VEC[:, bias_col + oc:bias_col + oc + 1],
                                            X[:, oc * TPC:(oc + 1) * TPC],
                                            ALU.add, ALU.add))
                                    else:
                                        inc("ev", vector.tensor_tensor(
                                            X[:, oc * TPC:(oc + 1) * TPC],
                                            X[:, oc * TPC:(oc + 1) * TPC],
                                            PSQs[oc % 2][:, :],
                                            ALU.add))
                                on("vector", ve_)
                else:
                    for p in range(4):
                        for tt in range(NT):
                            idx = p * NT + tt
                            def tm(tensor, p=p, tt=tt, idx=idx):
                                if tt == 0:
                                    wait(tensor, wlsem[p % 2], 16 * (wl0[p % 2] // 16 + p // 2 + 1))
                                if idx >= 2:
                                    wait(tensor, "ev", ev0 + idx - 1)
                                for dc in range(ND):
                                    tensor.matmul(
                                        PSQs[idx % 2][:, 0:512],
                                        XN[:, dc * TPC + tt * 128:dc * TPC + (tt + 1) * 128],
                                        WPN[:, (p % 2) * ND * 512 + dc * 512:
                                            (p % 2) * ND * 512 + (dc + 1) * 512],
                                        start=(dc == 0), stop=(dc == ND - 1))
                                tensor.drain()
                                inc("mm", tensor.nop(), 16)
                            on("tensor", tm)

                            def ve_(vector, p=p, tt=tt, idx=idx):
                                wait(vector, "mm", mm0 + 16 * (idx + 1))
                                if idx >= 2:
                                    wait(vector, sgsem[idx % 2],
                                         sg0[idx % 2] + 16 * ((idx - 2) // 2 + 1))
                                vector.drain()
                                inc("ev", vector.tensor_copy(
                                    STG[:, (idx % 2) * TPC:(idx % 2) * TPC + 512],
                                    PSQs[idx % 2][:, 0:512]))
                            on("vector", ve_)

                            def sd(sync, p=p, tt=tt, idx=idx):
                                wait(sync, "ev", ev0 + idx + 1)
                                incd(sgsem[idx % 2], sync.dma_start(
                                    out=bass.AP(dest.tensor,
                                                dest_off + tt * 128 * D + p * 512,
                                                [[D, 128], [1, 512]]),
                                    in_=STG[:, (idx % 2) * TPC:(idx % 2) * TPC + 512]))
                            on("sync", sd)
                def fin_w(sync):
                    wait(sync, "sgA")
                    wait(sync, "sgB")
                on("sync", fin_w)
            kb.barrier()

        def emit_attn(li, causal):
            def g_ag(gpsimd):
                wait(gpsimd, "ph")
                inc("ag", gpsimd.collective_compute(
                    "AllGather", ALU.bypass,
                    replica_groups=[[0, 1, 2, 3], [4, 5, 6, 7]],
                    ins=[ag1_in[li][:, :]], outs=[ag1_out[li][:, :]]))
            on("gpsimd", g_ag)

            with ExitStack() as c2:
                KH = c2.enter_context(nc.sbuf_tensor(sbname("atK"), [128, S], BF16))
                VH = c2.enter_context(nc.sbuf_tensor(sbname("atV"), [128, S], BF16))
                PRB = c2.enter_context(nc.sbuf_tensor(sbname("atP"), [128, 4 * 512], BF16))
                MKB = c2.enter_context(nc.sbuf_tensor(sbname("atM"), [128, 4 * 512], BF16))
                RB = c2.enter_context(nc.sbuf_tensor(sbname("atR"), [128, 512], F32))
                ZR = c2.enter_context(nc.sbuf_tensor(sbname("atZ"), [1, 512], F32))
                ZY = c2.enter_context(nc.sbuf_tensor(sbname("atY"), [1, 512], F32))
                ZT = c2.enter_context(nc.sbuf_tensor(sbname("atZT"), [1, 512], F32))
                T1 = c2.enter_context(nc.sbuf_tensor(sbname("atT"), [128, 512], F32))
                psSa = c2.enter_context(nc.psum_tensor(sbname("atPS"), [128, 512], F32))
                psSb = c2.enter_context(nc.psum_tensor(sbname("atPS"), [128, 512], F32))
                psSs = (psSa, psSb)
                psO = c2.enter_context(nc.psum_tensor(sbname("atPO"), [128, 512], F32))
                psZ = c2.enter_context(nc.psum_tensor(sbname("atPZ"), [1, 512], F32))
                psB = c2.enter_context(nc.psum_tensor(sbname("atPB"), [128, 512], F32))

                for h in range(H):
                    def gl(sync, h=h):
                        if h == 0:
                            wait(sync, "ag")
                        else:
                            wait(sync, "mm")
                        for r in range(4):
                            incd("pq", sync.dma_start(
                                out=KH[:, r * TPC:(r + 1) * TPC],
                                in_=bass.AP(ag1_out[li].tensor,
                                            r * BLK1 + h * 128 * TPC,
                                            [[TPC, 128], [1, TPC]])))
                            incd("pq", sync.dma_start(
                                out=bass.AP(VH, r * 8 * 128,
                                            [[S, 128], [128, 8], [1, 128]]),
                                in_=bass.AP(ag1_out[li].tensor,
                                            r * BLK1 + V_OFF + h * 128,
                                            [[D, 128], [128 * D, 8], [1, 128]])))
                    on("sync", gl)

                    pq_t = n_("pq")
                    for tq in range(2):
                        for ck in range(32):
                            def t_qk(tensor, h=h, tq=tq, ck=ck, pq_t=pq_t):
                                if ck == 0 and tq == 0:
                                    wait(tensor, "pq", pq_t)
                                if ck >= 2:
                                    wait(tensor, "act", n_("act"))
                                inc("mm", tensor.matmul(
                                    psSs[ck % 2][:, :],
                                    KH[:, ck * 128:(ck + 1) * 128],
                                    QT[:, h * TPC + tq * 512:h * TPC + (tq + 1) * 512],
                                    start=True, stop=True))
                            on("tensor", t_qk)
                            mm_v = n_("mm")

                            def s_exp(scalar, li=li, ck=ck, v=mm_v):
                                wait(scalar, "mm", v)
                                if ck >= 4:
                                    wait(scalar, "mm", n_("mm"))
                                scalar.drain()
                                inc("act", scalar.activation(
                                    PRB[:, (ck % 4) * 512:(ck % 4 + 1) * 512],
                                    psSs[ck % 2][:, :],
                                    AF.Exp, scale=SM_SCALE,
                                    bias=CS[:, 8 + li:9 + li]))
                            on("scalar", s_exp)
                            a_t = n_("act")

                            if causal:
                                def v_msk(vector, tq=tq, ck=ck, a_t=a_t):
                                    wait(vector, "act", a_t)
                                    vector.drain()
                                    vector.tensor_scalar(
                                        MKB[:, (ck % 4) * 512:(ck % 4 + 1) * 512],
                                        IOT0[:, :],
                                        NEGD[:, tq * 32 + ck:tq * 32 + ck + 1],
                                        None, ALU.is_ge)
                                    vector.drain()
                                    inc("dve", vector.tensor_tensor(
                                        PRB[:, (ck % 4) * 512:(ck % 4 + 1) * 512],
                                        PRB[:, (ck % 4) * 512:(ck % 4 + 1) * 512],
                                        MKB[:, (ck % 4) * 512:(ck % 4 + 1) * 512],
                                        ALU.mult))
                                on("vector", v_msk)
                            d_t = n_("dve")

                            def t_pv(tensor, ck=ck, a_t=a_t, d_t=d_t, causal=causal):
                                if causal:
                                    wait(tensor, "dve", d_t)
                                else:
                                    wait(tensor, "act", a_t)
                                inc("mm", tensor.matmul(
                                    psO[:, :], VH[:, ck * 128:(ck + 1) * 128],
                                    PRB[:, (ck % 4) * 512:(ck % 4 + 1) * 512],
                                    start=(ck == 0), stop=(ck == 31)))
                                inc("mm", tensor.matmul(
                                    psZ[0:1, :], ONEB[:, 0:1],
                                    PRB[:, (ck % 4) * 512:(ck % 4 + 1) * 512],
                                    start=(ck == 0), stop=(ck == 31)))
                            on("tensor", t_pv)

                        mm_t = n_("mm")
                        def v_z(vector, mm_t=mm_t):
                            wait(vector, "mm", mm_t)
                            vector.drain()
                            vector.tensor_copy(ZR[0:1, :], psZ[0:1, :])
                            vector.drain()
                            vector.tensor_scalar(
                                ZT[0:1, :].bitcast(I32), ZR[0:1, :].bitcast(I32),
                                23, None, ALU.logical_shift_right)
                            vector.drain()
                            inc("dve", vector.tensor_copy(ZY[0:1, :],
                                                          ZT[0:1, :].bitcast(I32)))
                        on("vector", v_z)

                        def s_seed(scalar):
                            wait(scalar, "dve")
                            scalar.drain()
                            inc("act", scalar.activation(
                                ZY[0:1, :], ZY[0:1, :], AF.Exp,
                                scale=-0.6931471805599453, bias=CS[0:1, 10:11]))
                        on("scalar", s_seed)

                        def v_nt(vector):
                            wait(vector, "act")
                            for _ in range(4):
                                vector.drain()
                                vector.tensor_tensor(ZT[0:1, :], ZR[0:1, :],
                                                     ZY[0:1, :], ALU.mult)
                                vector.drain()
                                vector.tensor_scalar(ZT[0:1, :], ZT[0:1, :], -1.0,
                                                     2.0, ALU.mult, ALU.add)
                                vector.drain()
                                vector.tensor_tensor(ZY[0:1, :], ZY[0:1, :],
                                                     ZT[0:1, :], ALU.mult)
                            vector.drain()
                            inc("dve", vector.nop())
                        on("vector", v_nt)

                        def t_bc(tensor):
                            wait(tensor, "dve")
                            inc("mm", tensor.matmul(
                                psB[:, :], ONE1[0:1, :], ZY[0:1, :],
                                start=True, stop=True))
                        on("tensor", t_bc)

                        def v_ev(vector, li=li, h=h, tq=tq):
                            wait(vector, "mm")
                            vector.drain()
                            vector.tensor_copy(RB[:, :], psB[:, :])
                            vector.drain()
                            vector.tensor_tensor(T1[:, :], psO[:, :], RB[:, :], ALU.mult)
                            vector.drain()
                            if li == 0:
                                inc("ev", vector.tensor_scalar(
                                    XN[:, h * TPC + tq * 512:h * TPC + (tq + 1) * 512],
                                    T1[:, :],
                                    VEC[:, 2 * ND + h:2 * ND + h + 1], None, ALU.add))
                            else:
                                inc("ev", vector.tensor_copy(
                                    XN[:, h * TPC + tq * 512:h * TPC + (tq + 1) * 512],
                                    T1[:, :]))
                        on("vector", v_ev)

                        def t_gate(tensor):
                            wait(tensor, "ev")
                            inc("mm", tensor.nop())
                        on("tensor", t_gate)
            kb.barrier()
        def emit_hquant(use_rsq, use_eps, rowsum):
            with ExitStack() as c2:
                TMQ = c2.enter_context(nc.sbuf_tensor(sbname("qT"), [128, NT * D], FP16))
                W0 = c2.enter_context(nc.sbuf_tensor(sbname("qW"), [128, D], F32))
                R32 = c2.enter_context(nc.sbuf_tensor(sbname("qR3"), [128, D], F32))
                R64 = c2.enter_context(nc.sbuf_tensor(sbname("qR6"), [128, D], F32))
                QTI = c2.enter_context(nc.sbuf_tensor(sbname("qQ"), [128, 2 * D], BF16))
                tp0 = (n_("tpA"), n_("tpB"))
                def s_tp(sync):
                    wait(sync, "ph")
                    for j in range(NT):
                        incd("tpA" if j % 2 == 0 else "tpB", sync.dma_start_transpose(
                            TMQ[:, j * D:(j + 1) * D],
                            bass.AP(t16_hbm.tensor, j * D * 128, [[128, D], [1, 128]])))
                on("sync", s_tp)

                def v_rm(vector):
                    wait(vector, "tpA")
                    wait(vector, "tpB")
                    for j in range(NT):
                        vector.drain()
                        inc("dve", vector.reduce_max(
                            STT[:, j:j + 1], TMQ[:, j * D:(j + 1) * D],
                            axis=AX.X, apply_absolute_value=True))
                on("vector", v_rm)

                def v_sc(vector):
                    wait(vector, "dve")
                    wait(vector, "aux")
                    vector.drain()
                    if use_rsq:
                        vector.tensor_tensor(STT[:, 40:48], STT[:, 0:8],
                                             STT[:, 32:40], ALU.mult)
                    else:
                        vector.tensor_copy(STT[:, 40:48], STT[:, 0:8])
                    vector.drain()
                    if use_eps:
                        vector.tensor_scalar(STT[:, 40:48], STT[:, 40:48],
                                             CS[:, 3:4], None, ALU.max)
                        vector.drain()
                    inc("dve", vector.nop())
                on("vector", v_sc)

                def s_sc(scalar):
                    wait(scalar, "dve")
                    scalar.drain()
                    scalar.activation(STT[:, 48:56], STT[:, 40:48], AF.Ln)
                    scalar.drain()
                    inc("act", scalar.activation(STT[:, 48:56], STT[:, 48:56],
                                                 AF.Exp, scale=-1.0))
                on("scalar", s_sc)

                def v_sc2(vector):
                    wait(vector, "act")
                    vector.drain()
                    vector.tensor_scalar(STT[:, 48:56], STT[:, 48:56], 127.0,
                                         None, ALU.mult)
                    vector.tensor_scalar(STT[:, 8:16], STT[:, 40:48], 1.0 / 127.0,
                                         None, ALU.mult)
                    vector.drain()
                    if use_rsq:
                        inc("dve", vector.tensor_tensor(
                            STT[:, 16:24], STT[:, 48:56], STT[:, 32:40], ALU.mult))
                    else:
                        inc("dve", vector.tensor_copy(STT[:, 16:24], STT[:, 48:56]))
                on("vector", v_sc2)

                tq0 = (n_("shA"), n_("shB"))
                for j in range(NT):
                    def v_q(vector, j=j):
                        if j == 0:
                            wait(vector, "dve")
                        if j >= 2:
                            wait(vector, "shA" if j % 2 == 0 else "shB",
                                 tq0[j % 2] + 16 * (j // 2))
                        vector.drain()
                        vector.tensor_scalar(W0[:, :], TMQ[:, j * D:(j + 1) * D],
                                             STT[:, 16 + j:17 + j], None, ALU.mult)
                        vector.drain()
                        vector.tensor_scalar(R32[:, :], W0[:, :], CS[:, 0:1], None, ALU.add)
                        vector.drain()
                        vector.tensor_scalar(R32[:, :], R32[:, :], CS[:, 0:1], None, ALU.subtract)
                        vector.tensor_scalar(R64[:, :], W0[:, :], CS[:, 1:2], None, ALU.add)
                        vector.drain()
                        vector.tensor_scalar(R64[:, :], R64[:, :], CS[:, 1:2], None, ALU.subtract)
                        vector.drain()
                        vector.tensor_tensor(R64[:, :], R64[:, :], R32[:, :], ALU.subtract)
                        vector.drain()
                        inc("dve", vector.scalar_tensor_tensor(
                            QTI[:, (j % 2) * D:(j % 2 + 1) * D], R64[:, :],
                            IND[:, 0:1], R32[:, :], ALU.mult, ALU.add))
                        if rowsum:
                            vector.drain()
                            inc("dve", vector.reduce_sum(
                                STT[:, 24 + j:25 + j],
                                QTI[:, (j % 2) * D:(j % 2 + 1) * D], axis=AX.X))
                    on("vector", v_q)

                    def s_sc3(sync, j=j):
                        wait(sync, "dve")
                        incd("shA" if j % 2 == 0 else "shB", sync.dma_start(
                            out=bass.AP(qi_hbm.tensor, j * 128 * 128,
                                        [[128, 128], [TPC * 128, ND], [1, 128]]),
                            in_=bass.AP(QTI, (j % 2) * D,
                                        [[2 * D, 128], [128, ND], [1, 128]])))
                    on("sync", s_sc3)

                def s_fin(sync):
                    wait(sync, "shA")
                    wait(sync, "shB")
                    for dc in range(ND):
                        incd("tp", sync.dma_start_transpose(
                            XN[:, dc * TPC:(dc + 1) * TPC],
                            bass.AP(qi_hbm.tensor, dc * TPC * 128, [[128, TPC], [1, 128]])))
                    wait(sync, "tp")
                    inc("aux", sync.nop())
                on("sync", s_fin)
            kb.barrier()

        def emit_gv():
            PW = 256
            with ExitStack() as c2:
                WF = c2.enter_context(nc.sbuf_tensor(sbname("gW"), [128, 2 * ND * PW], F32))
                WG = c2.enter_context(nc.sbuf_tensor(sbname("gG"), [128, 2 * ND * PW], F32))
                TV = c2.enter_context(nc.sbuf_tensor(sbname("gTV"), [128, ND * PW], BF16))
                TG = c2.enter_context(nc.sbuf_tensor(sbname("gTG"), [128, ND * PW], BF16))
                SA = c2.enter_context(nc.sbuf_tensor(sbname("gSA"), [128, PW], F32))
                SB_ = c2.enter_context(nc.sbuf_tensor(sbname("gSB"), [128, PW], F32))
                GS = c2.enter_context(nc.sbuf_tensor(sbname("gGS"), [128, 2 * PW], F32))
                UO = c2.enter_context(nc.sbuf_tensor(sbname("gUO"), [128, 2 * PW], FP16))
                SC2 = c2.enter_context(nc.sbuf_tensor(sbname("gS2"), [128, 8], F32))
                psVs = tuple(c2.enter_context(nc.psum_tensor(sbname("gPV"), [128, PW], F32))
                             for _ in range(2))
                psGs = tuple(c2.enter_context(nc.psum_tensor(sbname("gPG"), [128, PW], F32))
                             for _ in range(2))
                wl0 = (n_("shA"), n_("shB"))
                u0 = (n_("sgA"), n_("sgB"))
                evb = n_("ev")

                def v_pre(vector):
                    wait(vector, "ph")
                    vector.drain()
                    inc("dve", vector.tensor_scalar(SC2[:, :], STT[:, 8:16],
                                                    TAU[:, 4:5], None, ALU.mult))
                on("vector", v_pre)

                for p in range(32):
                    def s_w(sync, p=p):
                        if p >= 2:
                            wait(sync, "mm", n_("mm"))
                        incd("shA" if p % 2 == 0 else "shB", sync.dma_start(
                            out=bass.AP(WF, (p % 2) * ND * PW,
                                        [[2 * ND * PW, 128], [PW, ND], [1, PW]]),
                            in_=bass.AP(wgvT.tensor, PW * p,
                                        [[F1, 128], [128 * F1, ND], [1, PW]])))
                        incd("shA" if p % 2 == 0 else "shB", sync.dma_start(
                            out=bass.AP(WG, (p % 2) * ND * PW,
                                        [[2 * ND * PW, 128], [PW, ND], [1, PW]]),
                            in_=bass.AP(wgvT.tensor, F2 + PW * p,
                                        [[F1, 128], [128 * F1, ND], [1, PW]])))
                    on("sync", s_w)

                    def v_t(vector, p=p):
                        wait(vector, "shA" if p % 2 == 0 else "shB",
                             wl0[p % 2] + 32 * (p // 2 + 1))
                        if p >= 1:
                            wait(vector, "mm", n_("mm"))
                        for dc in range(ND):
                            for (src, dst, base) in ((WF, TV, (p % 2) * ND * PW),
                                                     (WG, TG, (p % 2) * ND * PW)):
                                ssl = slice(base + dc * PW, base + (dc + 1) * PW)
                                dsl = slice(dc * PW, (dc + 1) * PW)
                                vector.drain()
                                vector.tensor_scalar(SA[:, :], src[:, ssl],
                                                     TAU[:, 0:1], None, ALU.is_ge)
                                vector.tensor_scalar(SB_[:, :], src[:, ssl],
                                                     TAU[:, 1:2], None, ALU.is_le)
                                vector.drain()
                                vector.tensor_tensor(dst[:, dsl], SA[:, :], SB_[:, :],
                                                     ALU.subtract)
                        vector.drain()
                        inc("dve", vector.nop())
                    on("vector", v_t)
                    tern_t = n_("dve")

                    for tt in range(NT):
                        idx = p * NT + tt
                        def t_m(tensor, p=p, tt=tt, idx=idx, tern_t=tern_t):
                            if tt == 0:
                                wait(tensor, "dve", tern_t)
                            if idx >= 2:
                                wait(tensor, "ev", evb + idx - 1)
                            for dc in range(ND):
                                tensor.matmul(
                                    psVs[tt % 2][:, :],
                                    XN[:, dc * TPC + tt * 128:dc * TPC + (tt + 1) * 128],
                                    TV[:, dc * PW:(dc + 1) * PW],
                                    start=(dc == 0), stop=(dc == ND - 1))
                                tensor.matmul(
                                    psGs[tt % 2][:, :],
                                    XN[:, dc * TPC + tt * 128:dc * TPC + (tt + 1) * 128],
                                    TG[:, dc * PW:(dc + 1) * PW],
                                    start=(dc == 0), stop=(dc == ND - 1))
                            tensor.drain()
                            inc("mm", tensor.nop(), 32)
                        on("tensor", t_m)
                        mmv = n_("mm")

                        def s_g(scalar, tt=tt, idx=idx, mmv=mmv):
                            wait(scalar, "mm", mmv)
                            if idx >= 2:
                                wait(scalar, "ev", evb + idx - 1)
                            scalar.drain()
                            inc("act", scalar.activation(
                                GS[:, (idx % 2) * PW:(idx % 2 + 1) * PW],
                                psGs[tt % 2][:, :],
                                AF.Silu, scale=SC2[:, tt:tt + 1]))
                        on("scalar", s_g)
                        actv = n_("act")

                        def v_u(vector, p=p, tt=tt, idx=idx, actv=actv):
                            wait(vector, "act", actv)
                            if idx >= 2:
                                wait(vector, "sgA" if idx % 2 == 0 else "sgB",
                                     u0[idx % 2] + 16 * ((idx - 2) // 2 + 1))
                            vector.drain()
                            vector.scalar_tensor_tensor(
                                UO[:, (idx % 2) * PW:(idx % 2 + 1) * PW],
                                psVs[tt % 2][:, :], SC2[:, tt:tt + 1],
                                GS[:, (idx % 2) * PW:(idx % 2 + 1) * PW],
                                ALU.mult, ALU.mult)
                            vector.drain()
                            vector.reduce_max(STT[:, 48 + tt:49 + tt],
                                              UO[:, (idx % 2) * PW:(idx % 2 + 1) * PW],
                                              axis=AX.X, apply_absolute_value=True)
                            vector.drain()
                            if p == 0:
                                inc("dve", vector.tensor_copy(STT[:, 0 + tt:1 + tt],
                                                              STT[:, 48 + tt:49 + tt]))
                            else:
                                inc("dve", vector.tensor_tensor(
                                    STT[:, 0 + tt:1 + tt], STT[:, 0 + tt:1 + tt],
                                    STT[:, 48 + tt:49 + tt], ALU.max))
                            inc("ev", vector.nop())
                        on("vector", v_u)

                        def s_u(sync, p=p, tt=tt, idx=idx):
                            wait(sync, "ev")
                            incd("sgA" if idx % 2 == 0 else "sgB", sync.dma_start(
                                out=bass.AP(u_hbm.tensor,
                                            2 * p * TPC * 128 + tt * 128 * 128,
                                            [[128, 128], [TPC * 128, 2], [1, 128]]),
                                in_=bass.AP(UO, (idx % 2) * PW,
                                            [[2 * PW, 128], [128, 2], [1, 128]])))
                        on("sync", s_u)

                def s_wd(sync):
                    wait(sync, "sgA")
                    wait(sync, "sgB")
                    inc("aux", sync.nop())
                on("sync", s_wd)
            kb.barrier()

        def emit_ustats(descale_tau_col, out_negate2=False):
            def v_s(vector):
                wait(vector, "ph")
                vector.drain()
                vector.tensor_scalar(STT[:, 40:48], STT[:, 0:8], CS[:, 3:4],
                                     None, ALU.max)
                vector.drain()
                inc("dve", vector.nop())
            on("vector", v_s)

            def s_s(scalar):
                wait(scalar, "dve")
                scalar.drain()
                scalar.activation(STT[:, 48:56], STT[:, 40:48], AF.Ln)
                scalar.drain()
                inc("act", scalar.activation(STT[:, 48:56], STT[:, 48:56],
                                             AF.Exp, scale=-1.0))
            on("scalar", s_s)

            def v_s2(vector):
                wait(vector, "act")
                vector.drain()
                vector.tensor_scalar(STT[:, 48:56], STT[:, 48:56], 127.0, None, ALU.mult)
                vector.drain()
                vector.tensor_copy(STT[:, 16:24], STT[:, 48:56])
                vector.tensor_scalar(STT[:, 56:64], STT[:, 40:48], 1.0 / 127.0,
                                     None, ALU.mult)
                vector.drain()
                inc("dve", vector.tensor_scalar(STT[:, 56:64], STT[:, 56:64],
                                                TAU[:, descale_tau_col:descale_tau_col + 1],
                                                None, ALU.mult))
            on("vector", v_s2)

            def s_r(sync):
                wait(sync, "dve")
                incd("tp", sync.dma_start(
                    out=bass.AP(row_hbm.tensor, 0, [[1, 128], [128, 8]]),
                    in_=STT[:, 16:24], allow_slow_non_contiguous=True))
                incd("tp", sync.dma_start(
                    out=bass.AP(row_hbm.tensor, TPC, [[1, 128], [128, 8]]),
                    in_=STT[:, 56:64], allow_slow_non_contiguous=True))
                wait(sync, "tp")
                incd("tp", sync.dma_start(
                    out=ROW[0:1, 0:TPC],
                    in_=bass.AP(row_hbm.tensor, 0, [[1, TPC]])))
                incd("tp", sync.dma_start(
                    out=ROW[0:1, TPC:2 * TPC],
                    in_=bass.AP(row_hbm.tensor, TPC, [[1, TPC]])))
                wait(sync, "tp")
                inc("aux", sync.nop())
            on("sync", s_r)
            kb.barrier()
        def emit_mmK(li, w_panels, src_hbm, src_fp16, quantize, tern3):
            with ExitStack() as c2:
                BIG = c2.enter_context(nc.sbuf_tensor(sbname("mB"), [128, 64 * 512], BF16))
                CUB = c2.enter_context(nc.sbuf_tensor(sbname("mC"), [128, TPC], F32))
                DDB = c2.enter_context(nc.sbuf_tensor(sbname("mD"), [128, TPC], F32))
                UT = c2.enter_context(nc.sbuf_tensor(sbname("mU"), [128, 2 * 512],
                                                     FP16 if src_fp16 else BF16))
                W0 = c2.enter_context(nc.sbuf_tensor(sbname("mW0"), [128, 512], F32))
                R32 = c2.enter_context(nc.sbuf_tensor(sbname("mR3"), [128, 512], F32))
                R64 = c2.enter_context(nc.sbuf_tensor(sbname("mR6"), [128, 512], F32))
                WQF = c2.enter_context(nc.sbuf_tensor(sbname("mWQ"), [128, 2048], F32))
                SA = c2.enter_context(nc.sbuf_tensor(sbname("mSA"), [128, 512], F32))
                SB_ = c2.enter_context(nc.sbuf_tensor(sbname("mSB"), [128, 512], F32))
                TD = c2.enter_context(nc.sbuf_tensor(sbname("mTD"), [128, 64 * 128], BF16))
                XR = c2.enter_context(nc.sbuf_tensor(sbname("mXR"), [128, 2 * 512], F32))
                T2 = c2.enter_context(nc.sbuf_tensor(sbname("mT2"), [128, 512], F32))
                psO2s = tuple(c2.enter_context(nc.psum_tensor(sbname("mPO"), [128, 512], F32))
                              for _ in range(2))
                psBC = c2.enter_context(nc.psum_tensor(sbname("mPB"), [128, 2 * TPC], F32))

                def s_sp(sync):
                    wait(sync, "ph")
                    incd("tp", sync.dma_start(
                        out=bass.AP(x_hbm.tensor, 0, [[ND * TPC, 128], [1, ND * TPC]]),
                        in_=X[:, :]))
                    wait(sync, "tp")
                    inc("aux", sync.nop())
                on("sync", s_sp)

                def t_bc(tensor):
                    wait(tensor, "ph")
                    for h2 in range(2):
                        inc("mm", tensor.matmul(
                            psBC[:, h2 * 512:(h2 + 1) * 512], ONE1[0:1, :],
                            ROW[0:1, h2 * 512:(h2 + 1) * 512], start=True, stop=True))
                        inc("mm", tensor.matmul(
                            psBC[:, TPC + h2 * 512:TPC + (h2 + 1) * 512], ONE1[0:1, :],
                            ROW[0:1, TPC + h2 * 512:TPC + (h2 + 1) * 512],
                            start=True, stop=True))
                on("tensor", t_bc)

                def v_bc(vector):
                    wait(vector, "mm")
                    vector.drain()
                    vector.tensor_copy(CUB[:, :], psBC[:, 0:TPC])
                    inc("dve", vector.tensor_copy(DDB[:, :], psBC[:, TPC:2 * TPC]))
                on("vector", v_bc)

                for half in range(2):
                    tp0 = (n_("tpA"), n_("tpB"))
                    for kc in range(64):
                        def s_tp(sync, kc=kc, half=half):
                            if half == 1 and kc < 2:
                                wait(sync, "mm", n_("mm"))
                            if kc >= 2:
                                wait(sync, "dve", n_("dve"))
                            incd("tpA" if kc % 2 == 0 else "tpB",
                                 sync.dma_start_transpose(
                                     (UT[:, (kc % 2) * 512:(kc % 2 + 1) * 512]
                                      if quantize else
                                      BIG[:, kc * 512:(kc + 1) * 512]),
                                     bass.AP(src_hbm.tensor,
                                             kc * TPC * 128 + half * 512 * 128,
                                             [[128, 512], [1, 128]])))
                        on("sync", s_tp)

                        if quantize:
                            def v_q(vector, kc=kc, half=half, tp0=tp0):
                                wait(vector, "tpA" if kc % 2 == 0 else "tpB",
                                     tp0[kc % 2] + 16 * (kc // 2 + 1))
                                if kc == 0:
                                    wait(vector, "dve")
                                vector.drain()
                                vector.tensor_tensor(
                                    W0[:, :], UT[:, (kc % 2) * 512:(kc % 2 + 1) * 512],
                                    CUB[:, half * 512:(half + 1) * 512], ALU.mult)
                                vector.drain()
                                vector.tensor_scalar(R32[:, :], W0[:, :], CS[:, 0:1],
                                                     None, ALU.add)
                                vector.drain()
                                vector.tensor_scalar(R32[:, :], R32[:, :], CS[:, 0:1],
                                                     None, ALU.subtract)
                                vector.tensor_scalar(R64[:, :], W0[:, :], CS[:, 1:2],
                                                     None, ALU.add)
                                vector.drain()
                                vector.tensor_scalar(R64[:, :], R64[:, :], CS[:, 1:2],
                                                     None, ALU.subtract)
                                vector.drain()
                                vector.tensor_tensor(R64[:, :], R64[:, :], R32[:, :],
                                                     ALU.subtract)
                                vector.drain()
                                inc("dve", vector.scalar_tensor_tensor(
                                    BIG[:, kc * 512:(kc + 1) * 512], R64[:, :],
                                    IND[:, 0:1], R32[:, :], ALU.mult, ALU.add))
                            on("vector", v_q)
                    big_t = (n_("dve"), n_("tpA"), n_("tpB"))

                    wq0 = (n_("stA"), n_("stB"))
                    for o2c in range(16):
                        gidx = half * 16 + o2c
                        if tern3 and half == 0:
                            for q in range(4):
                                def s_w(sync, o2c=o2c, q=q):
                                    if o2c + q > 0:
                                        wait(sync, "dve", n_("dve"))
                                    incd("shA", sync.dma_start(
                                        out=bass.AP(WQF, 0,
                                                    [[2048, 128], [128, 16], [1, 128]]),
                                        in_=bass.AP(w_panels.tensor,
                                                    o2c * F2 * 128 + q * 2048 * 128,
                                                    [[128, 128], [128 * 128, 16], [1, 128]])))
                                on("sync", s_w)
                                def v_t(vector, o2c=o2c, q=q):
                                    wait(vector, "shA", n_("shA"))
                                    if q == 0:
                                        wait(vector, "mm", n_("mm"))
                                        wait(vector, "tp", n_("tp"))
                                    for sl4 in range(4):
                                        ssl = slice(sl4 * 512, (sl4 + 1) * 512)
                                        vector.drain()
                                        vector.tensor_scalar(SA[:, :], WQF[:, ssl],
                                                             TAU[:, 2:3], None, ALU.is_ge)
                                        vector.tensor_scalar(SB_[:, :], WQF[:, ssl],
                                                             TAU[:, 3:4], None, ALU.is_le)
                                        vector.drain()
                                        vector.tensor_tensor(
                                            TD[:, q * 2048 + sl4 * 512:
                                               q * 2048 + (sl4 + 1) * 512],
                                            SA[:, :], SB_[:, :], ALU.subtract)
                                    vector.drain()
                                    inc("dve", vector.nop())
                                on("vector", v_t)
                            def s_tc(sync, o2c=o2c):
                                wait(sync, "dve")
                                incd("tp", sync.dma_start(
                                    out=bass.AP(tern_hbm.tensor, o2c * F2 * 128,
                                                [[128, 128], [128 * 128, 64], [1, 128]]),
                                    in_=bass.AP(TD, 0, [[64 * 128, 128], [128, 64], [1, 128]])))
                            on("sync", s_tc)
                            td_t = ("dve", n_("dve"))
                        elif tern3:
                            def s_w(sync, o2c=o2c):
                                wait(sync, "tp", n_("tp"))
                                if o2c > 0:
                                    wait(sync, "mm", n_("mm"))
                                incd("tp", sync.dma_start(
                                    out=bass.AP(TD, 0, [[64 * 128, 128], [128, 64], [1, 128]]),
                                    in_=bass.AP(tern_hbm.tensor, o2c * F2 * 128,
                                                [[128, 128], [128 * 128, 64], [1, 128]])))
                            on("sync", s_w)
                            td_t = ("tp", n_("tp"))
                        else:
                            for q in range(4):
                                def s_w(sync, o2c=o2c, q=q):
                                    if o2c + q > 0:
                                        wait(sync, "dve", n_("dve"))
                                    incd("shA", sync.dma_start(
                                        out=bass.AP(WQF, 0,
                                                    [[2048, 128], [128, 16], [1, 128]]),
                                        in_=bass.AP(w_panels.tensor,
                                                    o2c * F2 * 128 + q * 2048 * 128,
                                                    [[128, 128], [128 * 128, 16], [1, 128]])))
                                on("sync", s_w)
                                def v_t(vector, o2c=o2c, q=q):
                                    wait(vector, "shA", n_("shA"))
                                    if q == 0:
                                        wait(vector, "mm", n_("mm"))
                                    vector.drain()
                                    inc("dve", vector.tensor_scalar(
                                        TD[:, q * 2048:(q + 1) * 2048],
                                        WQF[:, 0:2048], 0.0, None, ALU.is_ge))
                                on("vector", v_t)
                            td_t = ("dve", n_("dve"))

                        def t_m(tensor, half=half, o2c=o2c, gidx=gidx,
                                td_t=td_t, big_t=big_t):
                            wait(tensor, td_t[0], td_t[1])
                            if quantize:
                                wait(tensor, "dve", big_t[0])
                            wait(tensor, "tpA", big_t[1])
                            wait(tensor, "tpB", big_t[2])
                            if gidx >= 2:
                                wait(tensor, "ev", n_("ev"))
                            for kc in range(64):
                                tensor.matmul(
                                    psO2s[gidx % 2][:, :],
                                    TD[:, kc * 128:(kc + 1) * 128],
                                    BIG[:, kc * 512:(kc + 1) * 512],
                                    start=(kc == 0), stop=(kc == 63))
                            tensor.drain()
                            inc("mm", tensor.nop(), 64)
                        on("tensor", t_m)
                        mmv = n_("mm")

                        def s_x(sync, half=half, o2c=o2c, gidx=gidx):
                            wait(sync, "aux")
                            if gidx >= 2:
                                wait(sync, "ev", n_("ev"))
                            incd("sgA" if gidx % 2 == 0 else "sgB", sync.dma_start(
                                out=XR[:, (gidx % 2) * 512:(gidx % 2 + 1) * 512],
                                in_=bass.AP(x_hbm.tensor, o2c * TPC + half * 512,
                                            [[ND * TPC, 128], [1, 512]])))
                        on("sync", s_x)
                        sg_t = (n_("sgA"), n_("sgB"))

                        def v_e(vector, half=half, o2c=o2c, gidx=gidx, mmv=mmv, sg_t=sg_t):
                            wait(vector, "mm", mmv)
                            wait(vector, "sgA" if gidx % 2 == 0 else "sgB",
                                 sg_t[gidx % 2])
                            vector.drain()
                            if tern3:
                                vector.tensor_tensor(
                                    T2[:, :], psO2s[gidx % 2][:, :],
                                    DDB[:, half * 512:(half + 1) * 512], ALU.mult)
                            else:
                                vector.scalar_tensor_tensor(
                                    T2[:, :], psO2s[gidx % 2][:, :],
                                    2.0, DDB[:, half * 512:(half + 1) * 512],
                                    ALU.mult, ALU.subtract)
                            vector.drain()
                            inc("ev", vector.tensor_tensor(
                                X[:, o2c * TPC + half * 512:o2c * TPC + (half + 1) * 512],
                                XR[:, (gidx % 2) * 512:(gidx % 2 + 1) * 512],
                                T2[:, :], ALU.add))
                        on("vector", v_e)
            kb.barrier()

        def emit_gu():
            PW = 256
            with ExitStack() as c2:
                WF = c2.enter_context(nc.sbuf_tensor(sbname("hWF"), [128, 2 * ND * PW], F32))
                WG = c2.enter_context(nc.sbuf_tensor(sbname("hWG"), [128, 2 * ND * PW], F32))
                TV = c2.enter_context(nc.sbuf_tensor(sbname("hTV"), [128, ND * PW], BF16))
                TG = c2.enter_context(nc.sbuf_tensor(sbname("hTG"), [128, ND * PW], BF16))
                GS = c2.enter_context(nc.sbuf_tensor(sbname("hGS"), [128, 2 * PW], F32))
                US = c2.enter_context(nc.sbuf_tensor(sbname("hUS"), [128, PW], F32))
                PO = c2.enter_context(nc.sbuf_tensor(sbname("hPO"), [128, 2 * PW], BF16))
                SG2 = c2.enter_context(nc.sbuf_tensor(sbname("hS2"), [128, 8], F32))
                BG2 = c2.enter_context(nc.sbuf_tensor(sbname("hB2"), [128, 8], F32))
                psGs = tuple(c2.enter_context(nc.psum_tensor(sbname("hPG"), [128, PW], F32))
                             for _ in range(2))
                psUs = tuple(c2.enter_context(nc.psum_tensor(sbname("hPU"), [128, PW], F32))
                             for _ in range(2))
                wl0 = (n_("shA"), n_("shB"))
                u0 = (n_("sgA"), n_("sgB"))
                evb = n_("ev")

                def v_pre(vector):
                    wait(vector, "ph")
                    vector.drain()
                    vector.tensor_scalar(SG2[:, :], STT[:, 8:16], 2.0, None, ALU.mult)
                    vector.tensor_tensor(BG2[:, :], STT[:, 24:32], STT[:, 8:16], ALU.mult)
                    vector.drain()
                    vector.tensor_scalar(BG2[:, :], BG2[:, :], -1.0, None, ALU.mult)
                    vector.memset(STT[:, 56:64], 0.0)
                    vector.drain()
                    inc("dve", vector.nop())
                on("vector", v_pre)

                for p in range(32):
                    def s_w(sync, p=p):
                        if p >= 2:
                            wait(sync, "mm", n_("mm"))
                        incd("shA" if p % 2 == 0 else "shB", sync.dma_start(
                            out=bass.AP(WF, (p % 2) * ND * PW,
                                        [[2 * ND * PW, 128], [PW, ND], [1, PW]]),
                            in_=bass.AP(wgT.tensor, PW * p,
                                        [[F2, 128], [128 * F2, ND], [1, PW]])))
                        incd("shA" if p % 2 == 0 else "shB", sync.dma_start(
                            out=bass.AP(WG, (p % 2) * ND * PW,
                                        [[2 * ND * PW, 128], [PW, ND], [1, PW]]),
                            in_=bass.AP(wuT.tensor, PW * p,
                                        [[F2, 128], [128 * F2, ND], [1, PW]])))
                    on("sync", s_w)

                    def v_t(vector, p=p):
                        wait(vector, "shA" if p % 2 == 0 else "shB",
                             wl0[p % 2] + 32 * (p // 2 + 1))
                        if p >= 1:
                            wait(vector, "mm", n_("mm"))
                        for dc in range(ND):
                            base = (p % 2) * ND * PW
                            ssl = slice(base + dc * PW, base + (dc + 1) * PW)
                            dsl = slice(dc * PW, (dc + 1) * PW)
                            vector.drain()
                            vector.tensor_scalar(TV[:, dsl], WF[:, ssl], 0.0, None, ALU.is_ge)
                            vector.tensor_scalar(TG[:, dsl], WG[:, ssl], 0.0, None, ALU.is_ge)
                        vector.drain()
                        inc("dve", vector.nop())
                    on("vector", v_t)
                    tern_t = n_("dve")

                    for tt in range(NT):
                        idx = p * NT + tt
                        def t_m(tensor, p=p, tt=tt, idx=idx, tern_t=tern_t):
                            if tt == 0:
                                wait(tensor, "dve", tern_t)
                            if idx >= 2:
                                wait(tensor, "ev", evb + idx - 1)
                            for dc in range(ND):
                                tensor.matmul(
                                    psGs[tt % 2][:, :],
                                    XN[:, dc * TPC + tt * 128:dc * TPC + (tt + 1) * 128],
                                    TV[:, dc * PW:(dc + 1) * PW],
                                    start=(dc == 0), stop=(dc == ND - 1))
                                tensor.matmul(
                                    psUs[tt % 2][:, :],
                                    XN[:, dc * TPC + tt * 128:dc * TPC + (tt + 1) * 128],
                                    TG[:, dc * PW:(dc + 1) * PW],
                                    start=(dc == 0), stop=(dc == ND - 1))
                            tensor.drain()
                            inc("mm", tensor.nop(), 32)
                        on("tensor", t_m)
                        mmv = n_("mm")

                        def s_g(scalar, tt=tt, idx=idx, mmv=mmv):
                            wait(scalar, "mm", mmv)
                            if idx >= 2:
                                wait(scalar, "ev", evb + idx - 1)
                            scalar.drain()
                            inc("act", scalar.activation(
                                GS[:, (idx % 2) * PW:(idx % 2 + 1) * PW],
                                psGs[tt % 2][:, :],
                                AF.Silu, scale=SG2[:, tt:tt + 1], bias=BG2[:, tt:tt + 1]))
                        on("scalar", s_g)
                        actv = n_("act")

                        def v_u(vector, p=p, tt=tt, idx=idx, actv=actv):
                            wait(vector, "act", actv)
                            if idx >= 2:
                                wait(vector, "sgA" if idx % 2 == 0 else "sgB",
                                     u0[idx % 2] + 16 * ((idx - 2) // 2 + 1))
                            vector.drain()
                            vector.tensor_scalar(
                                US[:, :], psUs[tt % 2][:, :],
                                SG2[:, tt:tt + 1], BG2[:, tt:tt + 1], ALU.mult, ALU.add)
                            vector.drain()
                            vector.tensor_tensor(
                                PO[:, (idx % 2) * PW:(idx % 2 + 1) * PW],
                                GS[:, (idx % 2) * PW:(idx % 2 + 1) * PW],
                                US[:, :], ALU.mult)
                            vector.drain()
                            vector.reduce_sum(STT[:, 48 + tt:49 + tt],
                                              PO[:, (idx % 2) * PW:(idx % 2 + 1) * PW],
                                              axis=AX.X)
                            vector.drain()
                            inc("dve", vector.tensor_tensor(
                                STT[:, 56 + tt:57 + tt], STT[:, 56 + tt:57 + tt],
                                STT[:, 48 + tt:49 + tt], ALU.add))
                            inc("ev", vector.nop())
                        on("vector", v_u)

                        def s_u(sync, p=p, tt=tt, idx=idx):
                            wait(sync, "ev")
                            incd("sgA" if idx % 2 == 0 else "sgB", sync.dma_start(
                                out=bass.AP(p_hbm.tensor,
                                            2 * p * TPC * 128 + tt * 128 * 128,
                                            [[128, 128], [TPC * 128, 2], [1, 128]]),
                                in_=bass.AP(PO, (idx % 2) * PW,
                                            [[2 * PW, 128], [128, 2], [1, 128]])))
                        on("sync", s_u)

                def s_wd(sync):
                    wait(sync, "sgA")
                    wait(sync, "sgB")
                    inc("aux", sync.nop())
                on("sync", s_wd)

                def s_r(sync):
                    wait(sync, "dve")
                    incd("tp", sync.dma_start(
                        out=bass.AP(row_hbm.tensor, TPC, [[1, 128], [128, 8]]),
                        in_=STT[:, 56:64], allow_slow_non_contiguous=True))
                    wait(sync, "tp")
                    incd("tp", sync.dma_start(
                        out=ROW[0:1, TPC:2 * TPC],
                        in_=bass.AP(row_hbm.tensor, TPC, [[1, TPC]])))
                    wait(sync, "tp")
                    inc("aux", sync.nop())
                on("sync", s_r)
            kb.barrier()

        with ExitStack() as cL0:
            XN = cL0.enter_context(nc.sbuf_tensor(sbname("XNa"), [128, ND * TPC], BF16))
            emit_ln(X, False, 3, fold_R=False, h_out=None)
            tap("xn1", XN[:, :])
            if STAGE >= 2:
                with ExitStack() as cQ0:
                    QT = cQ0.enter_context(
                        nc.sbuf_tensor(sbname("QTa"), [128, ND * TPC], BF16))
                    emit_proj(WQ[0], 'fm_sb', 0, bias_col=0)
                    emit_proj(WK[0], 'fm', 0, bias_col=ND, dest=ag1_in[0], dest_off=0)
                    emit_proj(WV[0], 'tm', 0, dest=ag1_in[0], dest_off=V_OFF)
                    if STAGE >= 3:
                        emit_attn(0, causal=False)
                        tap("attnT0", XN[:, :])
            if STAGE >= 4:
                emit_proj(WO[0], 'res', 0, bias_col=3 * ND)
                tap("x1", X[:, :])
            if STAGE >= 5:
                emit_ln(X, False, 3, fold_R=True, h_out='h')
                emit_hquant(use_rsq=True, use_eps=True, rowsum=False)
                tap("qi0", XN[:, :])
                tap("stt", STT[:, :])
            if STAGE >= 6:
                emit_gv()
                emit_ustats(descale_tau_col=5)
        if STAGE >= 7:
            emit_mmK(0, woutP, u_hbm, src_fp16=True, quantize=True, tern3=True)
            tap("x2", X[:, :])
        if STAGE >= 8:
            with ExitStack() as cL1:
                XN = cL1.enter_context(nc.sbuf_tensor(sbname("XNb"), [128, ND * TPC], BF16))
                emit_ln(X, False, 3, fold_R=False, h_out=None)
                with ExitStack() as cQ1:
                    QT = cQ1.enter_context(
                        nc.sbuf_tensor(sbname("QTb"), [128, ND * TPC], BF16))
                    emit_proj(WQ[1], 'fm_sb', 1, bias_col=None)
                    tap("qt1", QT[:, :])
                    emit_proj(WK[1], 'fm', 1, bias_col=None, dest=ag1_in[1], dest_off=0)
                    emit_proj(WV[1], 'tm', 1, dest=ag1_in[1], dest_off=V_OFF)
                    emit_attn(1, causal=True)
                emit_proj(WO[1], 'res', 1, bias_col=None)
                tap("x3", X[:, :])
                if STAGE >= 9:
                    emit_ln(X, False, 3, fold_R=False, h_out=None)
                    emit_ln(XN, True, 4, fold_R=False, h_out='xq')
                    emit_hquant(use_rsq=False, use_eps=False, rowsum=True)
                    emit_gu()
        if STAGE >= 9:
            emit_mmK(1, wdP, p_hbm, src_fp16=False, quantize=False, tern3=False)

        kb.barrier()

        def fin(sync):
            incd("st", sync.dma_start(
                out=bass.AP(y_t.tensor, 0, [[TPC, 128], [128 * TPC, ND], [1, TPC]]),
                in_=bass.AP(X, 0, [[ND * TPC, 128], [TPC, ND], [1, TPC]])))
            wait(sync, "st")
        on("sync", fin)
    return nc


def shard_inputs(inputs):
    x = np.asarray(inputs["L0_attn_hidden_state"], np.float32)
    T = lambda a: np.ascontiguousarray(np.asarray(a, np.float32).T)
    C = lambda a: np.ascontiguousarray(np.asarray(a, np.float32))
    wp = np.asarray(inputs["L1_attn_w_project"], np.float32)
    wout_T = T(inputs["L0_ffn_w_out"])
    wd_T = T(inputs["L1_ffn_W_d"])
    woutP = np.ascontiguousarray(wout_T.reshape(F2, 16, 128).transpose(1, 0, 2)
                                 ).reshape(16 * F2, 128)
    wdP = np.ascontiguousarray(wd_T.reshape(F2, 16, 128).transpose(1, 0, 2)
                               ).reshape(16 * F2, 128)
    cstv = np.zeros((128, 12), np.float32)
    cstv[:, 0] = C32; cstv[:, 1] = C64; cstv[:, 2] = 0.75
    cstv[:, 3] = 1e-5; cstv[:, 4] = 1e-8; cstv[:, 5] = 1.0 / 2048.0
    cstv[:, 6] = 127.0
    cstv[:, 8] = -8.0
    cstv[:, 9] = -48.0
    cstv[:, 10] = np.float32(127 * np.log(2.0) + np.log(2.0 / 3.0))

    def colmaj(v):
        return np.ascontiguousarray(np.asarray(v, np.float32).reshape(ND, 128).T)
    shared = dict(
        cst=cstv,
        wq0=C(inputs["L0_attn_Wq"]), wk0=C(inputs["L0_attn_Wk"]),
        wv0=C(inputs["L0_attn_Wv"]), wo0=C(inputs["L0_attn_Wo"]),
        wgvT=T(inputs["L0_ffn_w_gv"]), woutP=woutP,
        wq1=T(wp[0:D]), wk1=T(wp[D:2 * D]), wv1=T(wp[2 * D:3 * D]),
        wo1=T(inputs["L1_attn_w_out"]),
        wgT=T(inputs["L1_ffn_W_g"]), wuT=T(inputs["L1_ffn_W_u"]), wdP=wdP)
    bq = colmaj(inputs["L0_attn_bq"]); bk = colmaj(inputs["L0_attn_bk"])
    bo = colmaj(inputs["L0_attn_bo"]); wn = colmaj(inputs["L0_ffn_w_norm"])
    bv = colmaj(inputs["L0_attn_bv"])
    vecs_shared = np.concatenate([bq, bk, bv, bo, wn], axis=1)
    maps = []
    for c in range(8):
        g, l = divmod(c, 4)
        vecsv = vecs_shared
        negdv = np.zeros((128, 64), np.float32)
        for tq in range(2):
            for ck in range(32):
                negdv[:, tq * 32 + ck] = 128 * ck - 1024 * l - 512 * tq
        ofsv = np.zeros((1, 96), np.int64)
        for hh in range(4):
            for r in range(4):
                col = 8 + (hh * 4 + r) * 3
                base = l * 4 * HD * TPC + r * BLK1 + hh * 128 * TPC
                ofsv[0, col] = base
                ofsv[0, col + 1] = base + D * TPC
                ofsv[0, col + 2] = l * 4 * HD + r * BLK1 + V_OFF + hh * 128
        for dc in range(ND):
            ofsv[0, 56 + dc] = l * TPC + dc * 128 * S
        for i in range(8):
            ofsv[0, 72 + i] = c * 256 * F1 + (i // 4) * 128 * F1 + (i % 4) * 4096
        for i in range(4):
            ofsv[0, 80 + i] = c * 16384 * 128 + i * 32 * 128 * 128
        m = dict(shared)
        m["vecs"] = vecsv
        m["negd"] = negdv
        m["x_t"] = np.ascontiguousarray(x[g, l * TPC:(l + 1) * TPC, :].T)
        m["ofs"] = ofsv.astype(np.int32)
        maps.append(m)
    return maps


def kernel(**inputs):
    maps = shard_inputs(inputs)
    nc = build_program(taps=TAPS)
    res = run_bass_kernel_spmd(nc, maps, core_ids=list(range(8)))
    out = np.zeros((B, S, D), np.float32)
    for c in range(8):
        g, l = divmod(c, 4)
        out[g, l * TPC:(l + 1) * TPC, :] = np.asarray(res.results[c]["y_t"]).T
    return out


# version 03
# speedup vs baseline: 1.0187x; 1.0187x over previous
import numpy as np
import ml_dtypes
from contextlib import ExitStack
import concourse.bass as bass
import concourse.mybir as mybir
from concourse.bass_utils import run_bass_kernel_spmd

F32 = mybir.dt.float32
BF16 = mybir.dt.bfloat16
FP16 = mybir.dt.float16
I32 = mybir.dt.int32
AF = mybir.ActivationFunctionType
ALU = mybir.AluOpType
AX = mybir.AxisListType

B, S, D, H, HD = 2, 4096, 2048, 16, 128
TPC = 1024
NT = TPC // 128
ND = D // 128
F1, F2 = 16384, 8192
C32 = 12582912.0
C64 = 6755399441055744.0
SM_SCALE = float(1.0 / np.sqrt(np.float64(HD)))
BLK1 = 2 * D * TPC
V_OFF = D * TPC

STAGE = 99
TAPS = []


class KB:

    def __init__(self, nc, block, ctx, sems):
        self.nc, self.block, self.ctx, self.sems = nc, block, ctx, sems

    def inc(self, sem, instr, k=1):
        instr.then_inc(self.sems[sem][0], k)
        self.sems[sem][1] += k

    def incd(self, sem, instr):
        self.inc(sem, instr, 16)

    def n(self, sem):
        return self.sems[sem][1]

    def wait(self, eng, sem, val=None):
        eng.wait_ge(self.sems[sem][0], self.sems[sem][1] if val is None else val)

    def on(self, engname, fn):
        getattr(self.block, engname)(fn)

    def barrier(self):
        engs = ("sync", "vector", "scalar", "tensor", "gpsimd")
        for en in engs:
            def f(eng):
                eng.drain()
                self.inc("ph", eng.nop())
            self.on(en, f)
        t = self.n("ph")
        for en in engs:
            def f(eng, t=t):
                eng.wait_ge(self.sems["ph"][0], t)
            self.on(en, f)


def build_program(taps=()):
    nc = bass.Bass("TRN2", target_bir_lowering=False, debug=False)

    def di(name, shape, dt=F32):
        return nc.dram_tensor(name, shape, dt, kind="ExternalInput").ap()

    x_t = di("x_t", [D, TPC])
    cst = di("cst", [128, 12])
    vecs = di("vecs", [128, 5 * ND])
    negd = di("negd", [128, 64])
    ofs = di("ofs", [1, 96], I32)
    wq0 = di("wq0", [D, D]); wk0 = di("wk0", [D, D]); wv0 = di("wv0", [D, D])
    wo0 = di("wo0", [D, D])
    wgvT = di("wgvT", [D, F1])
    woutP = di("woutP", [16 * F2, 128])
    wq1 = di("wq1", [D, D]); wk1 = di("wk1", [D, D]); wv1 = di("wv1", [D, D])
    wo1 = di("wo1", [D, D])
    wgT = di("wgT", [D, F2]); wuT = di("wuT", [D, F2])
    wdP = di("wdP", [16 * F2, 128])
    WQ = [wq0, wq1]; WK = [wk0, wk1]; WV = [wv0, wv1]; WO = [wo0, wo1]

    y_t = nc.dram_tensor("y_t", [D, TPC], F32, kind="ExternalOutput").ap()
    tap_out = {nm: nc.dram_tensor("tap_" + nm, sh, dt, kind="ExternalOutput").ap()
               for nm, sh, dt in taps}

    def dd(name, shape, dt):
        return nc.dram_tensor(name, shape, dt, kind="Internal").ap()

    ag1_in = [dd(f"ag1_in{L}", [2 * D, TPC], BF16) for L in range(2)]
    ag1_out = [dd(f"ag1_out{L}", [8 * D, TPC], BF16) for L in range(2)]
    ag2_in = [dd(f"ag2_in{L}", [4 * HD, S], BF16) for L in range(2)]
    ag2_out = [dd(f"ag2_out{L}", [D, S], BF16) for L in range(2)]
    ar_in = dd("ar_in", [1, 2], F32)
    ar_out = dd("ar_out", [1, 2], F32)
    t16_hbm = dd("t16_hbm", [NT * D * 128], FP16)
    qi_hbm = dd("qi_hbm", [ND * TPC * 128], BF16)
    u_hbm = dd("u_hbm", [64 * TPC * 128], FP16)
    p_hbm = dd("p_hbm", [64 * TPC * 128], BF16)
    x_hbm = dd("x_hbm", [D * TPC], F32)
    tern_hbm = dd("tern_hbm", [16 * F2, 128], BF16)
    row_hbm = dd("row_hbm", [4 * TPC], F32)

    with ExitStack() as ctx:
        E = ctx.enter_context
        X = E(nc.sbuf_tensor("X", [128, ND * TPC], F32))
        CS = E(nc.sbuf_tensor("CS", [128, 12], F32))
        VEC = E(nc.sbuf_tensor("VEC", [128, 5 * ND], F32))
        OFS = E(nc.sbuf_tensor("OFS", [1, 96], I32))
        IND = E(nc.sbuf_tensor("IND", [128, 2], F32))
        ONE1 = E(nc.sbuf_tensor("ONE1", [1, 128], F32))
        ONEF = E(nc.sbuf_tensor("ONEF", [128, 1], F32))
        ONEB = E(nc.sbuf_tensor("ONEB", [128, 2], BF16))
        TAU = E(nc.sbuf_tensor("TAU", [128, 8], F32))
        STT = E(nc.sbuf_tensor("STT", [128, 64], F32))
        ROW = E(nc.sbuf_tensor("ROW", [1, 4 * TPC], F32))
        NEGD = E(nc.sbuf_tensor("NEGD", [128, 64], F32))
        XN = None
        QT = None
        IOT0 = E(nc.sbuf_tensor("IOT0", [128, 512], F32))

        sems = {}
        for s in ("ph", "xld", "wld", "stat", "stA", "stB", "sgA", "sgB",
                  "shA", "shB", "ag", "mm", "dve", "act", "gp", "st", "tp",
                  "tpA", "tpB", "aux", "ev", "pq", "pqA", "pqB", "pr"):
            sems[s] = [E(nc.semaphore("sem_" + s)), 0]
        block = E(nc.Block())
        kb = KB(nc, block, ctx, sems)
        inc, incd, wait, on, n_ = kb.inc, kb.incd, kb.wait, kb.on, kb.n
        _uq = [0]
        def sbname(base):
            _uq[0] += 1
            return f"{base}_{_uq[0]}"

        def tap(name, src_ap):
            if name not in tap_out:
                return
            def f(sync):
                incd("st", sync.dma_start(out=tap_out[name][:, :], in_=src_ap))
                wait(sync, "st")
            on("sync", f)
            kb.barrier()

        def f(sync):
            incd("xld", sync.dma_start(out=CS[:, :], in_=cst[:, :]))
            incd("xld", sync.dma_start(out=VEC[:, :], in_=vecs[:, :]))
            incd("xld", sync.dma_start(out=OFS[:, :], in_=ofs[:, :]))
            incd("xld", sync.dma_start(out=NEGD[:, :], in_=negd[:, :]))
            incd("xld", sync.dma_start(
                out=bass.AP(X, 0, [[ND * TPC, 128], [TPC, ND], [1, TPC]]),
                in_=bass.AP(x_t.tensor, 0, [[TPC, 128], [128 * TPC, ND], [1, TPC]])))
        on("sync", f)

        def f(vector):
            wait(vector, "xld")
            vector.tensor_scalar(IND[:, 1:2], CS[:, 2:3], CS[:, 1:2], None, ALU.add)
            vector.drain()
            vector.tensor_scalar(IND[:, 0:1], IND[:, 1:2], CS[:, 1:2], None, ALU.subtract)
            inc("dve", vector.memset(ONE1[:, :], 1.0))
            inc("dve", vector.memset(ONEF[:, :], 1.0))
            inc("dve", vector.memset(ONEB[:, :], 1.0))
            inc("dve", vector.memset(STT[:, :], 0.0))
            inc("dve", vector.memset(ROW[0:1, :], 0.0))
        on("vector", f)

        def f(gpsimd):
            inc("gp", gpsimd.iota(IOT0[:, :], [[1, 512]], base=0,
                                  channel_multiplier=-1,
                                  allow_small_or_imprecise_dtypes=True))
        on("gpsimd", f)



        def stats_phase2():
            with ExitStack() as c2:
                WS = c2.enter_context(nc.sbuf_tensor(sbname("WS"), [128, 2 * 4096], F32))
                ACC = c2.enter_context(nc.sbuf_tensor(sbname("ACC"), [128, 8], F32))
                PSA = c2.enter_context(nc.psum_tensor(sbname("PSA"), [128, 8], F32))
                pieces = []
                for j in range(8):
                    pieces.append((wgvT, 3, (j // 4) * 128 * F1 + (j % 4) * 4096,
                                   [[F1, 128], [1, 4096]], 0))
                for j in range(4):
                    pieces.append((woutP, 4, j * 32 * 128 * 128,
                                   [[128, 128], [128 * 128, 32], [1, 128]], 1))

                def g(gpsimd):
                    wait(gpsimd, "xld")
                    for i, (src, col, off, dims, _w) in enumerate(pieces):
                        if i >= 2:
                            wait(gpsimd, "dve", n_("dve") - 0)
                red_base = n_("dve")
                for i, (src, col, off, dims, w_id) in enumerate(pieces):
                    def gl(gpsimd, i=i, src=src, dims=dims):
                        if i == 0:
                            wait(gpsimd, "xld")
                        if i >= 2:
                            wait(gpsimd, "dve", red_base + (i - 1))
                        with gpsimd.register() as r1:
                            gpsimd.reg_load(r1, OFS[0:1, 72 + i:73 + i])
                            incd("stA" if i % 2 == 0 else "stB", gpsimd.dma_start(
                                out=WS[:, (i % 2) * 4096:(i % 2 + 1) * 4096],
                                in_=bass.AP(src.tensor, r1, dims)))
                    on("gpsimd", gl)

                    def vr(vector, i=i, w_id=w_id):
                        wait(vector, "stA" if i % 2 == 0 else "stB",
                             16 * (i // 2 + 1))
                        if i == 0:
                            vector.memset(ACC[:, 0:2], 0.0)
                        vector.reduce_sum(ACC[:, 4 + (i % 2):5 + (i % 2)],
                                          WS[:, (i % 2) * 4096:(i % 2 + 1) * 4096],
                                          axis=AX.X, apply_absolute_value=True)
                        vector.drain()
                        inc("dve", vector.tensor_tensor(
                            ACC[:, w_id:w_id + 1], ACC[:, w_id:w_id + 1],
                            ACC[:, 4 + (i % 2):5 + (i % 2)], ALU.add))
                        vector.drain()
                    on("vector", vr)

                def vv(vector):
                    vector.drain()
                    inc("dve", vector.tensor_copy(ACC[:, 2:4], ACC[:, 0:2]))
                on("vector", vv)

                def t(tensor):
                    wait(tensor, "dve")
                    inc("mm", tensor.matmul(PSA[0:1, 0:2], ONEF[:, 0:1], ACC[:, 2:4],
                                            start=True, stop=True))
                on("tensor", t)

                def v3(vector):
                    wait(vector, "mm")
                    inc("dve", vector.tensor_copy(ACC[0:1, 0:2], PSA[0:1, 0:2]))
                on("vector", v3)

                def s(sync):
                    wait(sync, "dve")
                    incd("stat", sync.dma_start(out=ar_in[0:1, :], in_=ACC[0:1, 0:2]))
                on("sync", s)

                def g3(gpsimd):
                    wait(gpsimd, "stat")
                    inc("ag", gpsimd.collective_compute(
                        "AllReduce", ALU.add,
                        replica_groups=[[0, 1, 2, 3, 4, 5, 6, 7]],
                        ins=[ar_in[:, :]], outs=[ar_out[:, :]]))
                on("gpsimd", g3)

                def s2(sync):
                    wait(sync, "ag")
                    incd("stat", sync.dma_start(out=ACC[0:1, 0:2], in_=ar_out[0:1, :]))
                on("sync", s2)

                def v4(vector):
                    wait(vector, "stat")
                    vector.tensor_scalar(ACC[0:1, 0:1], ACC[0:1, 0:1],
                                         1.0 / (D * F1), None, ALU.mult)
                    vector.tensor_scalar(ACC[0:1, 1:2], ACC[0:1, 1:2],
                                         1.0 / (D * F2), None, ALU.mult)
                    vector.drain()
                    vector.tensor_scalar(ACC[0:1, 0:2], ACC[0:1, 0:2],
                                         CS[0:1, 3:4], None, ALU.max)
                    vector.drain()
                    inc("dve", vector.tensor_scalar(ACC[0:1, 2:4], ACC[0:1, 0:2],
                                                    0.5, None, ALU.mult))
                on("vector", v4)

                def t3(tensor):
                    wait(tensor, "dve")
                    inc("mm", tensor.matmul(PSA[:, 4:8], ONE1[0:1, :], ACC[0:1, 0:4],
                                            start=True, stop=True))
                on("tensor", t3)

                def v5(vector):
                    wait(vector, "mm")
                    vector.tensor_copy(TAU[:, 4:6], PSA[:, 4:6])
                    vector.tensor_copy(TAU[:, 0:1], PSA[:, 6:7])
                    vector.tensor_copy(TAU[:, 2:3], PSA[:, 7:8])
                    vector.drain()
                    inc("dve", vector.tensor_scalar(TAU[:, 1:2], TAU[:, 0:1],
                                                    -1.0, None, ALU.mult))
                    inc("dve", vector.tensor_scalar(TAU[:, 3:4], TAU[:, 2:3],
                                                    -1.0, None, ALU.mult))
                on("vector", v5)
            kb.barrier()
        stats_phase2()

        def emit_ln(src, src_bf16, eps_col, fold_R, h_out):
            with ExitStack() as c2:
                CB = c2.enter_context(nc.sbuf_tensor(sbname("lnCB"), [128, 2 * TPC], BF16))
                SQ = c2.enter_context(nc.sbuf_tensor(sbname("lnSQ"), [128, 2 * TPC], BF16))
                PS1 = c2.enter_context(nc.psum_tensor(sbname("lnPS1"), [1, TPC], F32))
                PS2 = c2.enter_context(nc.psum_tensor(sbname("lnPS2"), [1, TPC], F32))
                PSB = c2.enter_context(nc.psum_tensor(sbname("lnPSB"), [128, 2 * TPC], F32))
                HW16 = c2.enter_context(nc.sbuf_tensor(sbname("lnHW16"), [128, 2 * TPC], FP16))
                TMPA = c2.enter_context(nc.sbuf_tensor(sbname("lnTA"), [128, TPC], F32))
                TMPB = c2.enter_context(nc.sbuf_tensor(sbname("lnTB"), [128, TPC], F32))
                a0, d0, m0 = n_("act"), n_("dve"), n_("mm")
                tpa0, tpb0 = n_("tpA"), n_("tpB")
                for c in range(ND):
                    sl = slice(c * TPC, (c + 1) * TPC)
                    bsl = slice((c % 2) * TPC, (c % 2 + 1) * TPC)
                    if not src_bf16:
                        def vc(vector, c=c, sl=sl, bsl=bsl):
                            if c >= 2:
                                wait(vector, "mm", m0 + 4 * (c - 1))
                            inc("dve", vector.tensor_copy(CB[:, bsl], src[:, sl]))
                        on("vector", vc)
                    def sc(scalar, c=c, sl=sl, bsl=bsl):
                        if c == 0:
                            wait(scalar, "xld")
                        if c >= 2:
                            wait(scalar, "mm", m0 + 4 * (c - 1))
                        inc("act", scalar.activation(SQ[:, bsl], src[:, sl], AF.Square))
                    on("scalar", sc)
                    def tc(tensor, c=c, bsl=bsl):
                        if not src_bf16:
                            wait(tensor, "dve", d0 + (c + 1))
                        wait(tensor, "act", a0 + (c + 1))
                        for h in range(2):
                            hs = slice((c % 2) * TPC + 512 * h, (c % 2) * TPC + 512 * (h + 1))
                            os_ = slice(512 * h, 512 * (h + 1))
                            src_sl = (CB[:, hs] if not src_bf16 else
                                      src[:, c * TPC + 512 * h: c * TPC + 512 * (h + 1)])
                            inc("mm", tensor.matmul(PS1[0:1, os_], ONEB[:, 0:1], src_sl,
                                                    start=(c == 0), stop=(c == ND - 1)))
                            inc("mm", tensor.matmul(PS2[0:1, os_], ONEB[:, 0:1], SQ[:, hs],
                                                    start=(c == 0), stop=(c == ND - 1)))
                    on("tensor", tc)

                def ve(vector):
                    wait(vector, "mm")
                    vector.tensor_copy(ROW[0:1, 0:TPC], PS1[0:1, :])
                    vector.tensor_copy(ROW[0:1, TPC:2 * TPC], PS2[0:1, :])
                    vector.drain()
                    vector.tensor_scalar(ROW[0:1, 0:TPC], ROW[0:1, 0:TPC],
                                         CS[0:1, 5:6], None, ALU.mult)
                    vector.tensor_scalar(ROW[0:1, TPC:2 * TPC], ROW[0:1, TPC:2 * TPC],
                                         CS[0:1, 5:6], None, ALU.mult)
                    vector.drain()
                    vector.tensor_tensor(ROW[0:1, 2 * TPC:3 * TPC], ROW[0:1, 0:TPC],
                                         ROW[0:1, 0:TPC], ALU.mult)
                    vector.drain()
                    vector.tensor_tensor(ROW[0:1, TPC:2 * TPC], ROW[0:1, TPC:2 * TPC],
                                         ROW[0:1, 2 * TPC:3 * TPC], ALU.subtract)
                    vector.drain()
                    inc("dve", vector.tensor_scalar(
                        ROW[0:1, TPC:2 * TPC], ROW[0:1, TPC:2 * TPC],
                        CS[0:1, eps_col:eps_col + 1], None, ALU.add))
                on("vector", ve)

                def se(scalar):
                    wait(scalar, "dve")
                    scalar.activation(ROW[0:1, 2 * TPC:3 * TPC], ROW[0:1, TPC:2 * TPC],
                                      AF.Ln)
                    scalar.drain()
                    inc("act", scalar.activation(
                        ROW[0:1, TPC:2 * TPC], ROW[0:1, 2 * TPC:3 * TPC],
                        AF.Exp, scale=-0.5))
                on("scalar", se)

                def vf(vector):
                    wait(vector, "act")
                    inc("dve", vector.tensor_tensor(
                        ROW[0:1, 2 * TPC:3 * TPC], ROW[0:1, 0:TPC],
                        ROW[0:1, TPC:2 * TPC], ALU.mult))
                on("vector", vf)

                if h_out == 'h':
                    def vq(vector):
                        wait(vector, "act")
                        vector.tensor_tensor(ROW[0:1, 3 * TPC:4 * TPC],
                                             ROW[0:1, TPC:2 * TPC],
                                             ROW[0:1, TPC:2 * TPC], ALU.mult)
                        vector.drain()
                        inc("dve", vector.tensor_scalar(
                            ROW[0:1, 3 * TPC:4 * TPC], ROW[0:1, 3 * TPC:4 * TPC],
                            -1e-5, 1.0 + 1e-5, ALU.mult, ALU.add))
                    on("vector", vq)

                    def sq2(scalar):
                        wait(scalar, "dve")
                        scalar.activation(ROW[0:1, 3 * TPC:4 * TPC],
                                          ROW[0:1, 3 * TPC:4 * TPC], AF.Ln)
                        scalar.drain()
                        inc("act", scalar.activation(
                            ROW[0:1, 3 * TPC:4 * TPC], ROW[0:1, 3 * TPC:4 * TPC],
                            AF.Exp, scale=-0.5))
                    on("scalar", sq2)

                    def sq3(sync):
                        wait(sync, "act")
                        incd("tp", sync.dma_start(
                            out=bass.AP(row_hbm.tensor, 0, [[1, TPC]]),
                            in_=ROW[0:1, 3 * TPC:4 * TPC]))
                        wait(sync, "tp")
                        incd("tp", sync.dma_start(
                            out=STT[:, 32:40],
                            in_=bass.AP(row_hbm.tensor, 0, [[1, 128], [128, 8]]),
                            allow_slow_non_contiguous=True))
                        wait(sync, "tp")
                        inc("aux", sync.nop())
                    on("sync", sq3)

                def tb(tensor):
                    wait(tensor, "dve")
                    for h in range(2):
                        inc("mm", tensor.matmul(
                            PSB[:, 512 * h:512 * (h + 1)], ONE1[0:1, :],
                            ROW[0:1, TPC + 512 * h:TPC + 512 * (h + 1)],
                            start=True, stop=True))
                        inc("mm", tensor.matmul(
                            PSB[:, TPC + 512 * h:TPC + 512 * (h + 1)], ONE1[0:1, :],
                            ROW[0:1, 2 * TPC + 512 * h:2 * TPC + 512 * (h + 1)],
                            start=True, stop=True))
                on("tensor", tb)

                ms0 = n_("mm")
                for c in range(ND):
                    sl = slice(c * TPC, (c + 1) * TPC)
                    def va(vector, c=c, sl=sl):
                        if c == 0:
                            wait(vector, "mm", ms0)
                        vector.drain()
                        vector.tensor_tensor(TMPA[:, :], src[:, sl], PSB[:, 0:TPC],
                                             ALU.mult)
                        vector.drain()
                        if h_out == 'xq':
                            inc("dve", vector.tensor_tensor(TMPB[:, :], TMPA[:, :],
                                                            PSB[:, TPC:2 * TPC],
                                                            ALU.subtract))
                        elif fold_R:
                            vector.tensor_tensor(TMPB[:, :], TMPA[:, :],
                                                 PSB[:, TPC:2 * TPC], ALU.subtract)
                            vector.drain()
                            inc("dve", vector.tensor_tensor(X[:, sl], X[:, sl],
                                                            TMPB[:, :], ALU.add))
                        else:
                            inc("dve", vector.tensor_tensor(XN[:, sl], TMPA[:, :],
                                                            PSB[:, TPC:2 * TPC],
                                                            ALU.subtract))
                        if h_out == 'h':
                            vector.drain()
                            inc("dve", vector.tensor_scalar(
                                TMPA[:, :], TMPB[:, :],
                                VEC[:, 4 * ND + c:4 * ND + c + 1], None, ALU.mult))
                    on("vector", va)
                    if h_out is not None:
                        def vh(vector, c=c):
                            if c >= 2:
                                wait(vector, "tpA" if c % 2 == 0 else "tpB",
                                     tpa0 + 16 * ((c - 2) // 2 + 1) if c % 2 == 0
                                     else tpb0 + 16 * ((c - 2) // 2 + 1))
                            vector.drain()
                            inc("dve", vector.tensor_copy(
                                HW16[:, (c % 2) * TPC:(c % 2 + 1) * TPC],
                                TMPA[:, :] if h_out == 'h' else TMPB[:, :]))
                        on("vector", vh)
                        def sh(sync, c=c):
                            wait(sync, "dve")
                            incd("tpA" if c % 2 == 0 else "tpB", sync.dma_start(
                                out=bass.AP(t16_hbm.tensor, c * 128 * 128,
                                            [[128, 128], [D * 128, NT], [1, 128]]),
                                in_=HW16[:, (c % 2) * TPC:(c % 2 + 1) * TPC]))
                        on("sync", sh)
                if h_out is not None:
                    def sw(sync):
                        wait(sync, "tpA")
                        wait(sync, "tpB")
                        inc("aux", sync.nop())
                    on("sync", sw)
            kb.barrier()

        def emit_proj(wdram, mode, li, bias_col=None, dest=None, dest_off=0):
            with ExitStack() as c2:
                WPN = c2.enter_context(nc.sbuf_tensor(sbname("pjW"), [128, 2 * ND * 512], BF16))
                STG = c2.enter_context(nc.sbuf_tensor(sbname("pjS"), [128, 2 * TPC], BF16))
                PSQa = c2.enter_context(nc.psum_tensor(sbname("pjP"), [128, TPC], F32))
                PSQb = c2.enter_context(nc.psum_tensor(sbname("pjP"), [128, TPC], F32))
                PSQs = (PSQa, PSQb)
                wlsem = ("stA", "stB")
                wl0 = (n_("stA"), n_("stB"))
                sgsem = ("sgA", "sgB")
                sg0 = (n_("sgA"), n_("sgB"))
                mm0, ev0 = n_("mm"), n_("ev")

                for p in range(4):
                    def gw(gpsimd, p=p):
                        if p == 0:
                            wait(gpsimd, "ph")
                        if p >= 2:
                            wait(gpsimd, "mm", mm0 + 4 * (p - 1) *
                                 (32 if mode != 'tm' else 32))
                        incd(wlsem[p % 2], gpsimd.dma_start(
                            out=bass.AP(WPN, (p % 2) * ND * 512,
                                        [[2 * ND * 512, 128], [512, ND], [1, 512]]),
                            in_=bass.AP(wdram.tensor, 512 * p,
                                        [[D, 128], [128 * D, ND], [1, 512]])))
                    on("gpsimd", gw)

                if mode in ('fm', 'fm_sb', 'res'):
                    for p in range(4):
                        for ocr in range(4):
                            oc = 4 * p + ocr
                            def tm(tensor, p=p, ocr=ocr, oc=oc):
                                if ocr == 0:
                                    wait(tensor, wlsem[p % 2], 16 * (wl0[p % 2] // 16 + p // 2 + 1))
                                if oc >= 2:
                                    wait(tensor, "ev", ev0 + oc - 1)
                                for th in range(2):
                                    for dc in range(ND):
                                        tensor.matmul(
                                            PSQs[oc % 2][:, th * 512:(th + 1) * 512],
                                            WPN[:, (p % 2) * ND * 512 + dc * 512 +
                                                ocr * 128: (p % 2) * ND * 512 +
                                                dc * 512 + (ocr + 1) * 128],
                                            XN[:, dc * TPC + th * 512:
                                               dc * TPC + (th + 1) * 512],
                                            start=(dc == 0), stop=(dc == ND - 1))
                                tensor.drain()
                                inc("mm", tensor.nop(), 32)
                            on("tensor", tm)

                            if mode == 'fm_sb':
                                def se_(scalar, oc=oc):
                                    wait(scalar, "mm", mm0 + 32 * (oc + 1))
                                    scalar.drain()
                                    inc("ev", scalar.activation(
                                        QT[:, oc * TPC:(oc + 1) * TPC],
                                        PSQs[oc % 2][:, :],
                                        AF.Identity,
                                        bias=(VEC[:, bias_col + oc:bias_col + oc + 1]
                                              if bias_col is not None else CS[:, 7:8])))
                                on("scalar", se_)
                            elif mode == 'fm':
                                def se_(scalar, oc=oc):
                                    wait(scalar, "mm", mm0 + 32 * (oc + 1))
                                    if oc >= 2:
                                        wait(scalar, sgsem[oc % 2],
                                             sg0[oc % 2] + 16 * ((oc - 2) // 2 + 1))
                                    scalar.drain()
                                    inc("ev", scalar.activation(
                                        STG[:, (oc % 2) * TPC:(oc % 2 + 1) * TPC],
                                        PSQs[oc % 2][:, :],
                                        AF.Identity,
                                        bias=(VEC[:, bias_col + oc:bias_col + oc + 1]
                                              if bias_col is not None else CS[:, 7:8])))
                                on("scalar", se_)
                                def sd(sync, oc=oc):
                                    wait(sync, "ev", ev0 + oc + 1)
                                    incd(sgsem[oc % 2], sync.dma_start(
                                        out=bass.AP(dest.tensor, dest_off + oc * 128 * TPC,
                                                    [[TPC, 128], [1, TPC]]),
                                        in_=STG[:, (oc % 2) * TPC:(oc % 2 + 1) * TPC]))
                                on("sync", sd)
                            else:
                                def ve_(vector, oc=oc, li=li):
                                    wait(vector, "mm", mm0 + 32 * (oc + 1))
                                    vector.drain()
                                    if bias_col is not None:
                                        inc("ev", vector.scalar_tensor_tensor(
                                            X[:, oc * TPC:(oc + 1) * TPC],
                                            PSQs[oc % 2][:, :],
                                            VEC[:, bias_col + oc:bias_col + oc + 1],
                                            X[:, oc * TPC:(oc + 1) * TPC],
                                            ALU.add, ALU.add))
                                    else:
                                        inc("ev", vector.tensor_tensor(
                                            X[:, oc * TPC:(oc + 1) * TPC],
                                            X[:, oc * TPC:(oc + 1) * TPC],
                                            PSQs[oc % 2][:, :],
                                            ALU.add))
                                on("vector", ve_)
                else:
                    for p in range(4):
                        for tt in range(NT):
                            idx = p * NT + tt
                            def tm(tensor, p=p, tt=tt, idx=idx):
                                if tt == 0:
                                    wait(tensor, wlsem[p % 2], 16 * (wl0[p % 2] // 16 + p // 2 + 1))
                                if idx >= 2:
                                    wait(tensor, "ev", ev0 + idx - 1)
                                for dc in range(ND):
                                    tensor.matmul(
                                        PSQs[idx % 2][:, 0:512],
                                        XN[:, dc * TPC + tt * 128:dc * TPC + (tt + 1) * 128],
                                        WPN[:, (p % 2) * ND * 512 + dc * 512:
                                            (p % 2) * ND * 512 + (dc + 1) * 512],
                                        start=(dc == 0), stop=(dc == ND - 1))
                                tensor.drain()
                                inc("mm", tensor.nop(), 16)
                            on("tensor", tm)

                            def ve_(vector, p=p, tt=tt, idx=idx):
                                wait(vector, "mm", mm0 + 16 * (idx + 1))
                                if idx >= 2:
                                    wait(vector, sgsem[idx % 2],
                                         sg0[idx % 2] + 16 * ((idx - 2) // 2 + 1))
                                vector.drain()
                                inc("ev", vector.tensor_copy(
                                    STG[:, (idx % 2) * TPC:(idx % 2) * TPC + 512],
                                    PSQs[idx % 2][:, 0:512]))
                            on("vector", ve_)

                            def sd(sync, p=p, tt=tt, idx=idx):
                                wait(sync, "ev", ev0 + idx + 1)
                                incd(sgsem[idx % 2], sync.dma_start(
                                    out=bass.AP(dest.tensor,
                                                dest_off + tt * 128 * D + p * 512,
                                                [[D, 128], [1, 512]]),
                                    in_=STG[:, (idx % 2) * TPC:(idx % 2) * TPC + 512]))
                            on("sync", sd)
                def fin_w(sync):
                    wait(sync, "sgA")
                    wait(sync, "sgB")
                on("sync", fin_w)
            kb.barrier()

        def emit_attn(li, causal):
            def g_ag(gpsimd):
                wait(gpsimd, "ph")
                inc("ag", gpsimd.collective_compute(
                    "AllGather", ALU.bypass,
                    replica_groups=[[0, 1, 2, 3], [4, 5, 6, 7]],
                    ins=[ag1_in[li][:, :]], outs=[ag1_out[li][:, :]]))
            on("gpsimd", g_ag)

            with ExitStack() as c2:
                KH = c2.enter_context(nc.sbuf_tensor(sbname("atK"), [128, 2 * S], BF16))
                VH = c2.enter_context(nc.sbuf_tensor(sbname("atV"), [128, 2 * S], BF16))
                PRB = c2.enter_context(nc.sbuf_tensor(sbname("atP"), [128, 4 * 512], BF16))
                MKB = c2.enter_context(nc.sbuf_tensor(sbname("atM"), [128, 4 * 512], BF16))
                RB = c2.enter_context(nc.sbuf_tensor(sbname("atR"), [128, 512], F32))
                T1 = c2.enter_context(nc.sbuf_tensor(sbname("atT"), [128, 512], F32))
                ZR = c2.enter_context(nc.sbuf_tensor(sbname("atZ"), [1, 512], F32))
                ZY = c2.enter_context(nc.sbuf_tensor(sbname("atY"), [1, 512], F32))
                ZT = c2.enter_context(nc.sbuf_tensor(sbname("atZT"), [1, 512], F32))
                psSs = tuple(c2.enter_context(nc.psum_tensor(sbname("atPS"), [128, 512], F32))
                             for _ in range(2))
                psO = c2.enter_context(nc.psum_tensor(sbname("atPO"), [128, 512], F32))
                psZ = c2.enter_context(nc.psum_tensor(sbname("atPZ"), [1, 512], F32))
                psB = c2.enter_context(nc.psum_tensor(sbname("atPB"), [128, 512], F32))
                pq0 = (n_("pqA"), n_("pqB"))

                for h in range(H):
                    hs = (h % 2) * S
                    def gl(sync, h=h, hs=hs):
                        if h == 0:
                            wait(sync, "ag")
                        elif h >= 2:
                            wait(sync, "ev", n_("ev"))
                        pqs = "pqA" if h % 2 == 0 else "pqB"
                        for r in range(4):
                            incd(pqs, sync.dma_start(
                                out=KH[:, hs + r * TPC:hs + (r + 1) * TPC],
                                in_=bass.AP(ag1_out[li].tensor,
                                            r * BLK1 + h * 128 * TPC,
                                            [[TPC, 128], [1, TPC]])))
                            incd(pqs, sync.dma_start(
                                out=bass.AP(VH, hs + r * 8 * 128,
                                            [[2 * S, 128], [128, 8], [1, 128]]),
                                in_=bass.AP(ag1_out[li].tensor,
                                            r * BLK1 + V_OFF + h * 128,
                                            [[D, 128], [128 * D, 8], [1, 128]])))
                    on("sync", gl)
                    pq_t = ("pqA" if h % 2 == 0 else "pqB",
                            n_("pqA" if h % 2 == 0 else "pqB"))

                    for tq in range(2):
                        exp_done = {}
                        msk_done = {}
                        mm_qk = {}

                        def emit_qk(tensor, h, hs, tq, ck, pq_t):
                            if ck <= 1 and tq == 0:
                                wait(tensor, pq_t[0], pq_t[1])
                            if ck >= 2:
                                wait(tensor, "act", exp_done[ck - 2])
                            inc("mm", tensor.matmul(
                                psSs[ck % 2][:, :],
                                KH[:, hs + ck * 128:hs + (ck + 1) * 128],
                                QT[:, h * TPC + tq * 512:h * TPC + (tq + 1) * 512],
                                start=True, stop=True))
                            mm_qk[ck] = n_("mm")

                        def emit_exp(scalar, li, ck):
                            wait(scalar, "mm", mm_qk[ck])
                            if ck >= 4:
                                wait(scalar, "mm", n_("mm"))
                            scalar.drain()
                            inc("act", scalar.activation(
                                PRB[:, (ck % 4) * 512:(ck % 4 + 1) * 512],
                                psSs[ck % 2][:, :],
                                AF.Exp, scale=SM_SCALE, bias=CS[:, 8 + li:9 + li]))
                            exp_done[ck] = n_("act")

                        def emit_msk(vector, tq, ck):
                            wait(vector, "act", exp_done[ck])
                            vector.drain()
                            vector.tensor_scalar(
                                MKB[:, (ck % 4) * 512:(ck % 4 + 1) * 512],
                                IOT0[:, :],
                                NEGD[:, tq * 32 + ck:tq * 32 + ck + 1],
                                None, ALU.is_ge)
                            vector.drain()
                            inc("dve", vector.tensor_tensor(
                                PRB[:, (ck % 4) * 512:(ck % 4 + 1) * 512],
                                PRB[:, (ck % 4) * 512:(ck % 4 + 1) * 512],
                                MKB[:, (ck % 4) * 512:(ck % 4 + 1) * 512], ALU.mult))
                            msk_done[ck] = n_("dve")

                        def emit_pv(tensor, hs, ck, causal):
                            if causal:
                                wait(tensor, "dve", msk_done[ck])
                            else:
                                wait(tensor, "act", exp_done[ck])
                            inc("mm", tensor.matmul(
                                psO[:, :], VH[:, hs + ck * 128:hs + (ck + 1) * 128],
                                PRB[:, (ck % 4) * 512:(ck % 4 + 1) * 512],
                                start=(ck == 0), stop=(ck == 31)))
                            inc("mm", tensor.matmul(
                                psZ[0:1, :], ONEB[:, 0:1],
                                PRB[:, (ck % 4) * 512:(ck % 4 + 1) * 512],
                                start=(ck == 0), stop=(ck == 31)))

                        def t_pre(tensor, h=h, hs=hs, tq=tq, pq_t=pq_t):
                            emit_qk(tensor, h, hs, tq, 0, pq_t)
                            emit_qk(tensor, h, hs, tq, 1, pq_t)
                        on("tensor", t_pre)
                        for ck in range(32):
                            def s_e(scalar, li=li, ck=ck):
                                emit_exp(scalar, li, ck)
                            on("scalar", s_e)
                            if causal:
                                def v_m(vector, tq=tq, ck=ck):
                                    emit_msk(vector, tq, ck)
                                on("vector", v_m)
                            def t_b(tensor, h=h, hs=hs, tq=tq, ck=ck,
                                    pq_t=pq_t, causal=causal):
                                emit_pv(tensor, hs, ck, causal)
                                if ck + 2 < 32:
                                    emit_qk(tensor, h, hs, tq, ck + 2, pq_t)
                            on("tensor", t_b)

                        mm_t = n_("mm")
                        def v_z(vector, mm_t=mm_t):
                            wait(vector, "mm", mm_t)
                            vector.drain()
                            vector.tensor_copy(ZR[0:1, :], psZ[0:1, :])
                            vector.drain()
                            vector.tensor_scalar(
                                ZT[0:1, :].bitcast(I32), ZR[0:1, :].bitcast(I32),
                                23, None, ALU.logical_shift_right)
                            vector.drain()
                            inc("dve", vector.tensor_copy(ZY[0:1, :],
                                                          ZT[0:1, :].bitcast(I32)))
                        on("vector", v_z)

                        def s_seed(scalar):
                            wait(scalar, "dve")
                            scalar.drain()
                            inc("act", scalar.activation(
                                ZY[0:1, :], ZY[0:1, :], AF.Exp,
                                scale=-0.6931471805599453, bias=CS[0:1, 10:11]))
                        on("scalar", s_seed)

                        def v_nt(vector):
                            wait(vector, "act")
                            for _ in range(4):
                                vector.drain()
                                vector.tensor_tensor(ZT[0:1, :], ZR[0:1, :],
                                                     ZY[0:1, :], ALU.mult)
                                vector.drain()
                                vector.tensor_scalar(ZT[0:1, :], ZT[0:1, :], -1.0,
                                                     2.0, ALU.mult, ALU.add)
                                vector.drain()
                                vector.tensor_tensor(ZY[0:1, :], ZY[0:1, :],
                                                     ZT[0:1, :], ALU.mult)
                            vector.drain()
                            inc("dve", vector.nop())
                        on("vector", v_nt)

                        def t_bc(tensor):
                            wait(tensor, "dve")
                            inc("mm", tensor.matmul(
                                psB[:, :], ONE1[0:1, :], ZY[0:1, :],
                                start=True, stop=True))
                        on("tensor", t_bc)

                        def v_ev(vector, li=li, h=h, tq=tq):
                            wait(vector, "mm")
                            vector.drain()
                            vector.tensor_copy(RB[:, :], psB[:, :])
                            vector.drain()
                            vector.tensor_tensor(T1[:, :], psO[:, :], RB[:, :], ALU.mult)
                            vector.drain()
                            if li == 0:
                                inc("ev", vector.tensor_scalar(
                                    XN[:, h * TPC + tq * 512:h * TPC + (tq + 1) * 512],
                                    T1[:, :],
                                    VEC[:, 2 * ND + h:2 * ND + h + 1], None, ALU.add))
                            else:
                                inc("ev", vector.tensor_copy(
                                    XN[:, h * TPC + tq * 512:h * TPC + (tq + 1) * 512],
                                    T1[:, :]))
                        on("vector", v_ev)

                        def t_gate(tensor):
                            wait(tensor, "ev")
                            inc("mm", tensor.nop())
                        on("tensor", t_gate)
            kb.barrier()
        def emit_hquant(use_rsq, use_eps, rowsum):
            with ExitStack() as c2:
                TMQ = c2.enter_context(nc.sbuf_tensor(sbname("qT"), [128, NT * D], FP16))
                W0 = c2.enter_context(nc.sbuf_tensor(sbname("qW"), [128, D], F32))
                R32 = c2.enter_context(nc.sbuf_tensor(sbname("qR3"), [128, D], F32))
                R64 = c2.enter_context(nc.sbuf_tensor(sbname("qR6"), [128, D], F32))
                QTI = c2.enter_context(nc.sbuf_tensor(sbname("qQ"), [128, 2 * D], BF16))
                tp0 = (n_("tpA"), n_("tpB"))
                def s_tp(sync):
                    wait(sync, "ph")
                    for j in range(NT):
                        incd("tpA" if j % 2 == 0 else "tpB", sync.dma_start_transpose(
                            TMQ[:, j * D:(j + 1) * D],
                            bass.AP(t16_hbm.tensor, j * D * 128, [[128, D], [1, 128]])))
                on("sync", s_tp)

                def v_rm(vector):
                    wait(vector, "tpA")
                    wait(vector, "tpB")
                    for j in range(NT):
                        vector.drain()
                        inc("dve", vector.reduce_max(
                            STT[:, j:j + 1], TMQ[:, j * D:(j + 1) * D],
                            axis=AX.X, apply_absolute_value=True))
                on("vector", v_rm)

                def v_sc(vector):
                    wait(vector, "dve")
                    wait(vector, "aux")
                    vector.drain()
                    if use_rsq:
                        vector.tensor_tensor(STT[:, 40:48], STT[:, 0:8],
                                             STT[:, 32:40], ALU.mult)
                    else:
                        vector.tensor_copy(STT[:, 40:48], STT[:, 0:8])
                    vector.drain()
                    if use_eps:
                        vector.tensor_scalar(STT[:, 40:48], STT[:, 40:48],
                                             CS[:, 3:4], None, ALU.max)
                        vector.drain()
                    inc("dve", vector.nop())
                on("vector", v_sc)

                def s_sc(scalar):
                    wait(scalar, "dve")
                    scalar.drain()
                    scalar.activation(STT[:, 48:56], STT[:, 40:48], AF.Ln)
                    scalar.drain()
                    inc("act", scalar.activation(STT[:, 48:56], STT[:, 48:56],
                                                 AF.Exp, scale=-1.0))
                on("scalar", s_sc)

                def v_sc2(vector):
                    wait(vector, "act")
                    vector.drain()
                    vector.tensor_scalar(STT[:, 48:56], STT[:, 48:56], 127.0,
                                         None, ALU.mult)
                    vector.tensor_scalar(STT[:, 8:16], STT[:, 40:48], 1.0 / 127.0,
                                         None, ALU.mult)
                    vector.drain()
                    if use_rsq:
                        inc("dve", vector.tensor_tensor(
                            STT[:, 16:24], STT[:, 48:56], STT[:, 32:40], ALU.mult))
                    else:
                        inc("dve", vector.tensor_copy(STT[:, 16:24], STT[:, 48:56]))
                on("vector", v_sc2)

                tq0 = (n_("shA"), n_("shB"))
                for j in range(NT):
                    def v_q(vector, j=j):
                        if j == 0:
                            wait(vector, "dve")
                        if j >= 2:
                            wait(vector, "shA" if j % 2 == 0 else "shB",
                                 tq0[j % 2] + 16 * (j // 2))
                        vector.drain()
                        vector.tensor_scalar(W0[:, :], TMQ[:, j * D:(j + 1) * D],
                                             STT[:, 16 + j:17 + j], None, ALU.mult)
                        vector.drain()
                        vector.tensor_scalar(R32[:, :], W0[:, :], CS[:, 0:1], None, ALU.add)
                        vector.drain()
                        vector.tensor_scalar(R32[:, :], R32[:, :], CS[:, 0:1], None, ALU.subtract)
                        vector.tensor_scalar(R64[:, :], W0[:, :], CS[:, 1:2], None, ALU.add)
                        vector.drain()
                        vector.tensor_scalar(R64[:, :], R64[:, :], CS[:, 1:2], None, ALU.subtract)
                        vector.drain()
                        vector.tensor_tensor(R64[:, :], R64[:, :], R32[:, :], ALU.subtract)
                        vector.drain()
                        inc("dve", vector.scalar_tensor_tensor(
                            QTI[:, (j % 2) * D:(j % 2 + 1) * D], R64[:, :],
                            IND[:, 0:1], R32[:, :], ALU.mult, ALU.add))
                        if rowsum:
                            vector.drain()
                            inc("dve", vector.reduce_sum(
                                STT[:, 24 + j:25 + j],
                                QTI[:, (j % 2) * D:(j % 2 + 1) * D], axis=AX.X))
                    on("vector", v_q)

                    def s_sc3(sync, j=j):
                        wait(sync, "dve")
                        incd("shA" if j % 2 == 0 else "shB", sync.dma_start(
                            out=bass.AP(qi_hbm.tensor, j * 128 * 128,
                                        [[128, 128], [TPC * 128, ND], [1, 128]]),
                            in_=bass.AP(QTI, (j % 2) * D,
                                        [[2 * D, 128], [128, ND], [1, 128]])))
                    on("sync", s_sc3)

                def s_fin(sync):
                    wait(sync, "shA")
                    wait(sync, "shB")
                    for dc in range(ND):
                        incd("tp", sync.dma_start_transpose(
                            XN[:, dc * TPC:(dc + 1) * TPC],
                            bass.AP(qi_hbm.tensor, dc * TPC * 128, [[128, TPC], [1, 128]])))
                    wait(sync, "tp")
                    inc("aux", sync.nop())
                on("sync", s_fin)
            kb.barrier()

        def emit_gv():
            PW = 256
            with ExitStack() as c2:
                WF = c2.enter_context(nc.sbuf_tensor(sbname("gW"), [128, 2 * ND * PW], F32))
                WG = c2.enter_context(nc.sbuf_tensor(sbname("gG"), [128, 2 * ND * PW], F32))
                TV = c2.enter_context(nc.sbuf_tensor(sbname("gTV"), [128, ND * PW], BF16))
                TG = c2.enter_context(nc.sbuf_tensor(sbname("gTG"), [128, ND * PW], BF16))
                SA = c2.enter_context(nc.sbuf_tensor(sbname("gSA"), [128, PW], F32))
                SB_ = c2.enter_context(nc.sbuf_tensor(sbname("gSB"), [128, PW], F32))
                GS = c2.enter_context(nc.sbuf_tensor(sbname("gGS"), [128, 2 * PW], F32))
                UO = c2.enter_context(nc.sbuf_tensor(sbname("gUO"), [128, 2 * PW], FP16))
                SC2 = c2.enter_context(nc.sbuf_tensor(sbname("gS2"), [128, 8], F32))
                psVs = tuple(c2.enter_context(nc.psum_tensor(sbname("gPV"), [128, PW], F32))
                             for _ in range(2))
                psGs = tuple(c2.enter_context(nc.psum_tensor(sbname("gPG"), [128, PW], F32))
                             for _ in range(2))
                wl0 = (n_("shA"), n_("shB"))
                u0 = (n_("sgA"), n_("sgB"))
                evb = n_("ev")

                def v_pre(vector):
                    wait(vector, "ph")
                    vector.drain()
                    inc("dve", vector.tensor_scalar(SC2[:, :], STT[:, 8:16],
                                                    TAU[:, 4:5], None, ALU.mult))
                on("vector", v_pre)

                for p in range(32):
                    def s_w(sync, p=p):
                        if p >= 2:
                            wait(sync, "mm", n_("mm"))
                        incd("shA" if p % 2 == 0 else "shB", sync.dma_start(
                            out=bass.AP(WF, (p % 2) * ND * PW,
                                        [[2 * ND * PW, 128], [PW, ND], [1, PW]]),
                            in_=bass.AP(wgvT.tensor, PW * p,
                                        [[F1, 128], [128 * F1, ND], [1, PW]])))
                        incd("shA" if p % 2 == 0 else "shB", sync.dma_start(
                            out=bass.AP(WG, (p % 2) * ND * PW,
                                        [[2 * ND * PW, 128], [PW, ND], [1, PW]]),
                            in_=bass.AP(wgvT.tensor, F2 + PW * p,
                                        [[F1, 128], [128 * F1, ND], [1, PW]])))
                    on("sync", s_w)

                    def v_t(vector, p=p):
                        wait(vector, "shA" if p % 2 == 0 else "shB",
                             wl0[p % 2] + 32 * (p // 2 + 1))
                        if p >= 1:
                            wait(vector, "mm", n_("mm"))
                        for dc in range(ND):
                            for (src, dst, base) in ((WF, TV, (p % 2) * ND * PW),
                                                     (WG, TG, (p % 2) * ND * PW)):
                                ssl = slice(base + dc * PW, base + (dc + 1) * PW)
                                dsl = slice(dc * PW, (dc + 1) * PW)
                                vector.drain()
                                vector.tensor_scalar(SA[:, :], src[:, ssl],
                                                     TAU[:, 0:1], None, ALU.is_ge)
                                vector.tensor_scalar(SB_[:, :], src[:, ssl],
                                                     TAU[:, 1:2], None, ALU.is_le)
                                vector.drain()
                                vector.tensor_tensor(dst[:, dsl], SA[:, :], SB_[:, :],
                                                     ALU.subtract)
                        vector.drain()
                        inc("dve", vector.nop())
                    on("vector", v_t)
                    tern_t = n_("dve")

                    for tt in range(NT):
                        idx = p * NT + tt
                        def t_m(tensor, p=p, tt=tt, idx=idx, tern_t=tern_t):
                            if tt == 0:
                                wait(tensor, "dve", tern_t)
                            if idx >= 2:
                                wait(tensor, "ev", evb + idx - 1)
                            for dc in range(ND):
                                tensor.matmul(
                                    psVs[tt % 2][:, :],
                                    XN[:, dc * TPC + tt * 128:dc * TPC + (tt + 1) * 128],
                                    TV[:, dc * PW:(dc + 1) * PW],
                                    start=(dc == 0), stop=(dc == ND - 1))
                                tensor.matmul(
                                    psGs[tt % 2][:, :],
                                    XN[:, dc * TPC + tt * 128:dc * TPC + (tt + 1) * 128],
                                    TG[:, dc * PW:(dc + 1) * PW],
                                    start=(dc == 0), stop=(dc == ND - 1))
                            tensor.drain()
                            inc("mm", tensor.nop(), 32)
                        on("tensor", t_m)
                        mmv = n_("mm")

                        def s_g(scalar, tt=tt, idx=idx, mmv=mmv):
                            wait(scalar, "mm", mmv)
                            if idx >= 2:
                                wait(scalar, "ev", evb + idx - 1)
                            scalar.drain()
                            inc("act", scalar.activation(
                                GS[:, (idx % 2) * PW:(idx % 2 + 1) * PW],
                                psGs[tt % 2][:, :],
                                AF.Silu, scale=SC2[:, tt:tt + 1]))
                        on("scalar", s_g)
                        actv = n_("act")

                        def v_u(vector, p=p, tt=tt, idx=idx, actv=actv):
                            wait(vector, "act", actv)
                            if idx >= 2:
                                wait(vector, "sgA" if idx % 2 == 0 else "sgB",
                                     u0[idx % 2] + 16 * ((idx - 2) // 2 + 1))
                            vector.drain()
                            vector.scalar_tensor_tensor(
                                UO[:, (idx % 2) * PW:(idx % 2 + 1) * PW],
                                psVs[tt % 2][:, :], SC2[:, tt:tt + 1],
                                GS[:, (idx % 2) * PW:(idx % 2 + 1) * PW],
                                ALU.mult, ALU.mult)
                            vector.drain()
                            vector.reduce_max(STT[:, 48 + tt:49 + tt],
                                              UO[:, (idx % 2) * PW:(idx % 2 + 1) * PW],
                                              axis=AX.X, apply_absolute_value=True)
                            vector.drain()
                            if p == 0:
                                inc("dve", vector.tensor_copy(STT[:, 0 + tt:1 + tt],
                                                              STT[:, 48 + tt:49 + tt]))
                            else:
                                inc("dve", vector.tensor_tensor(
                                    STT[:, 0 + tt:1 + tt], STT[:, 0 + tt:1 + tt],
                                    STT[:, 48 + tt:49 + tt], ALU.max))
                            inc("ev", vector.nop())
                        on("vector", v_u)

                        def s_u(sync, p=p, tt=tt, idx=idx):
                            wait(sync, "ev")
                            incd("sgA" if idx % 2 == 0 else "sgB", sync.dma_start(
                                out=bass.AP(u_hbm.tensor,
                                            2 * p * TPC * 128 + tt * 128 * 128,
                                            [[128, 128], [TPC * 128, 2], [1, 128]]),
                                in_=bass.AP(UO, (idx % 2) * PW,
                                            [[2 * PW, 128], [128, 2], [1, 128]])))
                        on("sync", s_u)

                def s_wd(sync):
                    wait(sync, "sgA")
                    wait(sync, "sgB")
                    inc("aux", sync.nop())
                on("sync", s_wd)
            kb.barrier()

        def emit_ustats(descale_tau_col, out_negate2=False):
            def v_s(vector):
                wait(vector, "ph")
                vector.drain()
                vector.tensor_scalar(STT[:, 40:48], STT[:, 0:8], CS[:, 3:4],
                                     None, ALU.max)
                vector.drain()
                inc("dve", vector.nop())
            on("vector", v_s)

            def s_s(scalar):
                wait(scalar, "dve")
                scalar.drain()
                scalar.activation(STT[:, 48:56], STT[:, 40:48], AF.Ln)
                scalar.drain()
                inc("act", scalar.activation(STT[:, 48:56], STT[:, 48:56],
                                             AF.Exp, scale=-1.0))
            on("scalar", s_s)

            def v_s2(vector):
                wait(vector, "act")
                vector.drain()
                vector.tensor_scalar(STT[:, 48:56], STT[:, 48:56], 127.0, None, ALU.mult)
                vector.drain()
                vector.tensor_copy(STT[:, 16:24], STT[:, 48:56])
                vector.tensor_scalar(STT[:, 56:64], STT[:, 40:48], 1.0 / 127.0,
                                     None, ALU.mult)
                vector.drain()
                inc("dve", vector.tensor_scalar(STT[:, 56:64], STT[:, 56:64],
                                                TAU[:, descale_tau_col:descale_tau_col + 1],
                                                None, ALU.mult))
            on("vector", v_s2)

            def s_r(sync):
                wait(sync, "dve")
                incd("tp", sync.dma_start(
                    out=bass.AP(row_hbm.tensor, 0, [[1, 128], [128, 8]]),
                    in_=STT[:, 16:24], allow_slow_non_contiguous=True))
                incd("tp", sync.dma_start(
                    out=bass.AP(row_hbm.tensor, TPC, [[1, 128], [128, 8]]),
                    in_=STT[:, 56:64], allow_slow_non_contiguous=True))
                wait(sync, "tp")
                incd("tp", sync.dma_start(
                    out=ROW[0:1, 0:TPC],
                    in_=bass.AP(row_hbm.tensor, 0, [[1, TPC]])))
                incd("tp", sync.dma_start(
                    out=ROW[0:1, TPC:2 * TPC],
                    in_=bass.AP(row_hbm.tensor, TPC, [[1, TPC]])))
                wait(sync, "tp")
                inc("aux", sync.nop())
            on("sync", s_r)
            kb.barrier()
        def emit_mmK(li, w_panels, src_hbm, src_fp16, quantize, tern3):
            with ExitStack() as c2:
                BIG = c2.enter_context(nc.sbuf_tensor(sbname("mB"), [128, 64 * 512], BF16))
                CUB = c2.enter_context(nc.sbuf_tensor(sbname("mC"), [128, TPC], F32))
                DDB = c2.enter_context(nc.sbuf_tensor(sbname("mD"), [128, TPC], F32))
                UT = c2.enter_context(nc.sbuf_tensor(sbname("mU"), [128, 2 * 512],
                                                     FP16 if src_fp16 else BF16))
                W0 = c2.enter_context(nc.sbuf_tensor(sbname("mW0"), [128, 512], F32))
                R32 = c2.enter_context(nc.sbuf_tensor(sbname("mR3"), [128, 512], F32))
                R64 = c2.enter_context(nc.sbuf_tensor(sbname("mR6"), [128, 512], F32))
                WQF = c2.enter_context(nc.sbuf_tensor(sbname("mWQ"), [128, 2048], F32))
                SA = c2.enter_context(nc.sbuf_tensor(sbname("mSA"), [128, 512], F32))
                SB_ = c2.enter_context(nc.sbuf_tensor(sbname("mSB"), [128, 512], F32))
                TD = c2.enter_context(nc.sbuf_tensor(sbname("mTD"), [128, 64 * 128], BF16))
                XR = c2.enter_context(nc.sbuf_tensor(sbname("mXR"), [128, 2 * 512], F32))
                T2 = c2.enter_context(nc.sbuf_tensor(sbname("mT2"), [128, 512], F32))
                psO2s = tuple(c2.enter_context(nc.psum_tensor(sbname("mPO"), [128, 512], F32))
                              for _ in range(2))
                psBC = c2.enter_context(nc.psum_tensor(sbname("mPB"), [128, 2 * TPC], F32))

                def s_sp(sync):
                    wait(sync, "ph")
                    incd("tp", sync.dma_start(
                        out=bass.AP(x_hbm.tensor, 0, [[ND * TPC, 128], [1, ND * TPC]]),
                        in_=X[:, :]))
                    wait(sync, "tp")
                    inc("aux", sync.nop())
                on("sync", s_sp)

                def t_bc(tensor):
                    wait(tensor, "ph")
                    for h2 in range(2):
                        inc("mm", tensor.matmul(
                            psBC[:, h2 * 512:(h2 + 1) * 512], ONE1[0:1, :],
                            ROW[0:1, h2 * 512:(h2 + 1) * 512], start=True, stop=True))
                        inc("mm", tensor.matmul(
                            psBC[:, TPC + h2 * 512:TPC + (h2 + 1) * 512], ONE1[0:1, :],
                            ROW[0:1, TPC + h2 * 512:TPC + (h2 + 1) * 512],
                            start=True, stop=True))
                on("tensor", t_bc)

                def v_bc(vector):
                    wait(vector, "mm")
                    vector.drain()
                    vector.tensor_copy(CUB[:, :], psBC[:, 0:TPC])
                    inc("dve", vector.tensor_copy(DDB[:, :], psBC[:, TPC:2 * TPC]))
                on("vector", v_bc)

                for half in range(2):
                    tp0 = (n_("tpA"), n_("tpB"))
                    for kc in range(64):
                        def s_tp(sync, kc=kc, half=half):
                            if half == 1 and kc < 2:
                                wait(sync, "mm", n_("mm"))
                            if kc >= 2:
                                wait(sync, "dve", n_("dve"))
                            incd("tpA" if kc % 2 == 0 else "tpB",
                                 sync.dma_start_transpose(
                                     (UT[:, (kc % 2) * 512:(kc % 2 + 1) * 512]
                                      if quantize else
                                      BIG[:, kc * 512:(kc + 1) * 512]),
                                     bass.AP(src_hbm.tensor,
                                             kc * TPC * 128 + half * 512 * 128,
                                             [[128, 512], [1, 128]])))
                        on("sync", s_tp)

                        if quantize:
                            def v_q(vector, kc=kc, half=half, tp0=tp0):
                                wait(vector, "tpA" if kc % 2 == 0 else "tpB",
                                     tp0[kc % 2] + 16 * (kc // 2 + 1))
                                if kc == 0:
                                    wait(vector, "dve")
                                vector.drain()
                                vector.tensor_tensor(
                                    W0[:, :], UT[:, (kc % 2) * 512:(kc % 2 + 1) * 512],
                                    CUB[:, half * 512:(half + 1) * 512], ALU.mult)
                                vector.drain()
                                vector.tensor_scalar(R32[:, :], W0[:, :], CS[:, 0:1],
                                                     None, ALU.add)
                                vector.drain()
                                vector.tensor_scalar(R32[:, :], R32[:, :], CS[:, 0:1],
                                                     None, ALU.subtract)
                                vector.tensor_scalar(R64[:, :], W0[:, :], CS[:, 1:2],
                                                     None, ALU.add)
                                vector.drain()
                                vector.tensor_scalar(R64[:, :], R64[:, :], CS[:, 1:2],
                                                     None, ALU.subtract)
                                vector.drain()
                                vector.tensor_tensor(R64[:, :], R64[:, :], R32[:, :],
                                                     ALU.subtract)
                                vector.drain()
                                inc("dve", vector.scalar_tensor_tensor(
                                    BIG[:, kc * 512:(kc + 1) * 512], R64[:, :],
                                    IND[:, 0:1], R32[:, :], ALU.mult, ALU.add))
                            on("vector", v_q)
                    big_t = (n_("dve"), n_("tpA"), n_("tpB"))

                    wq0 = (n_("stA"), n_("stB"))
                    for o2c in range(16):
                        gidx = half * 16 + o2c
                        if tern3 and half == 0:
                            for q in range(4):
                                def s_w(sync, o2c=o2c, q=q):
                                    if o2c + q > 0:
                                        wait(sync, "dve", n_("dve"))
                                    incd("shA", sync.dma_start(
                                        out=bass.AP(WQF, 0,
                                                    [[2048, 128], [128, 16], [1, 128]]),
                                        in_=bass.AP(w_panels.tensor,
                                                    o2c * F2 * 128 + q * 2048 * 128,
                                                    [[128, 128], [128 * 128, 16], [1, 128]])))
                                on("sync", s_w)
                                def v_t(vector, o2c=o2c, q=q):
                                    wait(vector, "shA", n_("shA"))
                                    if q == 0:
                                        wait(vector, "mm", n_("mm"))
                                        wait(vector, "tp", n_("tp"))
                                    for sl4 in range(4):
                                        ssl = slice(sl4 * 512, (sl4 + 1) * 512)
                                        vector.drain()
                                        vector.tensor_scalar(SA[:, :], WQF[:, ssl],
                                                             TAU[:, 2:3], None, ALU.is_ge)
                                        vector.tensor_scalar(SB_[:, :], WQF[:, ssl],
                                                             TAU[:, 3:4], None, ALU.is_le)
                                        vector.drain()
                                        vector.tensor_tensor(
                                            TD[:, q * 2048 + sl4 * 512:
                                               q * 2048 + (sl4 + 1) * 512],
                                            SA[:, :], SB_[:, :], ALU.subtract)
                                    vector.drain()
                                    inc("dve", vector.nop())
                                on("vector", v_t)
                            def s_tc(sync, o2c=o2c):
                                wait(sync, "dve")
                                incd("tp", sync.dma_start(
                                    out=bass.AP(tern_hbm.tensor, o2c * F2 * 128,
                                                [[128, 128], [128 * 128, 64], [1, 128]]),
                                    in_=bass.AP(TD, 0, [[64 * 128, 128], [128, 64], [1, 128]])))
                            on("sync", s_tc)
                            td_t = ("dve", n_("dve"))
                        elif tern3:
                            def s_w(sync, o2c=o2c):
                                wait(sync, "tp", n_("tp"))
                                if o2c > 0:
                                    wait(sync, "mm", n_("mm"))
                                incd("tp", sync.dma_start(
                                    out=bass.AP(TD, 0, [[64 * 128, 128], [128, 64], [1, 128]]),
                                    in_=bass.AP(tern_hbm.tensor, o2c * F2 * 128,
                                                [[128, 128], [128 * 128, 64], [1, 128]])))
                            on("sync", s_w)
                            td_t = ("tp", n_("tp"))
                        else:
                            for q in range(4):
                                def s_w(sync, o2c=o2c, q=q):
                                    if o2c + q > 0:
                                        wait(sync, "dve", n_("dve"))
                                    incd("shA", sync.dma_start(
                                        out=bass.AP(WQF, 0,
                                                    [[2048, 128], [128, 16], [1, 128]]),
                                        in_=bass.AP(w_panels.tensor,
                                                    o2c * F2 * 128 + q * 2048 * 128,
                                                    [[128, 128], [128 * 128, 16], [1, 128]])))
                                on("sync", s_w)
                                def v_t(vector, o2c=o2c, q=q):
                                    wait(vector, "shA", n_("shA"))
                                    if q == 0:
                                        wait(vector, "mm", n_("mm"))
                                    vector.drain()
                                    inc("dve", vector.tensor_scalar(
                                        TD[:, q * 2048:(q + 1) * 2048],
                                        WQF[:, 0:2048], 0.0, None, ALU.is_ge))
                                on("vector", v_t)
                            td_t = ("dve", n_("dve"))

                        def t_m(tensor, half=half, o2c=o2c, gidx=gidx,
                                td_t=td_t, big_t=big_t):
                            wait(tensor, td_t[0], td_t[1])
                            if quantize:
                                wait(tensor, "dve", big_t[0])
                            wait(tensor, "tpA", big_t[1])
                            wait(tensor, "tpB", big_t[2])
                            if gidx >= 2:
                                wait(tensor, "ev", n_("ev"))
                            for kc in range(64):
                                tensor.matmul(
                                    psO2s[gidx % 2][:, :],
                                    TD[:, kc * 128:(kc + 1) * 128],
                                    BIG[:, kc * 512:(kc + 1) * 512],
                                    start=(kc == 0), stop=(kc == 63))
                            tensor.drain()
                            inc("mm", tensor.nop(), 64)
                        on("tensor", t_m)
                        mmv = n_("mm")

                        def s_x(sync, half=half, o2c=o2c, gidx=gidx):
                            wait(sync, "aux")
                            if gidx >= 2:
                                wait(sync, "ev", n_("ev"))
                            incd("sgA" if gidx % 2 == 0 else "sgB", sync.dma_start(
                                out=XR[:, (gidx % 2) * 512:(gidx % 2 + 1) * 512],
                                in_=bass.AP(x_hbm.tensor, o2c * TPC + half * 512,
                                            [[ND * TPC, 128], [1, 512]])))
                        on("sync", s_x)
                        sg_t = (n_("sgA"), n_("sgB"))

                        def v_e(vector, half=half, o2c=o2c, gidx=gidx, mmv=mmv, sg_t=sg_t):
                            wait(vector, "mm", mmv)
                            wait(vector, "sgA" if gidx % 2 == 0 else "sgB",
                                 sg_t[gidx % 2])
                            vector.drain()
                            if tern3:
                                vector.tensor_tensor(
                                    T2[:, :], psO2s[gidx % 2][:, :],
                                    DDB[:, half * 512:(half + 1) * 512], ALU.mult)
                            else:
                                vector.scalar_tensor_tensor(
                                    T2[:, :], psO2s[gidx % 2][:, :],
                                    2.0, DDB[:, half * 512:(half + 1) * 512],
                                    ALU.mult, ALU.subtract)
                            vector.drain()
                            inc("ev", vector.tensor_tensor(
                                X[:, o2c * TPC + half * 512:o2c * TPC + (half + 1) * 512],
                                XR[:, (gidx % 2) * 512:(gidx % 2 + 1) * 512],
                                T2[:, :], ALU.add))
                        on("vector", v_e)
            kb.barrier()

        def emit_gu():
            PW = 256
            with ExitStack() as c2:
                WF = c2.enter_context(nc.sbuf_tensor(sbname("hWF"), [128, 2 * ND * PW], F32))
                WG = c2.enter_context(nc.sbuf_tensor(sbname("hWG"), [128, 2 * ND * PW], F32))
                TV = c2.enter_context(nc.sbuf_tensor(sbname("hTV"), [128, ND * PW], BF16))
                TG = c2.enter_context(nc.sbuf_tensor(sbname("hTG"), [128, ND * PW], BF16))
                GS = c2.enter_context(nc.sbuf_tensor(sbname("hGS"), [128, 2 * PW], F32))
                US = c2.enter_context(nc.sbuf_tensor(sbname("hUS"), [128, PW], F32))
                PO = c2.enter_context(nc.sbuf_tensor(sbname("hPO"), [128, 2 * PW], BF16))
                SG2 = c2.enter_context(nc.sbuf_tensor(sbname("hS2"), [128, 8], F32))
                BG2 = c2.enter_context(nc.sbuf_tensor(sbname("hB2"), [128, 8], F32))
                psGs = tuple(c2.enter_context(nc.psum_tensor(sbname("hPG"), [128, PW], F32))
                             for _ in range(2))
                psUs = tuple(c2.enter_context(nc.psum_tensor(sbname("hPU"), [128, PW], F32))
                             for _ in range(2))
                wl0 = (n_("shA"), n_("shB"))
                u0 = (n_("sgA"), n_("sgB"))
                evb = n_("ev")

                def v_pre(vector):
                    wait(vector, "ph")
                    vector.drain()
                    vector.tensor_scalar(SG2[:, :], STT[:, 8:16], 2.0, None, ALU.mult)
                    vector.tensor_tensor(BG2[:, :], STT[:, 24:32], STT[:, 8:16], ALU.mult)
                    vector.drain()
                    vector.tensor_scalar(BG2[:, :], BG2[:, :], -1.0, None, ALU.mult)
                    vector.memset(STT[:, 56:64], 0.0)
                    vector.drain()
                    inc("dve", vector.nop())
                on("vector", v_pre)

                for p in range(32):
                    def s_w(sync, p=p):
                        if p >= 2:
                            wait(sync, "mm", n_("mm"))
                        incd("shA" if p % 2 == 0 else "shB", sync.dma_start(
                            out=bass.AP(WF, (p % 2) * ND * PW,
                                        [[2 * ND * PW, 128], [PW, ND], [1, PW]]),
                            in_=bass.AP(wgT.tensor, PW * p,
                                        [[F2, 128], [128 * F2, ND], [1, PW]])))
                        incd("shA" if p % 2 == 0 else "shB", sync.dma_start(
                            out=bass.AP(WG, (p % 2) * ND * PW,
                                        [[2 * ND * PW, 128], [PW, ND], [1, PW]]),
                            in_=bass.AP(wuT.tensor, PW * p,
                                        [[F2, 128], [128 * F2, ND], [1, PW]])))
                    on("sync", s_w)

                    def v_t(vector, p=p):
                        wait(vector, "shA" if p % 2 == 0 else "shB",
                             wl0[p % 2] + 32 * (p // 2 + 1))
                        if p >= 1:
                            wait(vector, "mm", n_("mm"))
                        for dc in range(ND):
                            base = (p % 2) * ND * PW
                            ssl = slice(base + dc * PW, base + (dc + 1) * PW)
                            dsl = slice(dc * PW, (dc + 1) * PW)
                            vector.drain()
                            vector.tensor_scalar(TV[:, dsl], WF[:, ssl], 0.0, None, ALU.is_ge)
                            vector.tensor_scalar(TG[:, dsl], WG[:, ssl], 0.0, None, ALU.is_ge)
                        vector.drain()
                        inc("dve", vector.nop())
                    on("vector", v_t)
                    tern_t = n_("dve")

                    for tt in range(NT):
                        idx = p * NT + tt
                        def t_m(tensor, p=p, tt=tt, idx=idx, tern_t=tern_t):
                            if tt == 0:
                                wait(tensor, "dve", tern_t)
                            if idx >= 2:
                                wait(tensor, "ev", evb + idx - 1)
                            for dc in range(ND):
                                tensor.matmul(
                                    psGs[tt % 2][:, :],
                                    XN[:, dc * TPC + tt * 128:dc * TPC + (tt + 1) * 128],
                                    TV[:, dc * PW:(dc + 1) * PW],
                                    start=(dc == 0), stop=(dc == ND - 1))
                                tensor.matmul(
                                    psUs[tt % 2][:, :],
                                    XN[:, dc * TPC + tt * 128:dc * TPC + (tt + 1) * 128],
                                    TG[:, dc * PW:(dc + 1) * PW],
                                    start=(dc == 0), stop=(dc == ND - 1))
                            tensor.drain()
                            inc("mm", tensor.nop(), 32)
                        on("tensor", t_m)
                        mmv = n_("mm")

                        def s_g(scalar, tt=tt, idx=idx, mmv=mmv):
                            wait(scalar, "mm", mmv)
                            if idx >= 2:
                                wait(scalar, "ev", evb + idx - 1)
                            scalar.drain()
                            inc("act", scalar.activation(
                                GS[:, (idx % 2) * PW:(idx % 2 + 1) * PW],
                                psGs[tt % 2][:, :],
                                AF.Silu, scale=SG2[:, tt:tt + 1], bias=BG2[:, tt:tt + 1]))
                        on("scalar", s_g)
                        actv = n_("act")

                        def v_u(vector, p=p, tt=tt, idx=idx, actv=actv):
                            wait(vector, "act", actv)
                            if idx >= 2:
                                wait(vector, "sgA" if idx % 2 == 0 else "sgB",
                                     u0[idx % 2] + 16 * ((idx - 2) // 2 + 1))
                            vector.drain()
                            vector.tensor_scalar(
                                US[:, :], psUs[tt % 2][:, :],
                                SG2[:, tt:tt + 1], BG2[:, tt:tt + 1], ALU.mult, ALU.add)
                            vector.drain()
                            vector.tensor_tensor(
                                PO[:, (idx % 2) * PW:(idx % 2 + 1) * PW],
                                GS[:, (idx % 2) * PW:(idx % 2 + 1) * PW],
                                US[:, :], ALU.mult)
                            vector.drain()
                            vector.reduce_sum(STT[:, 48 + tt:49 + tt],
                                              PO[:, (idx % 2) * PW:(idx % 2 + 1) * PW],
                                              axis=AX.X)
                            vector.drain()
                            inc("dve", vector.tensor_tensor(
                                STT[:, 56 + tt:57 + tt], STT[:, 56 + tt:57 + tt],
                                STT[:, 48 + tt:49 + tt], ALU.add))
                            inc("ev", vector.nop())
                        on("vector", v_u)

                        def s_u(sync, p=p, tt=tt, idx=idx):
                            wait(sync, "ev")
                            incd("sgA" if idx % 2 == 0 else "sgB", sync.dma_start(
                                out=bass.AP(p_hbm.tensor,
                                            2 * p * TPC * 128 + tt * 128 * 128,
                                            [[128, 128], [TPC * 128, 2], [1, 128]]),
                                in_=bass.AP(PO, (idx % 2) * PW,
                                            [[2 * PW, 128], [128, 2], [1, 128]])))
                        on("sync", s_u)

                def s_wd(sync):
                    wait(sync, "sgA")
                    wait(sync, "sgB")
                    inc("aux", sync.nop())
                on("sync", s_wd)

                def s_r(sync):
                    wait(sync, "dve")
                    incd("tp", sync.dma_start(
                        out=bass.AP(row_hbm.tensor, TPC, [[1, 128], [128, 8]]),
                        in_=STT[:, 56:64], allow_slow_non_contiguous=True))
                    wait(sync, "tp")
                    incd("tp", sync.dma_start(
                        out=ROW[0:1, TPC:2 * TPC],
                        in_=bass.AP(row_hbm.tensor, TPC, [[1, TPC]])))
                    wait(sync, "tp")
                    inc("aux", sync.nop())
                on("sync", s_r)
            kb.barrier()

        with ExitStack() as cL0:
            XN = cL0.enter_context(nc.sbuf_tensor(sbname("XNa"), [128, ND * TPC], BF16))
            emit_ln(X, False, 3, fold_R=False, h_out=None)
            tap("xn1", XN[:, :])
            if STAGE >= 2:
                with ExitStack() as cQ0:
                    QT = cQ0.enter_context(
                        nc.sbuf_tensor(sbname("QTa"), [128, ND * TPC], BF16))
                    emit_proj(WQ[0], 'fm_sb', 0, bias_col=0)
                    emit_proj(WK[0], 'fm', 0, bias_col=ND, dest=ag1_in[0], dest_off=0)
                    emit_proj(WV[0], 'tm', 0, dest=ag1_in[0], dest_off=V_OFF)
                    if STAGE >= 3:
                        emit_attn(0, causal=False)
                        tap("attnT0", XN[:, :])
            if STAGE >= 4:
                emit_proj(WO[0], 'res', 0, bias_col=3 * ND)
                tap("x1", X[:, :])
            if STAGE >= 5:
                emit_ln(X, False, 3, fold_R=True, h_out='h')
                emit_hquant(use_rsq=True, use_eps=True, rowsum=False)
                tap("qi0", XN[:, :])
                tap("stt", STT[:, :])
            if STAGE >= 6:
                emit_gv()
                emit_ustats(descale_tau_col=5)
        if STAGE >= 7:
            emit_mmK(0, woutP, u_hbm, src_fp16=True, quantize=True, tern3=True)
            tap("x2", X[:, :])
        if STAGE >= 8:
            with ExitStack() as cL1:
                XN = cL1.enter_context(nc.sbuf_tensor(sbname("XNb"), [128, ND * TPC], BF16))
                emit_ln(X, False, 3, fold_R=False, h_out=None)
                with ExitStack() as cQ1:
                    QT = cQ1.enter_context(
                        nc.sbuf_tensor(sbname("QTb"), [128, ND * TPC], BF16))
                    emit_proj(WQ[1], 'fm_sb', 1, bias_col=None)
                    tap("qt1", QT[:, :])
                    emit_proj(WK[1], 'fm', 1, bias_col=None, dest=ag1_in[1], dest_off=0)
                    emit_proj(WV[1], 'tm', 1, dest=ag1_in[1], dest_off=V_OFF)
                    emit_attn(1, causal=True)
                emit_proj(WO[1], 'res', 1, bias_col=None)
                tap("x3", X[:, :])
                if STAGE >= 9:
                    emit_ln(X, False, 3, fold_R=False, h_out=None)
                    emit_ln(XN, True, 4, fold_R=False, h_out='xq')
                    emit_hquant(use_rsq=False, use_eps=False, rowsum=True)
                    emit_gu()
        if STAGE >= 9:
            emit_mmK(1, wdP, p_hbm, src_fp16=False, quantize=False, tern3=False)

        kb.barrier()

        def fin(sync):
            incd("st", sync.dma_start(
                out=bass.AP(y_t.tensor, 0, [[TPC, 128], [128 * TPC, ND], [1, TPC]]),
                in_=bass.AP(X, 0, [[ND * TPC, 128], [TPC, ND], [1, TPC]])))
            wait(sync, "st")
        on("sync", fin)
    return nc


def shard_inputs(inputs):
    x = np.asarray(inputs["L0_attn_hidden_state"], np.float32)
    T = lambda a: np.ascontiguousarray(np.asarray(a, np.float32).T)
    C = lambda a: np.ascontiguousarray(np.asarray(a, np.float32))
    wp = np.asarray(inputs["L1_attn_w_project"], np.float32)
    wout_T = T(inputs["L0_ffn_w_out"])
    wd_T = T(inputs["L1_ffn_W_d"])
    woutP = np.ascontiguousarray(wout_T.reshape(F2, 16, 128).transpose(1, 0, 2)
                                 ).reshape(16 * F2, 128)
    wdP = np.ascontiguousarray(wd_T.reshape(F2, 16, 128).transpose(1, 0, 2)
                               ).reshape(16 * F2, 128)
    cstv = np.zeros((128, 12), np.float32)
    cstv[:, 0] = C32; cstv[:, 1] = C64; cstv[:, 2] = 0.75
    cstv[:, 3] = 1e-5; cstv[:, 4] = 1e-8; cstv[:, 5] = 1.0 / 2048.0
    cstv[:, 6] = 127.0
    cstv[:, 8] = -8.0
    cstv[:, 9] = -48.0
    cstv[:, 10] = np.float32(127 * np.log(2.0) + np.log(2.0 / 3.0))

    def colmaj(v):
        return np.ascontiguousarray(np.asarray(v, np.float32).reshape(ND, 128).T)
    shared = dict(
        cst=cstv,
        wq0=C(inputs["L0_attn_Wq"]), wk0=C(inputs["L0_attn_Wk"]),
        wv0=C(inputs["L0_attn_Wv"]), wo0=C(inputs["L0_attn_Wo"]),
        wgvT=T(inputs["L0_ffn_w_gv"]), woutP=woutP,
        wq1=T(wp[0:D]), wk1=T(wp[D:2 * D]), wv1=T(wp[2 * D:3 * D]),
        wo1=T(inputs["L1_attn_w_out"]),
        wgT=T(inputs["L1_ffn_W_g"]), wuT=T(inputs["L1_ffn_W_u"]), wdP=wdP)
    bq = colmaj(inputs["L0_attn_bq"]); bk = colmaj(inputs["L0_attn_bk"])
    bo = colmaj(inputs["L0_attn_bo"]); wn = colmaj(inputs["L0_ffn_w_norm"])
    bv = colmaj(inputs["L0_attn_bv"])
    vecs_shared = np.concatenate([bq, bk, bv, bo, wn], axis=1)
    maps = []
    for c in range(8):
        g, l = divmod(c, 4)
        vecsv = vecs_shared
        negdv = np.zeros((128, 64), np.float32)
        for tq in range(2):
            for ck in range(32):
                negdv[:, tq * 32 + ck] = 128 * ck - 1024 * l - 512 * tq
        ofsv = np.zeros((1, 96), np.int64)
        for hh in range(4):
            for r in range(4):
                col = 8 + (hh * 4 + r) * 3
                base = l * 4 * HD * TPC + r * BLK1 + hh * 128 * TPC
                ofsv[0, col] = base
                ofsv[0, col + 1] = base + D * TPC
                ofsv[0, col + 2] = l * 4 * HD + r * BLK1 + V_OFF + hh * 128
        for dc in range(ND):
            ofsv[0, 56 + dc] = l * TPC + dc * 128 * S
        for i in range(8):
            ofsv[0, 72 + i] = c * 256 * F1 + (i // 4) * 128 * F1 + (i % 4) * 4096
        for i in range(4):
            ofsv[0, 80 + i] = c * 16384 * 128 + i * 32 * 128 * 128
        m = dict(shared)
        m["vecs"] = vecsv
        m["negd"] = negdv
        m["x_t"] = np.ascontiguousarray(x[g, l * TPC:(l + 1) * TPC, :].T)
        m["ofs"] = ofsv.astype(np.int32)
        maps.append(m)
    return maps


def kernel(**inputs):
    maps = shard_inputs(inputs)
    nc = build_program(taps=TAPS)
    res = run_bass_kernel_spmd(nc, maps, core_ids=list(range(8)))
    out = np.zeros((B, S, D), np.float32)
    for c in range(8):
        g, l = divmod(c, 4)
        out[g, l * TPC:(l + 1) * TPC, :] = np.asarray(res.results[c]["y_t"]).T
    return out


# version 04
# speedup vs baseline: 1.3324x; 1.3079x over previous
import numpy as np
import ml_dtypes
from contextlib import ExitStack
import concourse.bass as bass
import concourse.mybir as mybir
from concourse.bass_utils import run_bass_kernel_spmd

F32 = mybir.dt.float32
BF16 = mybir.dt.bfloat16
FP16 = mybir.dt.float16
I32 = mybir.dt.int32
AF = mybir.ActivationFunctionType
ALU = mybir.AluOpType
AX = mybir.AxisListType

B, S, D, H, HD = 2, 4096, 2048, 16, 128
TPC = 1024
NT = TPC // 128
ND = D // 128
F1, F2 = 16384, 8192
C32 = 12582912.0
C64 = 6755399441055744.0
SM_SCALE = float(1.0 / np.sqrt(np.float64(HD)))
BLK1 = 2 * D * TPC
V_OFF = D * TPC

STAGE = 99
TAPS = []


class KB:

    def __init__(self, nc, block, ctx, sems):
        self.nc, self.block, self.ctx, self.sems = nc, block, ctx, sems

    def inc(self, sem, instr, k=1):
        instr.then_inc(self.sems[sem][0], k)
        self.sems[sem][1] += k

    def incd(self, sem, instr):
        self.inc(sem, instr, 16)

    def n(self, sem):
        return self.sems[sem][1]

    def wait(self, eng, sem, val=None):
        eng.wait_ge(self.sems[sem][0], self.sems[sem][1] if val is None else val)

    def on(self, engname, fn):
        getattr(self.block, engname)(fn)

    def barrier(self):
        engs = ("sync", "vector", "scalar", "tensor", "gpsimd")
        for en in engs:
            def f(eng):
                eng.drain()
                self.inc("ph", eng.nop())
            self.on(en, f)
        t = self.n("ph")
        for en in engs:
            def f(eng, t=t):
                eng.wait_ge(self.sems["ph"][0], t)
            self.on(en, f)


def build_program(taps=()):
    nc = bass.Bass("TRN2", target_bir_lowering=False, debug=False)

    def di(name, shape, dt=F32):
        return nc.dram_tensor(name, shape, dt, kind="ExternalInput").ap()

    x_t = di("x_t", [D, TPC])
    cst = di("cst", [128, 12])
    vecs = di("vecs", [128, 5 * ND])
    negd = di("negd", [128, 64])
    ofs = di("ofs", [1, 96], I32)
    wq0 = di("wq0", [D, D]); wk0 = di("wk0", [D, D]); wv0 = di("wv0", [D, D])
    wo0 = di("wo0", [D, D])
    wgvT = di("wgvT", [D, F1])
    woutP = di("woutP", [16 * F2, 128])
    wq1 = di("wq1", [D, D]); wk1 = di("wk1", [D, D]); wv1 = di("wv1", [D, D])
    wo1 = di("wo1", [D, D])
    wgT = di("wgT", [D, F2]); wuT = di("wuT", [D, F2])
    wdP = di("wdP", [16 * F2, 128])
    WQ = [wq0, wq1]; WK = [wk0, wk1]; WV = [wv0, wv1]; WO = [wo0, wo1]

    y_t = nc.dram_tensor("y_t", [D, TPC], F32, kind="ExternalOutput").ap()
    tap_out = {nm: nc.dram_tensor("tap_" + nm, sh, dt, kind="ExternalOutput").ap()
               for nm, sh, dt in taps}

    def dd(name, shape, dt):
        return nc.dram_tensor(name, shape, dt, kind="Internal").ap()

    ag1_in = [dd(f"ag1_in{L}", [2 * D, TPC], BF16) for L in range(2)]
    ag1_out = [dd(f"ag1_out{L}", [8 * D, TPC], BF16) for L in range(2)]
    ag2_in = [dd(f"ag2_in{L}", [4 * HD, S], BF16) for L in range(2)]
    ag2_out = [dd(f"ag2_out{L}", [D, S], BF16) for L in range(2)]
    ar_in = dd("ar_in", [1, 2], F32)
    ar_out = dd("ar_out", [1, 2], F32)
    t16_hbm = dd("t16_hbm", [NT * D * 128], FP16)
    qi_hbm = dd("qi_hbm", [ND * TPC * 128], BF16)
    u_hbm = dd("u_hbm", [64 * TPC * 128], FP16)
    p_hbm = dd("p_hbm", [64 * TPC * 128], BF16)
    x_hbm = dd("x_hbm", [D * TPC], F32)
    tern_hbm = dd("tern_hbm", [16 * F2, 128], BF16)
    row_hbm = dd("row_hbm", [4 * TPC], F32)

    with ExitStack() as ctx:
        E = ctx.enter_context
        X = E(nc.sbuf_tensor("X", [128, ND * TPC], F32))
        CS = E(nc.sbuf_tensor("CS", [128, 12], F32))
        VEC = E(nc.sbuf_tensor("VEC", [128, 5 * ND], F32))
        OFS = E(nc.sbuf_tensor("OFS", [1, 96], I32))
        IND = E(nc.sbuf_tensor("IND", [128, 2], F32))
        ONE1 = E(nc.sbuf_tensor("ONE1", [1, 128], F32))
        ONEF = E(nc.sbuf_tensor("ONEF", [128, 1], F32))
        ONEB = E(nc.sbuf_tensor("ONEB", [128, 2], BF16))
        TAU = E(nc.sbuf_tensor("TAU", [128, 8], F32))
        STT = E(nc.sbuf_tensor("STT", [128, 64], F32))
        ROW = E(nc.sbuf_tensor("ROW", [1, 4 * TPC], F32))
        NEGD = E(nc.sbuf_tensor("NEGD", [128, 64], F32))
        XN = None
        QT = None
        IOT0 = E(nc.sbuf_tensor("IOT0", [128, 512], F32))

        sems = {}
        for s in ("ph", "xld", "wld", "stat", "stA", "stB", "sgA", "sgB",
                  "shA", "shB", "ag", "mm", "dve", "act", "gp", "st", "tp",
                  "tpA", "tpB", "aux", "ev", "pq", "pqA", "pqB", "pr"):
            sems[s] = [E(nc.semaphore("sem_" + s)), 0]
        block = E(nc.Block())
        kb = KB(nc, block, ctx, sems)
        inc, incd, wait, on, n_ = kb.inc, kb.incd, kb.wait, kb.on, kb.n
        _uq = [0]
        def sbname(base):
            _uq[0] += 1
            return f"{base}_{_uq[0]}"

        def tap(name, src_ap):
            if name not in tap_out:
                return
            def f(sync):
                incd("st", sync.dma_start(out=tap_out[name][:, :], in_=src_ap))
                wait(sync, "st")
            on("sync", f)
            kb.barrier()

        def f(sync):
            incd("xld", sync.dma_start(out=CS[:, :], in_=cst[:, :]))
            incd("xld", sync.dma_start(out=VEC[:, :], in_=vecs[:, :]))
            incd("xld", sync.dma_start(out=OFS[:, :], in_=ofs[:, :]))
            incd("xld", sync.dma_start(out=NEGD[:, :], in_=negd[:, :]))
            incd("xld", sync.dma_start(
                out=bass.AP(X, 0, [[ND * TPC, 128], [TPC, ND], [1, TPC]]),
                in_=bass.AP(x_t.tensor, 0, [[TPC, 128], [128 * TPC, ND], [1, TPC]])))
        on("sync", f)

        def f(vector):
            wait(vector, "xld")
            vector.tensor_scalar(IND[:, 1:2], CS[:, 2:3], CS[:, 1:2], None, ALU.add)
            vector.drain()
            vector.tensor_scalar(IND[:, 0:1], IND[:, 1:2], CS[:, 1:2], None, ALU.subtract)
            inc("dve", vector.memset(ONE1[:, :], 1.0))
            inc("dve", vector.memset(ONEF[:, :], 1.0))
            inc("dve", vector.memset(ONEB[:, :], 1.0))
            inc("dve", vector.memset(STT[:, :], 0.0))
            inc("dve", vector.memset(ROW[0:1, :], 0.0))
        on("vector", f)

        def f(gpsimd):
            inc("gp", gpsimd.iota(IOT0[:, :], [[1, 512]], base=0,
                                  channel_multiplier=-1,
                                  allow_small_or_imprecise_dtypes=True))
        on("gpsimd", f)



        def stats_phase2():
            with ExitStack() as c2:
                WS = c2.enter_context(nc.sbuf_tensor(sbname("WS"), [128, 2 * 4096], F32))
                ACC = c2.enter_context(nc.sbuf_tensor(sbname("ACC"), [128, 8], F32))
                PSA = c2.enter_context(nc.psum_tensor(sbname("PSA"), [128, 8], F32))
                pieces = []
                for j in range(8):
                    pieces.append((wgvT, 3, (j // 4) * 128 * F1 + (j % 4) * 4096,
                                   [[F1, 128], [1, 4096]], 0))
                for j in range(4):
                    pieces.append((woutP, 4, j * 32 * 128 * 128,
                                   [[128, 128], [128 * 128, 32], [1, 128]], 1))

                def g(gpsimd):
                    wait(gpsimd, "xld")
                    for i, (src, col, off, dims, _w) in enumerate(pieces):
                        if i >= 2:
                            wait(gpsimd, "dve", n_("dve") - 0)
                red_base = n_("dve")
                for i, (src, col, off, dims, w_id) in enumerate(pieces):
                    def gl(gpsimd, i=i, src=src, dims=dims):
                        if i == 0:
                            wait(gpsimd, "xld")
                        if i >= 2:
                            wait(gpsimd, "dve", red_base + (i - 1))
                        with gpsimd.register() as r1:
                            gpsimd.reg_load(r1, OFS[0:1, 72 + i:73 + i])
                            incd("stA" if i % 2 == 0 else "stB", gpsimd.dma_start(
                                out=WS[:, (i % 2) * 4096:(i % 2 + 1) * 4096],
                                in_=bass.AP(src.tensor, r1, dims)))
                    on("gpsimd", gl)

                    def vr(vector, i=i, w_id=w_id):
                        wait(vector, "stA" if i % 2 == 0 else "stB",
                             16 * (i // 2 + 1))
                        if i == 0:
                            vector.memset(ACC[:, 0:2], 0.0)
                        vector.reduce_sum(ACC[:, 4 + (i % 2):5 + (i % 2)],
                                          WS[:, (i % 2) * 4096:(i % 2 + 1) * 4096],
                                          axis=AX.X, apply_absolute_value=True)
                        vector.drain()
                        inc("dve", vector.tensor_tensor(
                            ACC[:, w_id:w_id + 1], ACC[:, w_id:w_id + 1],
                            ACC[:, 4 + (i % 2):5 + (i % 2)], ALU.add))
                        vector.drain()
                    on("vector", vr)

                def vv(vector):
                    vector.drain()
                    inc("dve", vector.tensor_copy(ACC[:, 2:4], ACC[:, 0:2]))
                on("vector", vv)

                def t(tensor):
                    wait(tensor, "dve")
                    inc("mm", tensor.matmul(PSA[0:1, 0:2], ONEF[:, 0:1], ACC[:, 2:4],
                                            start=True, stop=True))
                on("tensor", t)

                def v3(vector):
                    wait(vector, "mm")
                    inc("dve", vector.tensor_copy(ACC[0:1, 0:2], PSA[0:1, 0:2]))
                on("vector", v3)

                def s(sync):
                    wait(sync, "dve")
                    incd("stat", sync.dma_start(out=ar_in[0:1, :], in_=ACC[0:1, 0:2]))
                on("sync", s)

                def g3(gpsimd):
                    wait(gpsimd, "stat")
                    inc("ag", gpsimd.collective_compute(
                        "AllReduce", ALU.add,
                        replica_groups=[[0, 1, 2, 3, 4, 5, 6, 7]],
                        ins=[ar_in[:, :]], outs=[ar_out[:, :]]))
                on("gpsimd", g3)

                def s2(sync):
                    wait(sync, "ag")
                    incd("stat", sync.dma_start(out=ACC[0:1, 0:2], in_=ar_out[0:1, :]))
                on("sync", s2)

                def v4(vector):
                    wait(vector, "stat")
                    vector.tensor_scalar(ACC[0:1, 0:1], ACC[0:1, 0:1],
                                         1.0 / (D * F1), None, ALU.mult)
                    vector.tensor_scalar(ACC[0:1, 1:2], ACC[0:1, 1:2],
                                         1.0 / (D * F2), None, ALU.mult)
                    vector.drain()
                    vector.tensor_scalar(ACC[0:1, 0:2], ACC[0:1, 0:2],
                                         CS[0:1, 3:4], None, ALU.max)
                    vector.drain()
                    inc("dve", vector.tensor_scalar(ACC[0:1, 2:4], ACC[0:1, 0:2],
                                                    0.5, None, ALU.mult))
                on("vector", v4)

                def t3(tensor):
                    wait(tensor, "dve")
                    inc("mm", tensor.matmul(PSA[:, 4:8], ONE1[0:1, :], ACC[0:1, 0:4],
                                            start=True, stop=True))
                on("tensor", t3)

                def v5(vector):
                    wait(vector, "mm")
                    vector.tensor_copy(TAU[:, 4:6], PSA[:, 4:6])
                    vector.tensor_copy(TAU[:, 0:1], PSA[:, 6:7])
                    vector.tensor_copy(TAU[:, 2:3], PSA[:, 7:8])
                    vector.drain()
                    inc("dve", vector.tensor_scalar(TAU[:, 1:2], TAU[:, 0:1],
                                                    -1.0, None, ALU.mult))
                    inc("dve", vector.tensor_scalar(TAU[:, 3:4], TAU[:, 2:3],
                                                    -1.0, None, ALU.mult))
                on("vector", v5)
            kb.barrier()
        stats_phase2()

        def emit_ln(src, src_bf16, eps_col, fold_R, h_out):
            with ExitStack() as c2:
                CB = c2.enter_context(nc.sbuf_tensor(sbname("lnCB"), [128, 2 * TPC], BF16))
                SQ = c2.enter_context(nc.sbuf_tensor(sbname("lnSQ"), [128, 2 * TPC], BF16))
                PS1 = c2.enter_context(nc.psum_tensor(sbname("lnPS1"), [1, TPC], F32))
                PS2 = c2.enter_context(nc.psum_tensor(sbname("lnPS2"), [1, TPC], F32))
                PSB = c2.enter_context(nc.psum_tensor(sbname("lnPSB"), [128, 2 * TPC], F32))
                HW16 = c2.enter_context(nc.sbuf_tensor(sbname("lnHW16"), [128, 2 * TPC], FP16))
                TMPA = c2.enter_context(nc.sbuf_tensor(sbname("lnTA"), [128, TPC], F32))
                TMPB = c2.enter_context(nc.sbuf_tensor(sbname("lnTB"), [128, TPC], F32))
                a0, d0, m0 = n_("act"), n_("dve"), n_("mm")
                tpa0, tpb0 = n_("tpA"), n_("tpB")
                for c in range(ND):
                    sl = slice(c * TPC, (c + 1) * TPC)
                    bsl = slice((c % 2) * TPC, (c % 2 + 1) * TPC)
                    if not src_bf16:
                        def vc(vector, c=c, sl=sl, bsl=bsl):
                            if c >= 2:
                                wait(vector, "mm", m0 + 4 * (c - 1))
                            inc("dve", vector.tensor_copy(CB[:, bsl], src[:, sl]))
                        on("vector", vc)
                    def sc(scalar, c=c, sl=sl, bsl=bsl):
                        if c == 0:
                            wait(scalar, "xld")
                        if c >= 2:
                            wait(scalar, "mm", m0 + 4 * (c - 1))
                        inc("act", scalar.activation(SQ[:, bsl], src[:, sl], AF.Square))
                    on("scalar", sc)
                    def tc(tensor, c=c, bsl=bsl):
                        if not src_bf16:
                            wait(tensor, "dve", d0 + (c + 1))
                        wait(tensor, "act", a0 + (c + 1))
                        for h in range(2):
                            hs = slice((c % 2) * TPC + 512 * h, (c % 2) * TPC + 512 * (h + 1))
                            os_ = slice(512 * h, 512 * (h + 1))
                            src_sl = (CB[:, hs] if not src_bf16 else
                                      src[:, c * TPC + 512 * h: c * TPC + 512 * (h + 1)])
                            inc("mm", tensor.matmul(PS1[0:1, os_], ONEB[:, 0:1], src_sl,
                                                    start=(c == 0), stop=(c == ND - 1)))
                            inc("mm", tensor.matmul(PS2[0:1, os_], ONEB[:, 0:1], SQ[:, hs],
                                                    start=(c == 0), stop=(c == ND - 1)))
                    on("tensor", tc)

                def ve(vector):
                    wait(vector, "mm")
                    vector.tensor_copy(ROW[0:1, 0:TPC], PS1[0:1, :])
                    vector.tensor_copy(ROW[0:1, TPC:2 * TPC], PS2[0:1, :])
                    vector.drain()
                    vector.tensor_scalar(ROW[0:1, 0:TPC], ROW[0:1, 0:TPC],
                                         CS[0:1, 5:6], None, ALU.mult)
                    vector.tensor_scalar(ROW[0:1, TPC:2 * TPC], ROW[0:1, TPC:2 * TPC],
                                         CS[0:1, 5:6], None, ALU.mult)
                    vector.drain()
                    vector.tensor_tensor(ROW[0:1, 2 * TPC:3 * TPC], ROW[0:1, 0:TPC],
                                         ROW[0:1, 0:TPC], ALU.mult)
                    vector.drain()
                    vector.tensor_tensor(ROW[0:1, TPC:2 * TPC], ROW[0:1, TPC:2 * TPC],
                                         ROW[0:1, 2 * TPC:3 * TPC], ALU.subtract)
                    vector.drain()
                    inc("dve", vector.tensor_scalar(
                        ROW[0:1, TPC:2 * TPC], ROW[0:1, TPC:2 * TPC],
                        CS[0:1, eps_col:eps_col + 1], None, ALU.add))
                on("vector", ve)

                def se(scalar):
                    wait(scalar, "dve")
                    scalar.activation(ROW[0:1, 2 * TPC:3 * TPC], ROW[0:1, TPC:2 * TPC],
                                      AF.Ln)
                    scalar.drain()
                    inc("act", scalar.activation(
                        ROW[0:1, TPC:2 * TPC], ROW[0:1, 2 * TPC:3 * TPC],
                        AF.Exp, scale=-0.5))
                on("scalar", se)

                def vf(vector):
                    wait(vector, "act")
                    inc("dve", vector.tensor_tensor(
                        ROW[0:1, 2 * TPC:3 * TPC], ROW[0:1, 0:TPC],
                        ROW[0:1, TPC:2 * TPC], ALU.mult))
                on("vector", vf)

                if h_out == 'h':
                    def vq(vector):
                        wait(vector, "act")
                        vector.tensor_tensor(ROW[0:1, 3 * TPC:4 * TPC],
                                             ROW[0:1, TPC:2 * TPC],
                                             ROW[0:1, TPC:2 * TPC], ALU.mult)
                        vector.drain()
                        inc("dve", vector.tensor_scalar(
                            ROW[0:1, 3 * TPC:4 * TPC], ROW[0:1, 3 * TPC:4 * TPC],
                            -1e-5, 1.0 + 1e-5, ALU.mult, ALU.add))
                    on("vector", vq)

                    def sq2(scalar):
                        wait(scalar, "dve")
                        scalar.activation(ROW[0:1, 3 * TPC:4 * TPC],
                                          ROW[0:1, 3 * TPC:4 * TPC], AF.Ln)
                        scalar.drain()
                        inc("act", scalar.activation(
                            ROW[0:1, 3 * TPC:4 * TPC], ROW[0:1, 3 * TPC:4 * TPC],
                            AF.Exp, scale=-0.5))
                    on("scalar", sq2)

                    def sq3(sync):
                        wait(sync, "act")
                        incd("tp", sync.dma_start(
                            out=bass.AP(row_hbm.tensor, 0, [[1, TPC]]),
                            in_=ROW[0:1, 3 * TPC:4 * TPC]))
                        wait(sync, "tp")
                        incd("tp", sync.dma_start(
                            out=STT[:, 32:40],
                            in_=bass.AP(row_hbm.tensor, 0, [[1, 128], [128, 8]]),
                            allow_slow_non_contiguous=True))
                        wait(sync, "tp")
                        inc("aux", sync.nop())
                    on("sync", sq3)

                def tb(tensor):
                    wait(tensor, "dve")
                    for h in range(2):
                        inc("mm", tensor.matmul(
                            PSB[:, 512 * h:512 * (h + 1)], ONE1[0:1, :],
                            ROW[0:1, TPC + 512 * h:TPC + 512 * (h + 1)],
                            start=True, stop=True))
                        inc("mm", tensor.matmul(
                            PSB[:, TPC + 512 * h:TPC + 512 * (h + 1)], ONE1[0:1, :],
                            ROW[0:1, 2 * TPC + 512 * h:2 * TPC + 512 * (h + 1)],
                            start=True, stop=True))
                on("tensor", tb)

                ms0 = n_("mm")
                for c in range(ND):
                    sl = slice(c * TPC, (c + 1) * TPC)
                    def va(vector, c=c, sl=sl):
                        if c == 0:
                            wait(vector, "mm", ms0)
                        vector.drain()
                        vector.tensor_tensor(TMPA[:, :], src[:, sl], PSB[:, 0:TPC],
                                             ALU.mult)
                        vector.drain()
                        if h_out == 'xq':
                            inc("dve", vector.tensor_tensor(TMPB[:, :], TMPA[:, :],
                                                            PSB[:, TPC:2 * TPC],
                                                            ALU.subtract))
                        elif fold_R:
                            vector.tensor_tensor(TMPB[:, :], TMPA[:, :],
                                                 PSB[:, TPC:2 * TPC], ALU.subtract)
                            vector.drain()
                            inc("dve", vector.tensor_tensor(X[:, sl], X[:, sl],
                                                            TMPB[:, :], ALU.add))
                        else:
                            inc("dve", vector.tensor_tensor(XN[:, sl], TMPA[:, :],
                                                            PSB[:, TPC:2 * TPC],
                                                            ALU.subtract))
                        if h_out == 'h':
                            vector.drain()
                            inc("dve", vector.tensor_scalar(
                                TMPA[:, :], TMPB[:, :],
                                VEC[:, 4 * ND + c:4 * ND + c + 1], None, ALU.mult))
                    on("vector", va)
                    if h_out is not None:
                        def vh(vector, c=c):
                            if c >= 2:
                                wait(vector, "tpA" if c % 2 == 0 else "tpB",
                                     tpa0 + 16 * ((c - 2) // 2 + 1) if c % 2 == 0
                                     else tpb0 + 16 * ((c - 2) // 2 + 1))
                            vector.drain()
                            inc("dve", vector.tensor_copy(
                                HW16[:, (c % 2) * TPC:(c % 2 + 1) * TPC],
                                TMPA[:, :] if h_out == 'h' else TMPB[:, :]))
                        on("vector", vh)
                        def sh(sync, c=c):
                            wait(sync, "dve")
                            incd("tpA" if c % 2 == 0 else "tpB", sync.dma_start(
                                out=bass.AP(t16_hbm.tensor, c * 128 * 128,
                                            [[128, 128], [D * 128, NT], [1, 128]]),
                                in_=HW16[:, (c % 2) * TPC:(c % 2 + 1) * TPC]))
                        on("sync", sh)
                if h_out is not None:
                    def sw(sync):
                        wait(sync, "tpA")
                        wait(sync, "tpB")
                        inc("aux", sync.nop())
                    on("sync", sw)
            kb.barrier()

        def emit_proj(wdram, mode, li, bias_col=None, dest=None, dest_off=0):
            with ExitStack() as c2:
                WPN = c2.enter_context(nc.sbuf_tensor(sbname("pjW"), [128, 2 * ND * 512], BF16))
                STG = c2.enter_context(nc.sbuf_tensor(sbname("pjS"), [128, 2 * TPC], BF16))
                PSQa = c2.enter_context(nc.psum_tensor(sbname("pjP"), [128, TPC], F32))
                PSQb = c2.enter_context(nc.psum_tensor(sbname("pjP"), [128, TPC], F32))
                PSQs = (PSQa, PSQb)
                wlsem = ("stA", "stB")
                wl0 = (n_("stA"), n_("stB"))
                sgsem = ("sgA", "sgB")
                sg0 = (n_("sgA"), n_("sgB"))
                mm0, ev0 = n_("mm"), n_("ev")

                for p in range(4):
                    def gw(gpsimd, p=p):
                        if p == 0:
                            wait(gpsimd, "ph")
                        if p >= 2:
                            wait(gpsimd, "mm", mm0 + 4 * (p - 1) *
                                 (32 if mode != 'tm' else 32))
                        incd(wlsem[p % 2], gpsimd.dma_start(
                            out=bass.AP(WPN, (p % 2) * ND * 512,
                                        [[2 * ND * 512, 128], [512, ND], [1, 512]]),
                            in_=bass.AP(wdram.tensor, 512 * p,
                                        [[D, 128], [128 * D, ND], [1, 512]])))
                    on("gpsimd", gw)

                if mode in ('fm', 'fm_sb', 'res'):
                    for p in range(4):
                        for ocr in range(4):
                            oc = 4 * p + ocr
                            def tm(tensor, p=p, ocr=ocr, oc=oc):
                                if ocr == 0:
                                    wait(tensor, wlsem[p % 2], 16 * (wl0[p % 2] // 16 + p // 2 + 1))
                                if oc >= 2:
                                    wait(tensor, "ev", ev0 + oc - 1)
                                for th in range(2):
                                    for dc in range(ND):
                                        tensor.matmul(
                                            PSQs[oc % 2][:, th * 512:(th + 1) * 512],
                                            WPN[:, (p % 2) * ND * 512 + dc * 512 +
                                                ocr * 128: (p % 2) * ND * 512 +
                                                dc * 512 + (ocr + 1) * 128],
                                            XN[:, dc * TPC + th * 512:
                                               dc * TPC + (th + 1) * 512],
                                            start=(dc == 0), stop=(dc == ND - 1))
                                tensor.drain()
                                inc("mm", tensor.nop(), 32)
                            on("tensor", tm)

                            if mode == 'fm_sb':
                                def se_(scalar, oc=oc):
                                    wait(scalar, "mm", mm0 + 32 * (oc + 1))
                                    scalar.drain()
                                    inc("ev", scalar.activation(
                                        QT[:, oc * TPC:(oc + 1) * TPC],
                                        PSQs[oc % 2][:, :],
                                        AF.Identity,
                                        bias=(VEC[:, bias_col + oc:bias_col + oc + 1]
                                              if bias_col is not None else CS[:, 7:8])))
                                on("scalar", se_)
                            elif mode == 'fm':
                                def se_(scalar, oc=oc):
                                    wait(scalar, "mm", mm0 + 32 * (oc + 1))
                                    if oc >= 2:
                                        wait(scalar, sgsem[oc % 2],
                                             sg0[oc % 2] + 16 * ((oc - 2) // 2 + 1))
                                    scalar.drain()
                                    inc("ev", scalar.activation(
                                        STG[:, (oc % 2) * TPC:(oc % 2 + 1) * TPC],
                                        PSQs[oc % 2][:, :],
                                        AF.Identity,
                                        bias=(VEC[:, bias_col + oc:bias_col + oc + 1]
                                              if bias_col is not None else CS[:, 7:8])))
                                on("scalar", se_)
                                def sd(sync, oc=oc):
                                    wait(sync, "ev", ev0 + oc + 1)
                                    incd(sgsem[oc % 2], sync.dma_start(
                                        out=bass.AP(dest.tensor, dest_off + oc * 128 * TPC,
                                                    [[TPC, 128], [1, TPC]]),
                                        in_=STG[:, (oc % 2) * TPC:(oc % 2 + 1) * TPC]))
                                on("sync", sd)
                            else:
                                def ve_(vector, oc=oc, li=li):
                                    wait(vector, "mm", mm0 + 32 * (oc + 1))
                                    vector.drain()
                                    if bias_col is not None:
                                        inc("ev", vector.scalar_tensor_tensor(
                                            X[:, oc * TPC:(oc + 1) * TPC],
                                            PSQs[oc % 2][:, :],
                                            VEC[:, bias_col + oc:bias_col + oc + 1],
                                            X[:, oc * TPC:(oc + 1) * TPC],
                                            ALU.add, ALU.add))
                                    else:
                                        inc("ev", vector.tensor_tensor(
                                            X[:, oc * TPC:(oc + 1) * TPC],
                                            X[:, oc * TPC:(oc + 1) * TPC],
                                            PSQs[oc % 2][:, :],
                                            ALU.add))
                                on("vector", ve_)
                else:
                    for p in range(4):
                        for tt in range(NT):
                            idx = p * NT + tt
                            def tm(tensor, p=p, tt=tt, idx=idx):
                                if tt == 0:
                                    wait(tensor, wlsem[p % 2], 16 * (wl0[p % 2] // 16 + p // 2 + 1))
                                if idx >= 2:
                                    wait(tensor, "ev", ev0 + idx - 1)
                                for dc in range(ND):
                                    tensor.matmul(
                                        PSQs[idx % 2][:, 0:512],
                                        XN[:, dc * TPC + tt * 128:dc * TPC + (tt + 1) * 128],
                                        WPN[:, (p % 2) * ND * 512 + dc * 512:
                                            (p % 2) * ND * 512 + (dc + 1) * 512],
                                        start=(dc == 0), stop=(dc == ND - 1))
                                tensor.drain()
                                inc("mm", tensor.nop(), 16)
                            on("tensor", tm)

                            def ve_(vector, p=p, tt=tt, idx=idx):
                                wait(vector, "mm", mm0 + 16 * (idx + 1))
                                if idx >= 2:
                                    wait(vector, sgsem[idx % 2],
                                         sg0[idx % 2] + 16 * ((idx - 2) // 2 + 1))
                                vector.drain()
                                inc("ev", vector.tensor_copy(
                                    STG[:, (idx % 2) * TPC:(idx % 2) * TPC + 512],
                                    PSQs[idx % 2][:, 0:512]))
                            on("vector", ve_)

                            def sd(sync, p=p, tt=tt, idx=idx):
                                wait(sync, "ev", ev0 + idx + 1)
                                incd(sgsem[idx % 2], sync.dma_start(
                                    out=bass.AP(dest.tensor,
                                                dest_off + tt * 128 * D + p * 512,
                                                [[D, 128], [1, 512]]),
                                    in_=STG[:, (idx % 2) * TPC:(idx % 2) * TPC + 512]))
                            on("sync", sd)
                def fin_w(sync):
                    wait(sync, "sgA")
                    wait(sync, "sgB")
                on("sync", fin_w)
            kb.barrier()

        def emit_attn(li, causal):
            def g_ag(gpsimd):
                wait(gpsimd, "ph")
                inc("ag", gpsimd.collective_compute(
                    "AllGather", ALU.bypass,
                    replica_groups=[[0, 1, 2, 3], [4, 5, 6, 7]],
                    ins=[ag1_in[li][:, :]], outs=[ag1_out[li][:, :]]))
            on("gpsimd", g_ag)

            with ExitStack() as c2:
                KH = c2.enter_context(nc.sbuf_tensor(sbname("atK"), [128, 2 * S], BF16))
                VH = c2.enter_context(nc.sbuf_tensor(sbname("atV"), [128, 2 * S], BF16))
                PRB = c2.enter_context(nc.sbuf_tensor(sbname("atP"), [128, 6 * 512], BF16))
                MKB = c2.enter_context(nc.sbuf_tensor(sbname("atM"), [128, 6 * 512], BF16))
                RB = c2.enter_context(nc.sbuf_tensor(sbname("atR"), [128, 512], F32))
                T1 = c2.enter_context(nc.sbuf_tensor(sbname("atT"), [128, 512], F32))
                ZR = c2.enter_context(nc.sbuf_tensor(sbname("atZ"), [1, 512], F32))
                ZY = c2.enter_context(nc.sbuf_tensor(sbname("atY"), [1, 512], F32))
                ZT = c2.enter_context(nc.sbuf_tensor(sbname("atZT"), [1, 512], F32))
                psSs = tuple(c2.enter_context(nc.psum_tensor(sbname("atPS"), [128, 512], F32))
                             for _ in range(4))
                psO = c2.enter_context(nc.psum_tensor(sbname("atPO"), [128, 512], F32))
                psZ = c2.enter_context(nc.psum_tensor(sbname("atPZ"), [1, 512], F32))
                psB = c2.enter_context(nc.psum_tensor(sbname("atPB"), [128, 512], F32))
                pq0 = (n_("pqA"), n_("pqB"))

                for h in range(H):
                    hs = (h % 2) * S
                    def gl(sync, h=h, hs=hs):
                        if h == 0:
                            wait(sync, "ag")
                        elif h >= 2:
                            wait(sync, "ev", n_("ev"))
                        pqs = "pqA" if h % 2 == 0 else "pqB"
                        for r in range(4):
                            incd(pqs, sync.dma_start(
                                out=KH[:, hs + r * TPC:hs + (r + 1) * TPC],
                                in_=bass.AP(ag1_out[li].tensor,
                                            r * BLK1 + h * 128 * TPC,
                                            [[TPC, 128], [1, TPC]])))
                            incd(pqs, sync.dma_start(
                                out=bass.AP(VH, hs + r * 8 * 128,
                                            [[2 * S, 128], [128, 8], [1, 128]]),
                                in_=bass.AP(ag1_out[li].tensor,
                                            r * BLK1 + V_OFF + h * 128,
                                            [[D, 128], [128 * D, 8], [1, 128]])))
                    on("sync", gl)
                    pq_t = ("pqA" if h % 2 == 0 else "pqB",
                            n_("pqA" if h % 2 == 0 else "pqB"))

                    for tq in range(2):
                        exp_done = {}
                        msk_done = {}
                        mm_qk = {}
                        pv_done = {}

                        def emit_qk(tensor, h, hs, tq, ck, pq_t):
                            if ck <= 3 and tq == 0:
                                wait(tensor, pq_t[0], pq_t[1])
                            if ck >= 4:
                                wait(tensor, "act", exp_done[ck - 4])
                            inc("mm", tensor.matmul(
                                psSs[ck % 4][:, :],
                                KH[:, hs + ck * 128:hs + (ck + 1) * 128],
                                QT[:, h * TPC + tq * 512:h * TPC + (tq + 1) * 512],
                                start=True, stop=True))
                            mm_qk[ck] = n_("mm")

                        def emit_exp(scalar, li, ck):
                            wait(scalar, "mm", mm_qk[ck])
                            if ck >= 6:
                                wait(scalar, "mm", pv_done[ck - 6])
                            scalar.drain()
                            inc("act", scalar.activation(
                                PRB[:, (ck % 6) * 512:(ck % 6 + 1) * 512],
                                psSs[ck % 4][:, :],
                                AF.Exp, scale=SM_SCALE, bias=CS[:, 8 + li:9 + li]))
                            exp_done[ck] = n_("act")

                        def emit_msk(vector, tq, ck):
                            wait(vector, "act", exp_done[ck])
                            vector.drain()
                            vector.tensor_scalar(
                                MKB[:, (ck % 6) * 512:(ck % 6 + 1) * 512],
                                IOT0[:, :],
                                NEGD[:, tq * 32 + ck:tq * 32 + ck + 1],
                                None, ALU.is_ge)
                            vector.drain()
                            inc("dve", vector.tensor_tensor(
                                PRB[:, (ck % 6) * 512:(ck % 6 + 1) * 512],
                                PRB[:, (ck % 6) * 512:(ck % 6 + 1) * 512],
                                MKB[:, (ck % 6) * 512:(ck % 6 + 1) * 512], ALU.mult))
                            msk_done[ck] = n_("dve")

                        def emit_pv(tensor, hs, ck, causal):
                            if causal:
                                wait(tensor, "dve", msk_done[ck])
                            else:
                                wait(tensor, "act", exp_done[ck])
                            inc("mm", tensor.matmul(
                                psO[:, :], VH[:, hs + ck * 128:hs + (ck + 1) * 128],
                                PRB[:, (ck % 6) * 512:(ck % 6 + 1) * 512],
                                start=(ck == 0), stop=(ck == 31)))
                            inc("mm", tensor.matmul(
                                psZ[0:1, :], ONEB[:, 0:1],
                                PRB[:, (ck % 6) * 512:(ck % 6 + 1) * 512],
                                start=(ck == 0), stop=(ck == 31)))
                            pv_done[ck] = n_("mm")

                        def t_pre(tensor, h=h, hs=hs, tq=tq, pq_t=pq_t):
                            for c0 in range(4):
                                emit_qk(tensor, h, hs, tq, c0, pq_t)
                        on("tensor", t_pre)
                        for ck in range(32):
                            def s_e(scalar, li=li, ck=ck):
                                emit_exp(scalar, li, ck)
                            on("scalar", s_e)
                            if causal:
                                def v_m(vector, tq=tq, ck=ck):
                                    emit_msk(vector, tq, ck)
                                on("vector", v_m)
                            def t_b(tensor, h=h, hs=hs, tq=tq, ck=ck,
                                    pq_t=pq_t, causal=causal):
                                emit_pv(tensor, hs, ck, causal)
                                if ck + 4 < 32:
                                    emit_qk(tensor, h, hs, tq, ck + 4, pq_t)
                            on("tensor", t_b)

                        mm_t = n_("mm")
                        def v_z(vector, mm_t=mm_t):
                            wait(vector, "mm", mm_t)
                            vector.drain()
                            vector.tensor_copy(ZR[0:1, :], psZ[0:1, :])
                            vector.drain()
                            vector.tensor_scalar(
                                ZT[0:1, :].bitcast(I32), ZR[0:1, :].bitcast(I32),
                                23, None, ALU.logical_shift_right)
                            vector.drain()
                            inc("dve", vector.tensor_copy(ZY[0:1, :],
                                                          ZT[0:1, :].bitcast(I32)))
                        on("vector", v_z)

                        def s_seed(scalar):
                            wait(scalar, "dve")
                            scalar.drain()
                            inc("act", scalar.activation(
                                ZY[0:1, :], ZY[0:1, :], AF.Exp,
                                scale=-0.6931471805599453, bias=CS[0:1, 10:11]))
                        on("scalar", s_seed)

                        def v_nt(vector):
                            wait(vector, "act")
                            for _ in range(4):
                                vector.drain()
                                vector.tensor_tensor(ZT[0:1, :], ZR[0:1, :],
                                                     ZY[0:1, :], ALU.mult)
                                vector.drain()
                                vector.tensor_scalar(ZT[0:1, :], ZT[0:1, :], -1.0,
                                                     2.0, ALU.mult, ALU.add)
                                vector.drain()
                                vector.tensor_tensor(ZY[0:1, :], ZY[0:1, :],
                                                     ZT[0:1, :], ALU.mult)
                            vector.drain()
                            inc("dve", vector.nop())
                        on("vector", v_nt)

                        def t_bc(tensor):
                            wait(tensor, "dve")
                            inc("mm", tensor.matmul(
                                psB[:, :], ONE1[0:1, :], ZY[0:1, :],
                                start=True, stop=True))
                        on("tensor", t_bc)

                        def v_ev(vector, li=li, h=h, tq=tq):
                            wait(vector, "mm")
                            vector.drain()
                            vector.tensor_copy(RB[:, :], psB[:, :])
                            vector.drain()
                            vector.tensor_tensor(T1[:, :], psO[:, :], RB[:, :], ALU.mult)
                            vector.drain()
                            if li == 0:
                                inc("ev", vector.tensor_scalar(
                                    XN[:, h * TPC + tq * 512:h * TPC + (tq + 1) * 512],
                                    T1[:, :],
                                    VEC[:, 2 * ND + h:2 * ND + h + 1], None, ALU.add))
                            else:
                                inc("ev", vector.tensor_copy(
                                    XN[:, h * TPC + tq * 512:h * TPC + (tq + 1) * 512],
                                    T1[:, :]))
                        on("vector", v_ev)

                        def t_gate(tensor):
                            wait(tensor, "ev")
                            inc("mm", tensor.nop())
                        on("tensor", t_gate)
            kb.barrier()
        def emit_hquant(use_rsq, use_eps, rowsum):
            with ExitStack() as c2:
                TMQ = c2.enter_context(nc.sbuf_tensor(sbname("qT"), [128, NT * D], FP16))
                W0 = c2.enter_context(nc.sbuf_tensor(sbname("qW"), [128, D], F32))
                R32 = c2.enter_context(nc.sbuf_tensor(sbname("qR3"), [128, D], F32))
                R64 = c2.enter_context(nc.sbuf_tensor(sbname("qR6"), [128, D], F32))
                QTI = c2.enter_context(nc.sbuf_tensor(sbname("qQ"), [128, 2 * D], BF16))
                tp0 = (n_("tpA"), n_("tpB"))
                def s_tp(sync):
                    wait(sync, "ph")
                    for j in range(NT):
                        incd("tpA" if j % 2 == 0 else "tpB", sync.dma_start_transpose(
                            TMQ[:, j * D:(j + 1) * D],
                            bass.AP(t16_hbm.tensor, j * D * 128, [[128, D], [1, 128]])))
                on("sync", s_tp)

                def v_rm(vector):
                    wait(vector, "tpA")
                    wait(vector, "tpB")
                    for j in range(NT):
                        vector.drain()
                        inc("dve", vector.reduce_max(
                            STT[:, j:j + 1], TMQ[:, j * D:(j + 1) * D],
                            axis=AX.X, apply_absolute_value=True))
                on("vector", v_rm)

                def v_sc(vector):
                    wait(vector, "dve")
                    wait(vector, "aux")
                    vector.drain()
                    if use_rsq:
                        vector.tensor_tensor(STT[:, 40:48], STT[:, 0:8],
                                             STT[:, 32:40], ALU.mult)
                    else:
                        vector.tensor_copy(STT[:, 40:48], STT[:, 0:8])
                    vector.drain()
                    if use_eps:
                        vector.tensor_scalar(STT[:, 40:48], STT[:, 40:48],
                                             CS[:, 3:4], None, ALU.max)
                        vector.drain()
                    inc("dve", vector.nop())
                on("vector", v_sc)

                def s_sc(scalar):
                    wait(scalar, "dve")
                    scalar.drain()
                    scalar.activation(STT[:, 48:56], STT[:, 40:48], AF.Ln)
                    scalar.drain()
                    inc("act", scalar.activation(STT[:, 48:56], STT[:, 48:56],
                                                 AF.Exp, scale=-1.0))
                on("scalar", s_sc)

                def v_sc2(vector):
                    wait(vector, "act")
                    vector.drain()
                    vector.tensor_scalar(STT[:, 48:56], STT[:, 48:56], 127.0,
                                         None, ALU.mult)
                    vector.tensor_scalar(STT[:, 8:16], STT[:, 40:48], 1.0 / 127.0,
                                         None, ALU.mult)
                    vector.drain()
                    if use_rsq:
                        inc("dve", vector.tensor_tensor(
                            STT[:, 16:24], STT[:, 48:56], STT[:, 32:40], ALU.mult))
                    else:
                        inc("dve", vector.tensor_copy(STT[:, 16:24], STT[:, 48:56]))
                on("vector", v_sc2)

                tq0 = (n_("shA"), n_("shB"))
                for j in range(NT):
                    def v_q(vector, j=j):
                        if j == 0:
                            wait(vector, "dve")
                        if j >= 2:
                            wait(vector, "shA" if j % 2 == 0 else "shB",
                                 tq0[j % 2] + 16 * (j // 2))
                        vector.drain()
                        vector.tensor_scalar(W0[:, :], TMQ[:, j * D:(j + 1) * D],
                                             STT[:, 16 + j:17 + j], None, ALU.mult)
                        vector.drain()
                        vector.tensor_scalar(R32[:, :], W0[:, :], CS[:, 0:1], None, ALU.add)
                        vector.drain()
                        vector.tensor_scalar(R32[:, :], R32[:, :], CS[:, 0:1], None, ALU.subtract)
                        vector.tensor_scalar(R64[:, :], W0[:, :], CS[:, 1:2], None, ALU.add)
                        vector.drain()
                        vector.tensor_scalar(R64[:, :], R64[:, :], CS[:, 1:2], None, ALU.subtract)
                        vector.drain()
                        vector.tensor_tensor(R64[:, :], R64[:, :], R32[:, :], ALU.subtract)
                        vector.drain()
                        inc("dve", vector.scalar_tensor_tensor(
                            QTI[:, (j % 2) * D:(j % 2 + 1) * D], R64[:, :],
                            IND[:, 0:1], R32[:, :], ALU.mult, ALU.add))
                        if rowsum:
                            vector.drain()
                            inc("dve", vector.reduce_sum(
                                STT[:, 24 + j:25 + j],
                                QTI[:, (j % 2) * D:(j % 2 + 1) * D], axis=AX.X))
                    on("vector", v_q)

                    def s_sc3(sync, j=j):
                        wait(sync, "dve")
                        incd("shA" if j % 2 == 0 else "shB", sync.dma_start(
                            out=bass.AP(qi_hbm.tensor, j * 128 * 128,
                                        [[128, 128], [TPC * 128, ND], [1, 128]]),
                            in_=bass.AP(QTI, (j % 2) * D,
                                        [[2 * D, 128], [128, ND], [1, 128]])))
                    on("sync", s_sc3)

                def s_fin(sync):
                    wait(sync, "shA")
                    wait(sync, "shB")
                    for dc in range(ND):
                        incd("tp", sync.dma_start_transpose(
                            XN[:, dc * TPC:(dc + 1) * TPC],
                            bass.AP(qi_hbm.tensor, dc * TPC * 128, [[128, TPC], [1, 128]])))
                    wait(sync, "tp")
                    inc("aux", sync.nop())
                on("sync", s_fin)
            kb.barrier()

        def emit_gv():
            PW = 256
            with ExitStack() as c2:
                WF = c2.enter_context(nc.sbuf_tensor(sbname("gW"), [128, 2 * ND * PW], F32))
                WG = c2.enter_context(nc.sbuf_tensor(sbname("gG"), [128, 2 * ND * PW], F32))
                TV = c2.enter_context(nc.sbuf_tensor(sbname("gTV"), [128, ND * PW], BF16))
                TG = c2.enter_context(nc.sbuf_tensor(sbname("gTG"), [128, ND * PW], BF16))
                SA = c2.enter_context(nc.sbuf_tensor(sbname("gSA"), [128, 1024], F32))
                SB_ = c2.enter_context(nc.sbuf_tensor(sbname("gSB"), [128, 1024], F32))
                GS = c2.enter_context(nc.sbuf_tensor(sbname("gGS"), [128, 2 * PW], F32))
                UO = c2.enter_context(nc.sbuf_tensor(sbname("gUO"), [128, 2 * PW], FP16))
                SC2 = c2.enter_context(nc.sbuf_tensor(sbname("gS2"), [128, 8], F32))
                psVs = tuple(c2.enter_context(nc.psum_tensor(sbname("gPV"), [128, PW], F32))
                             for _ in range(2))
                psGs = tuple(c2.enter_context(nc.psum_tensor(sbname("gPG"), [128, PW], F32))
                             for _ in range(2))
                wl0 = (n_("shA"), n_("shB"))
                u0 = (n_("sgA"), n_("sgB"))
                evb = n_("ev")

                def v_pre(vector):
                    wait(vector, "ph")
                    vector.drain()
                    inc("dve", vector.tensor_scalar(SC2[:, :], STT[:, 8:16],
                                                    TAU[:, 4:5], None, ALU.mult))
                on("vector", v_pre)

                for p in range(32):
                    def s_w(sync, p=p):
                        if p >= 2:
                            wait(sync, "mm", n_("mm"))
                        incd("shA" if p % 2 == 0 else "shB", sync.dma_start(
                            out=bass.AP(WF, (p % 2) * ND * PW,
                                        [[2 * ND * PW, 128], [PW, ND], [1, PW]]),
                            in_=bass.AP(wgvT.tensor, PW * p,
                                        [[F1, 128], [128 * F1, ND], [1, PW]])))
                        incd("shA" if p % 2 == 0 else "shB", sync.dma_start(
                            out=bass.AP(WG, (p % 2) * ND * PW,
                                        [[2 * ND * PW, 128], [PW, ND], [1, PW]]),
                            in_=bass.AP(wgvT.tensor, F2 + PW * p,
                                        [[F1, 128], [128 * F1, ND], [1, PW]])))
                    on("sync", s_w)

                    def v_t(vector, p=p):
                        wait(vector, "shA" if p % 2 == 0 else "shB",
                             wl0[p % 2] + 32 * (p // 2 + 1))
                        if p >= 1:
                            wait(vector, "mm", n_("mm"))
                        for half2 in range(4):
                            for (src, dst, base) in ((WF, TV, (p % 2) * ND * PW),
                                                     (WG, TG, (p % 2) * ND * PW)):
                                ssl = slice(base + half2 * 1024, base + (half2 + 1) * 1024)
                                dsl = slice(half2 * 1024, (half2 + 1) * 1024)
                                vector.drain()
                                vector.tensor_scalar(SA[:, :], src[:, ssl],
                                                     TAU[:, 0:1], None, ALU.is_ge)
                                vector.tensor_scalar(SB_[:, :], src[:, ssl],
                                                     TAU[:, 1:2], None, ALU.is_le)
                                vector.drain()
                                vector.tensor_tensor(dst[:, dsl], SA[:, :], SB_[:, :],
                                                     ALU.subtract)
                        vector.drain()
                        inc("dve", vector.nop())
                    on("vector", v_t)
                    tern_t = n_("dve")

                    for tt in range(NT):
                        idx = p * NT + tt
                        def t_m(tensor, p=p, tt=tt, idx=idx, tern_t=tern_t):
                            if tt == 0:
                                wait(tensor, "dve", tern_t)
                            if idx >= 2:
                                wait(tensor, "ev", evb + idx - 1)
                            for dc in range(ND):
                                tensor.matmul(
                                    psVs[tt % 2][:, :],
                                    XN[:, dc * TPC + tt * 128:dc * TPC + (tt + 1) * 128],
                                    TV[:, dc * PW:(dc + 1) * PW],
                                    start=(dc == 0), stop=(dc == ND - 1))
                                tensor.matmul(
                                    psGs[tt % 2][:, :],
                                    XN[:, dc * TPC + tt * 128:dc * TPC + (tt + 1) * 128],
                                    TG[:, dc * PW:(dc + 1) * PW],
                                    start=(dc == 0), stop=(dc == ND - 1))
                            tensor.drain()
                            inc("mm", tensor.nop(), 32)
                        on("tensor", t_m)
                        mmv = n_("mm")

                        def s_g(scalar, tt=tt, idx=idx, mmv=mmv):
                            wait(scalar, "mm", mmv)
                            if idx >= 2:
                                wait(scalar, "ev", evb + idx - 1)
                            scalar.drain()
                            inc("act", scalar.activation(
                                GS[:, (idx % 2) * PW:(idx % 2 + 1) * PW],
                                psGs[tt % 2][:, :],
                                AF.Silu, scale=SC2[:, tt:tt + 1]))
                        on("scalar", s_g)
                        actv = n_("act")

                        def v_u(vector, p=p, tt=tt, idx=idx, actv=actv):
                            wait(vector, "act", actv)
                            if idx >= 2:
                                wait(vector, "sgA" if idx % 2 == 0 else "sgB",
                                     u0[idx % 2] + 16 * ((idx - 2) // 2 + 1))
                            vector.drain()
                            vector.scalar_tensor_tensor(
                                UO[:, (idx % 2) * PW:(idx % 2 + 1) * PW],
                                psVs[tt % 2][:, :], SC2[:, tt:tt + 1],
                                GS[:, (idx % 2) * PW:(idx % 2 + 1) * PW],
                                ALU.mult, ALU.mult)
                            vector.drain()
                            vector.reduce_max(STT[:, 48 + tt:49 + tt],
                                              UO[:, (idx % 2) * PW:(idx % 2 + 1) * PW],
                                              axis=AX.X, apply_absolute_value=True)
                            vector.drain()
                            if p == 0:
                                inc("dve", vector.tensor_copy(STT[:, 0 + tt:1 + tt],
                                                              STT[:, 48 + tt:49 + tt]))
                            else:
                                inc("dve", vector.tensor_tensor(
                                    STT[:, 0 + tt:1 + tt], STT[:, 0 + tt:1 + tt],
                                    STT[:, 48 + tt:49 + tt], ALU.max))
                            inc("ev", vector.nop())
                        on("vector", v_u)

                        def s_u(sync, p=p, tt=tt, idx=idx):
                            wait(sync, "ev")
                            incd("sgA" if idx % 2 == 0 else "sgB", sync.dma_start(
                                out=bass.AP(u_hbm.tensor,
                                            2 * p * TPC * 128 + tt * 128 * 128,
                                            [[128, 128], [TPC * 128, 2], [1, 128]]),
                                in_=bass.AP(UO, (idx % 2) * PW,
                                            [[2 * PW, 128], [128, 2], [1, 128]])))
                        on("sync", s_u)

                def s_wd(sync):
                    wait(sync, "sgA")
                    wait(sync, "sgB")
                    inc("aux", sync.nop())
                on("sync", s_wd)
            kb.barrier()

        def emit_ustats(descale_tau_col, out_negate2=False):
            def v_s(vector):
                wait(vector, "ph")
                vector.drain()
                vector.tensor_scalar(STT[:, 40:48], STT[:, 0:8], CS[:, 3:4],
                                     None, ALU.max)
                vector.drain()
                inc("dve", vector.nop())
            on("vector", v_s)

            def s_s(scalar):
                wait(scalar, "dve")
                scalar.drain()
                scalar.activation(STT[:, 48:56], STT[:, 40:48], AF.Ln)
                scalar.drain()
                inc("act", scalar.activation(STT[:, 48:56], STT[:, 48:56],
                                             AF.Exp, scale=-1.0))
            on("scalar", s_s)

            def v_s2(vector):
                wait(vector, "act")
                vector.drain()
                vector.tensor_scalar(STT[:, 48:56], STT[:, 48:56], 127.0, None, ALU.mult)
                vector.drain()
                vector.tensor_copy(STT[:, 16:24], STT[:, 48:56])
                vector.tensor_scalar(STT[:, 56:64], STT[:, 40:48], 1.0 / 127.0,
                                     None, ALU.mult)
                vector.drain()
                inc("dve", vector.tensor_scalar(STT[:, 56:64], STT[:, 56:64],
                                                TAU[:, descale_tau_col:descale_tau_col + 1],
                                                None, ALU.mult))
            on("vector", v_s2)

            def s_r(sync):
                wait(sync, "dve")
                incd("tp", sync.dma_start(
                    out=bass.AP(row_hbm.tensor, 0, [[1, 128], [128, 8]]),
                    in_=STT[:, 16:24], allow_slow_non_contiguous=True))
                incd("tp", sync.dma_start(
                    out=bass.AP(row_hbm.tensor, TPC, [[1, 128], [128, 8]]),
                    in_=STT[:, 56:64], allow_slow_non_contiguous=True))
                wait(sync, "tp")
                incd("tp", sync.dma_start(
                    out=ROW[0:1, 0:TPC],
                    in_=bass.AP(row_hbm.tensor, 0, [[1, TPC]])))
                incd("tp", sync.dma_start(
                    out=ROW[0:1, TPC:2 * TPC],
                    in_=bass.AP(row_hbm.tensor, TPC, [[1, TPC]])))
                wait(sync, "tp")
                inc("aux", sync.nop())
            on("sync", s_r)
            kb.barrier()
        def emit_mmK(li, w_panels, src_hbm, src_fp16, quantize, tern3):
            with ExitStack() as c2:
                BIG = c2.enter_context(nc.sbuf_tensor(sbname("mB"), [128, 64 * 512], BF16))
                CUB = c2.enter_context(nc.sbuf_tensor(sbname("mC"), [128, TPC], F32))
                CUB2 = c2.enter_context(nc.sbuf_tensor(sbname("mC2"), [128, 1024], F32))
                DDB = c2.enter_context(nc.sbuf_tensor(sbname("mD"), [128, TPC], F32))
                UT = c2.enter_context(nc.sbuf_tensor(sbname("mU"), [128, 2 * 1024],
                                                     FP16 if src_fp16 else BF16))
                R32 = c2.enter_context(nc.sbuf_tensor(sbname("mR3"), [128, 1024], F32))
                R64 = c2.enter_context(nc.sbuf_tensor(sbname("mR6"), [128, 1024], F32))
                WQF = c2.enter_context(nc.sbuf_tensor(sbname("mWQ"), [128, 2048], F32))
                SA = c2.enter_context(nc.sbuf_tensor(sbname("mSA"), [128, 512], F32))
                SB_ = c2.enter_context(nc.sbuf_tensor(sbname("mSB"), [128, 512], F32))
                TD = c2.enter_context(nc.sbuf_tensor(sbname("mTD"), [128, 64 * 128], BF16))
                XR = c2.enter_context(nc.sbuf_tensor(sbname("mXR"), [128, 2 * 512], F32))
                T2 = c2.enter_context(nc.sbuf_tensor(sbname("mT2"), [128, 512], F32))
                psO2s = tuple(c2.enter_context(nc.psum_tensor(sbname("mPO"), [128, 512], F32))
                              for _ in range(2))
                psBC = c2.enter_context(nc.psum_tensor(sbname("mPB"), [128, 2 * TPC], F32))

                def s_sp(sync):
                    wait(sync, "ph")
                    incd("tp", sync.dma_start(
                        out=bass.AP(x_hbm.tensor, 0, [[ND * TPC, 128], [1, ND * TPC]]),
                        in_=X[:, :]))
                    wait(sync, "tp")
                    inc("aux", sync.nop())
                on("sync", s_sp)

                def t_bc(tensor):
                    wait(tensor, "ph")
                    for h2 in range(2):
                        inc("mm", tensor.matmul(
                            psBC[:, h2 * 512:(h2 + 1) * 512], ONE1[0:1, :],
                            ROW[0:1, h2 * 512:(h2 + 1) * 512], start=True, stop=True))
                        inc("mm", tensor.matmul(
                            psBC[:, TPC + h2 * 512:TPC + (h2 + 1) * 512], ONE1[0:1, :],
                            ROW[0:1, TPC + h2 * 512:TPC + (h2 + 1) * 512],
                            start=True, stop=True))
                on("tensor", t_bc)

                def v_bc(vector):
                    wait(vector, "mm")
                    vector.drain()
                    vector.tensor_copy(CUB[:, :], psBC[:, 0:TPC])
                    inc("dve", vector.tensor_copy(DDB[:, :], psBC[:, TPC:2 * TPC]))
                on("vector", v_bc)

                for half in range(2):
                    if quantize:
                        def v_c2(vector, half=half):
                            wait(vector, "dve")
                            vector.drain()
                            vector.tensor_copy(CUB2[:, 0:512],
                                               CUB[:, half * 512:(half + 1) * 512])
                            vector.drain()
                            inc("dve", vector.tensor_copy(
                                CUB2[:, 512:1024],
                                CUB[:, half * 512:(half + 1) * 512]))
                        on("vector", v_c2)
                    tp0 = (n_("tpA"), n_("tpB"))
                    for kp in range(32):
                        def s_tp(sync, kp=kp, half=half):
                            if half == 1 and kp == 0:
                                wait(sync, "mm", n_("mm"))
                            if kp >= 2:
                                wait(sync, "dve", n_("dve"))
                            for sub in range(2):
                                kc = 2 * kp + sub
                                incd("tpA" if kp % 2 == 0 else "tpB",
                                     sync.dma_start_transpose(
                                         (UT[:, (kp % 2) * 1024 + sub * 512:
                                             (kp % 2) * 1024 + (sub + 1) * 512]
                                          if quantize else
                                          BIG[:, kc * 512:(kc + 1) * 512]),
                                         bass.AP(src_hbm.tensor,
                                                 kc * TPC * 128 + half * 512 * 128,
                                                 [[128, 512], [1, 128]])))
                        on("sync", s_tp)

                        if quantize:
                            def v_q(vector, kp=kp, half=half, tp0=tp0):
                                wait(vector, "tpA" if kp % 2 == 0 else "tpB",
                                     tp0[kp % 2] + 32 * (kp // 2 + 1))
                                if kp == 0:
                                    wait(vector, "dve")
                                usl = slice((kp % 2) * 1024, (kp % 2 + 1) * 1024)
                                vector.drain()
                                vector.tensor_tensor(R64[:, :], UT[:, usl],
                                                     CUB2[:, :], ALU.mult)
                                vector.drain()
                                vector.tensor_scalar(R32[:, :], R64[:, :], CS[:, 0:1],
                                                     None, ALU.add)
                                vector.drain()
                                vector.tensor_scalar(R32[:, :], R32[:, :], CS[:, 0:1],
                                                     None, ALU.subtract)
                                vector.tensor_scalar(R64[:, :], R64[:, :], CS[:, 1:2],
                                                     None, ALU.add)
                                vector.drain()
                                vector.tensor_scalar(R64[:, :], R64[:, :], CS[:, 1:2],
                                                     None, ALU.subtract)
                                vector.drain()
                                vector.tensor_tensor(R64[:, :], R64[:, :], R32[:, :],
                                                     ALU.subtract)
                                vector.drain()
                                inc("dve", vector.scalar_tensor_tensor(
                                    bass.AP(BIG, 2 * kp * 512,
                                            [[64 * 512, 128], [1, 1024]]),
                                    R64[:, :], IND[:, 0:1], R32[:, :],
                                    ALU.mult, ALU.add))
                            on("vector", v_q)
                    big_t = (n_("dve"), n_("tpA"), n_("tpB"))

                    wq0 = (n_("stA"), n_("stB"))
                    for o2c in range(16):
                        gidx = half * 16 + o2c
                        if tern3 and half == 0:
                            for q in range(4):
                                def s_w(sync, o2c=o2c, q=q):
                                    if o2c + q > 0:
                                        wait(sync, "dve", n_("dve"))
                                    incd("shA", sync.dma_start(
                                        out=bass.AP(WQF, 0,
                                                    [[2048, 128], [128, 16], [1, 128]]),
                                        in_=bass.AP(w_panels.tensor,
                                                    o2c * F2 * 128 + q * 2048 * 128,
                                                    [[128, 128], [128 * 128, 16], [1, 128]])))
                                on("sync", s_w)
                                def v_t(vector, o2c=o2c, q=q):
                                    wait(vector, "shA", n_("shA"))
                                    if q == 0:
                                        wait(vector, "mm", n_("mm"))
                                        wait(vector, "tp", n_("tp"))
                                    for sl4 in range(4):
                                        ssl = slice(sl4 * 512, (sl4 + 1) * 512)
                                        vector.drain()
                                        vector.tensor_scalar(SA[:, :], WQF[:, ssl],
                                                             TAU[:, 2:3], None, ALU.is_ge)
                                        vector.tensor_scalar(SB_[:, :], WQF[:, ssl],
                                                             TAU[:, 3:4], None, ALU.is_le)
                                        vector.drain()
                                        vector.tensor_tensor(
                                            TD[:, q * 2048 + sl4 * 512:
                                               q * 2048 + (sl4 + 1) * 512],
                                            SA[:, :], SB_[:, :], ALU.subtract)
                                    vector.drain()
                                    inc("dve", vector.nop())
                                on("vector", v_t)
                            def s_tc(sync, o2c=o2c):
                                wait(sync, "dve")
                                incd("tp", sync.dma_start(
                                    out=bass.AP(tern_hbm.tensor, o2c * F2 * 128,
                                                [[128, 128], [128 * 128, 64], [1, 128]]),
                                    in_=bass.AP(TD, 0, [[64 * 128, 128], [128, 64], [1, 128]])))
                            on("sync", s_tc)
                            td_t = ("dve", n_("dve"))
                        elif tern3:
                            def s_w(sync, o2c=o2c):
                                wait(sync, "tp", n_("tp"))
                                if o2c > 0:
                                    wait(sync, "mm", n_("mm"))
                                incd("tp", sync.dma_start(
                                    out=bass.AP(TD, 0, [[64 * 128, 128], [128, 64], [1, 128]]),
                                    in_=bass.AP(tern_hbm.tensor, o2c * F2 * 128,
                                                [[128, 128], [128 * 128, 64], [1, 128]])))
                            on("sync", s_w)
                            td_t = ("tp", n_("tp"))
                        else:
                            for q in range(4):
                                def s_w(sync, o2c=o2c, q=q):
                                    if o2c + q > 0:
                                        wait(sync, "dve", n_("dve"))
                                    incd("shA", sync.dma_start(
                                        out=bass.AP(WQF, 0,
                                                    [[2048, 128], [128, 16], [1, 128]]),
                                        in_=bass.AP(w_panels.tensor,
                                                    o2c * F2 * 128 + q * 2048 * 128,
                                                    [[128, 128], [128 * 128, 16], [1, 128]])))
                                on("sync", s_w)
                                def v_t(vector, o2c=o2c, q=q):
                                    wait(vector, "shA", n_("shA"))
                                    if q == 0:
                                        wait(vector, "mm", n_("mm"))
                                    vector.drain()
                                    inc("dve", vector.tensor_scalar(
                                        TD[:, q * 2048:(q + 1) * 2048],
                                        WQF[:, 0:2048], 0.0, None, ALU.is_ge))
                                on("vector", v_t)
                            td_t = ("dve", n_("dve"))

                        def t_m(tensor, half=half, o2c=o2c, gidx=gidx,
                                td_t=td_t, big_t=big_t):
                            wait(tensor, td_t[0], td_t[1])
                            if quantize:
                                wait(tensor, "dve", big_t[0])
                            wait(tensor, "tpA", big_t[1])
                            wait(tensor, "tpB", big_t[2])
                            if gidx >= 2:
                                wait(tensor, "ev", n_("ev"))
                            for kc in range(64):
                                tensor.matmul(
                                    psO2s[gidx % 2][:, :],
                                    TD[:, kc * 128:(kc + 1) * 128],
                                    BIG[:, kc * 512:(kc + 1) * 512],
                                    start=(kc == 0), stop=(kc == 63))
                            tensor.drain()
                            inc("mm", tensor.nop(), 64)
                        on("tensor", t_m)
                        mmv = n_("mm")

                        def s_x(sync, half=half, o2c=o2c, gidx=gidx):
                            wait(sync, "aux")
                            if gidx >= 2:
                                wait(sync, "ev", n_("ev"))
                            incd("sgA" if gidx % 2 == 0 else "sgB", sync.dma_start(
                                out=XR[:, (gidx % 2) * 512:(gidx % 2 + 1) * 512],
                                in_=bass.AP(x_hbm.tensor, o2c * TPC + half * 512,
                                            [[ND * TPC, 128], [1, 512]])))
                        on("sync", s_x)
                        sg_t = (n_("sgA"), n_("sgB"))

                        def v_e(vector, half=half, o2c=o2c, gidx=gidx, mmv=mmv, sg_t=sg_t):
                            wait(vector, "mm", mmv)
                            wait(vector, "sgA" if gidx % 2 == 0 else "sgB",
                                 sg_t[gidx % 2])
                            vector.drain()
                            if tern3:
                                vector.tensor_tensor(
                                    T2[:, :], psO2s[gidx % 2][:, :],
                                    DDB[:, half * 512:(half + 1) * 512], ALU.mult)
                            else:
                                vector.scalar_tensor_tensor(
                                    T2[:, :], psO2s[gidx % 2][:, :],
                                    2.0, DDB[:, half * 512:(half + 1) * 512],
                                    ALU.mult, ALU.subtract)
                            vector.drain()
                            inc("ev", vector.tensor_tensor(
                                X[:, o2c * TPC + half * 512:o2c * TPC + (half + 1) * 512],
                                XR[:, (gidx % 2) * 512:(gidx % 2 + 1) * 512],
                                T2[:, :], ALU.add))
                        on("vector", v_e)
            kb.barrier()

        def emit_gu():
            PW = 256
            with ExitStack() as c2:
                WF = c2.enter_context(nc.sbuf_tensor(sbname("hWF"), [128, 2 * ND * PW], F32))
                WG = c2.enter_context(nc.sbuf_tensor(sbname("hWG"), [128, 2 * ND * PW], F32))
                TV = c2.enter_context(nc.sbuf_tensor(sbname("hTV"), [128, ND * PW], BF16))
                TG = c2.enter_context(nc.sbuf_tensor(sbname("hTG"), [128, ND * PW], BF16))
                GS = c2.enter_context(nc.sbuf_tensor(sbname("hGS"), [128, 2 * PW], F32))
                US = c2.enter_context(nc.sbuf_tensor(sbname("hUS"), [128, PW], F32))
                PO = c2.enter_context(nc.sbuf_tensor(sbname("hPO"), [128, 2 * PW], BF16))
                SG2 = c2.enter_context(nc.sbuf_tensor(sbname("hS2"), [128, 8], F32))
                BG2 = c2.enter_context(nc.sbuf_tensor(sbname("hB2"), [128, 8], F32))
                psGs = tuple(c2.enter_context(nc.psum_tensor(sbname("hPG"), [128, PW], F32))
                             for _ in range(2))
                psUs = tuple(c2.enter_context(nc.psum_tensor(sbname("hPU"), [128, PW], F32))
                             for _ in range(2))
                wl0 = (n_("shA"), n_("shB"))
                u0 = (n_("sgA"), n_("sgB"))
                evb = n_("ev")

                def v_pre(vector):
                    wait(vector, "ph")
                    vector.drain()
                    vector.tensor_scalar(SG2[:, :], STT[:, 8:16], 2.0, None, ALU.mult)
                    vector.tensor_tensor(BG2[:, :], STT[:, 24:32], STT[:, 8:16], ALU.mult)
                    vector.drain()
                    vector.tensor_scalar(BG2[:, :], BG2[:, :], -1.0, None, ALU.mult)
                    vector.memset(STT[:, 56:64], 0.0)
                    vector.drain()
                    inc("dve", vector.nop())
                on("vector", v_pre)

                for p in range(32):
                    def s_w(sync, p=p):
                        if p >= 2:
                            wait(sync, "mm", n_("mm"))
                        incd("shA" if p % 2 == 0 else "shB", sync.dma_start(
                            out=bass.AP(WF, (p % 2) * ND * PW,
                                        [[2 * ND * PW, 128], [PW, ND], [1, PW]]),
                            in_=bass.AP(wgT.tensor, PW * p,
                                        [[F2, 128], [128 * F2, ND], [1, PW]])))
                        incd("shA" if p % 2 == 0 else "shB", sync.dma_start(
                            out=bass.AP(WG, (p % 2) * ND * PW,
                                        [[2 * ND * PW, 128], [PW, ND], [1, PW]]),
                            in_=bass.AP(wuT.tensor, PW * p,
                                        [[F2, 128], [128 * F2, ND], [1, PW]])))
                    on("sync", s_w)

                    def v_t(vector, p=p):
                        wait(vector, "shA" if p % 2 == 0 else "shB",
                             wl0[p % 2] + 32 * (p // 2 + 1))
                        if p >= 1:
                            wait(vector, "mm", n_("mm"))
                        base = (p % 2) * ND * PW
                        vector.drain()
                        vector.tensor_scalar(TV[:, :], WF[:, base:base + ND * PW],
                                             0.0, None, ALU.is_ge)
                        vector.tensor_scalar(TG[:, :], WG[:, base:base + ND * PW],
                                             0.0, None, ALU.is_ge)
                        vector.drain()
                        inc("dve", vector.nop())
                    on("vector", v_t)
                    tern_t = n_("dve")

                    for tt in range(NT):
                        idx = p * NT + tt
                        def t_m(tensor, p=p, tt=tt, idx=idx, tern_t=tern_t):
                            if tt == 0:
                                wait(tensor, "dve", tern_t)
                            if idx >= 2:
                                wait(tensor, "ev", evb + idx - 1)
                            for dc in range(ND):
                                tensor.matmul(
                                    psGs[tt % 2][:, :],
                                    XN[:, dc * TPC + tt * 128:dc * TPC + (tt + 1) * 128],
                                    TV[:, dc * PW:(dc + 1) * PW],
                                    start=(dc == 0), stop=(dc == ND - 1))
                                tensor.matmul(
                                    psUs[tt % 2][:, :],
                                    XN[:, dc * TPC + tt * 128:dc * TPC + (tt + 1) * 128],
                                    TG[:, dc * PW:(dc + 1) * PW],
                                    start=(dc == 0), stop=(dc == ND - 1))
                            tensor.drain()
                            inc("mm", tensor.nop(), 32)
                        on("tensor", t_m)
                        mmv = n_("mm")

                        def s_g(scalar, tt=tt, idx=idx, mmv=mmv):
                            wait(scalar, "mm", mmv)
                            if idx >= 2:
                                wait(scalar, "ev", evb + idx - 1)
                            scalar.drain()
                            inc("act", scalar.activation(
                                GS[:, (idx % 2) * PW:(idx % 2 + 1) * PW],
                                psGs[tt % 2][:, :],
                                AF.Silu, scale=SG2[:, tt:tt + 1], bias=BG2[:, tt:tt + 1]))
                        on("scalar", s_g)
                        actv = n_("act")

                        def v_u(vector, p=p, tt=tt, idx=idx, actv=actv):
                            wait(vector, "act", actv)
                            if idx >= 2:
                                wait(vector, "sgA" if idx % 2 == 0 else "sgB",
                                     u0[idx % 2] + 16 * ((idx - 2) // 2 + 1))
                            vector.drain()
                            vector.tensor_scalar(
                                US[:, :], psUs[tt % 2][:, :],
                                SG2[:, tt:tt + 1], BG2[:, tt:tt + 1], ALU.mult, ALU.add)
                            vector.drain()
                            vector.tensor_tensor(
                                PO[:, (idx % 2) * PW:(idx % 2 + 1) * PW],
                                GS[:, (idx % 2) * PW:(idx % 2 + 1) * PW],
                                US[:, :], ALU.mult)
                            vector.drain()
                            vector.reduce_sum(STT[:, 48 + tt:49 + tt],
                                              PO[:, (idx % 2) * PW:(idx % 2 + 1) * PW],
                                              axis=AX.X)
                            vector.drain()
                            inc("dve", vector.tensor_tensor(
                                STT[:, 56 + tt:57 + tt], STT[:, 56 + tt:57 + tt],
                                STT[:, 48 + tt:49 + tt], ALU.add))
                            inc("ev", vector.nop())
                        on("vector", v_u)

                        def s_u(sync, p=p, tt=tt, idx=idx):
                            wait(sync, "ev")
                            incd("sgA" if idx % 2 == 0 else "sgB", sync.dma_start(
                                out=bass.AP(p_hbm.tensor,
                                            2 * p * TPC * 128 + tt * 128 * 128,
                                            [[128, 128], [TPC * 128, 2], [1, 128]]),
                                in_=bass.AP(PO, (idx % 2) * PW,
                                            [[2 * PW, 128], [128, 2], [1, 128]])))
                        on("sync", s_u)

                def s_wd(sync):
                    wait(sync, "sgA")
                    wait(sync, "sgB")
                    inc("aux", sync.nop())
                on("sync", s_wd)

                def s_r(sync):
                    wait(sync, "dve")
                    incd("tp", sync.dma_start(
                        out=bass.AP(row_hbm.tensor, TPC, [[1, 128], [128, 8]]),
                        in_=STT[:, 56:64], allow_slow_non_contiguous=True))
                    wait(sync, "tp")
                    incd("tp", sync.dma_start(
                        out=ROW[0:1, TPC:2 * TPC],
                        in_=bass.AP(row_hbm.tensor, TPC, [[1, TPC]])))
                    wait(sync, "tp")
                    inc("aux", sync.nop())
                on("sync", s_r)
            kb.barrier()

        with ExitStack() as cL0:
            XN = cL0.enter_context(nc.sbuf_tensor(sbname("XNa"), [128, ND * TPC], BF16))
            emit_ln(X, False, 3, fold_R=False, h_out=None)
            tap("xn1", XN[:, :])
            if STAGE >= 2:
                with ExitStack() as cQ0:
                    QT = cQ0.enter_context(
                        nc.sbuf_tensor(sbname("QTa"), [128, ND * TPC], BF16))
                    emit_proj(WQ[0], 'fm_sb', 0, bias_col=0)
                    emit_proj(WK[0], 'fm', 0, bias_col=ND, dest=ag1_in[0], dest_off=0)
                    emit_proj(WV[0], 'tm', 0, dest=ag1_in[0], dest_off=V_OFF)
                    if STAGE >= 3:
                        emit_attn(0, causal=False)
                        tap("attnT0", XN[:, :])
            if STAGE >= 4:
                emit_proj(WO[0], 'res', 0, bias_col=3 * ND)
                tap("x1", X[:, :])
            if STAGE >= 5:
                emit_ln(X, False, 3, fold_R=True, h_out='h')
                emit_hquant(use_rsq=True, use_eps=True, rowsum=False)
                tap("qi0", XN[:, :])
                tap("stt", STT[:, :])
            if STAGE >= 6:
                emit_gv()
                emit_ustats(descale_tau_col=5)
        if STAGE >= 7:
            emit_mmK(0, woutP, u_hbm, src_fp16=True, quantize=True, tern3=True)
            tap("x2", X[:, :])
        if STAGE >= 8:
            with ExitStack() as cL1:
                XN = cL1.enter_context(nc.sbuf_tensor(sbname("XNb"), [128, ND * TPC], BF16))
                emit_ln(X, False, 3, fold_R=False, h_out=None)
                with ExitStack() as cQ1:
                    QT = cQ1.enter_context(
                        nc.sbuf_tensor(sbname("QTb"), [128, ND * TPC], BF16))
                    emit_proj(WQ[1], 'fm_sb', 1, bias_col=None)
                    tap("qt1", QT[:, :])
                    emit_proj(WK[1], 'fm', 1, bias_col=None, dest=ag1_in[1], dest_off=0)
                    emit_proj(WV[1], 'tm', 1, dest=ag1_in[1], dest_off=V_OFF)
                    emit_attn(1, causal=True)
                emit_proj(WO[1], 'res', 1, bias_col=None)
                tap("x3", X[:, :])
                if STAGE >= 9:
                    emit_ln(X, False, 3, fold_R=False, h_out=None)
                    emit_ln(XN, True, 4, fold_R=False, h_out='xq')
                    emit_hquant(use_rsq=False, use_eps=False, rowsum=True)
                    emit_gu()
        if STAGE >= 9:
            emit_mmK(1, wdP, p_hbm, src_fp16=False, quantize=False, tern3=False)

        kb.barrier()

        def fin(sync):
            incd("st", sync.dma_start(
                out=bass.AP(y_t.tensor, 0, [[TPC, 128], [128 * TPC, ND], [1, TPC]]),
                in_=bass.AP(X, 0, [[ND * TPC, 128], [TPC, ND], [1, TPC]])))
            wait(sync, "st")
        on("sync", fin)
    return nc


def shard_inputs(inputs):
    x = np.asarray(inputs["L0_attn_hidden_state"], np.float32)
    T = lambda a: np.ascontiguousarray(np.asarray(a, np.float32).T)
    C = lambda a: np.ascontiguousarray(np.asarray(a, np.float32))
    wp = np.asarray(inputs["L1_attn_w_project"], np.float32)
    wout_T = T(inputs["L0_ffn_w_out"])
    wd_T = T(inputs["L1_ffn_W_d"])
    woutP = np.ascontiguousarray(wout_T.reshape(F2, 16, 128).transpose(1, 0, 2)
                                 ).reshape(16 * F2, 128)
    wdP = np.ascontiguousarray(wd_T.reshape(F2, 16, 128).transpose(1, 0, 2)
                               ).reshape(16 * F2, 128)
    cstv = np.zeros((128, 12), np.float32)
    cstv[:, 0] = C32; cstv[:, 1] = C64; cstv[:, 2] = 0.75
    cstv[:, 3] = 1e-5; cstv[:, 4] = 1e-8; cstv[:, 5] = 1.0 / 2048.0
    cstv[:, 6] = 127.0
    cstv[:, 8] = -8.0
    cstv[:, 9] = -48.0
    cstv[:, 10] = np.float32(127 * np.log(2.0) + np.log(2.0 / 3.0))

    def colmaj(v):
        return np.ascontiguousarray(np.asarray(v, np.float32).reshape(ND, 128).T)
    shared = dict(
        cst=cstv,
        wq0=C(inputs["L0_attn_Wq"]), wk0=C(inputs["L0_attn_Wk"]),
        wv0=C(inputs["L0_attn_Wv"]), wo0=C(inputs["L0_attn_Wo"]),
        wgvT=T(inputs["L0_ffn_w_gv"]), woutP=woutP,
        wq1=T(wp[0:D]), wk1=T(wp[D:2 * D]), wv1=T(wp[2 * D:3 * D]),
        wo1=T(inputs["L1_attn_w_out"]),
        wgT=T(inputs["L1_ffn_W_g"]), wuT=T(inputs["L1_ffn_W_u"]), wdP=wdP)
    bq = colmaj(inputs["L0_attn_bq"]); bk = colmaj(inputs["L0_attn_bk"])
    bo = colmaj(inputs["L0_attn_bo"]); wn = colmaj(inputs["L0_ffn_w_norm"])
    bv = colmaj(inputs["L0_attn_bv"])
    vecs_shared = np.concatenate([bq, bk, bv, bo, wn], axis=1)
    maps = []
    for c in range(8):
        g, l = divmod(c, 4)
        vecsv = vecs_shared
        negdv = np.zeros((128, 64), np.float32)
        for tq in range(2):
            for ck in range(32):
                negdv[:, tq * 32 + ck] = 128 * ck - 1024 * l - 512 * tq
        ofsv = np.zeros((1, 96), np.int64)
        for hh in range(4):
            for r in range(4):
                col = 8 + (hh * 4 + r) * 3
                base = l * 4 * HD * TPC + r * BLK1 + hh * 128 * TPC
                ofsv[0, col] = base
                ofsv[0, col + 1] = base + D * TPC
                ofsv[0, col + 2] = l * 4 * HD + r * BLK1 + V_OFF + hh * 128
        for dc in range(ND):
            ofsv[0, 56 + dc] = l * TPC + dc * 128 * S
        for i in range(8):
            ofsv[0, 72 + i] = c * 256 * F1 + (i // 4) * 128 * F1 + (i % 4) * 4096
        for i in range(4):
            ofsv[0, 80 + i] = c * 16384 * 128 + i * 32 * 128 * 128
        m = dict(shared)
        m["vecs"] = vecsv
        m["negd"] = negdv
        m["x_t"] = np.ascontiguousarray(x[g, l * TPC:(l + 1) * TPC, :].T)
        m["ofs"] = ofsv.astype(np.int32)
        maps.append(m)
    return maps


def kernel(**inputs):
    maps = shard_inputs(inputs)
    nc = build_program(taps=TAPS)
    res = run_bass_kernel_spmd(nc, maps, core_ids=list(range(8)))
    out = np.zeros((B, S, D), np.float32)
    for c in range(8):
        g, l = divmod(c, 4)
        out[g, l * TPC:(l + 1) * TPC, :] = np.asarray(res.results[c]["y_t"]).T
    return out


# version 05
# speedup vs baseline: 1.5213x; 1.1417x over previous
import numpy as np
import ml_dtypes
from contextlib import ExitStack
import concourse.bass as bass
import concourse.mybir as mybir
from concourse.bass_utils import run_bass_kernel_spmd

F32 = mybir.dt.float32
BF16 = mybir.dt.bfloat16
FP16 = mybir.dt.float16
I32 = mybir.dt.int32
AF = mybir.ActivationFunctionType
ALU = mybir.AluOpType
AX = mybir.AxisListType

B, S, D, H, HD = 2, 4096, 2048, 16, 128
TPC = 1024
NT = TPC // 128
ND = D // 128
F1, F2 = 16384, 8192
C32 = 12582912.0
C64 = 6755399441055744.0
SM_SCALE = float(1.0 / np.sqrt(np.float64(HD)))
BLK1 = 2 * D * TPC
V_OFF = D * TPC

STAGE = 99
TAPS = []


class KB:

    def __init__(self, nc, block, ctx, sems):
        self.nc, self.block, self.ctx, self.sems = nc, block, ctx, sems

    def inc(self, sem, instr, k=1):
        instr.then_inc(self.sems[sem][0], k)
        self.sems[sem][1] += k

    def incd(self, sem, instr):
        self.inc(sem, instr, 16)

    def n(self, sem):
        return self.sems[sem][1]

    def wait(self, eng, sem, val=None):
        eng.wait_ge(self.sems[sem][0], self.sems[sem][1] if val is None else val)

    def on(self, engname, fn):
        getattr(self.block, engname)(fn)

    def barrier(self):
        engs = ("sync", "vector", "scalar", "tensor", "gpsimd")
        for en in engs:
            def f(eng):
                eng.drain()
                self.inc("ph", eng.nop())
            self.on(en, f)
        t = self.n("ph")
        for en in engs:
            def f(eng, t=t):
                eng.wait_ge(self.sems["ph"][0], t)
            self.on(en, f)


def build_program(taps=()):
    nc = bass.Bass("TRN2", target_bir_lowering=False, debug=False)

    def di(name, shape, dt=F32):
        return nc.dram_tensor(name, shape, dt, kind="ExternalInput").ap()

    x_t = di("x_t", [D, TPC])
    cst = di("cst", [128, 12])
    vecs = di("vecs", [128, 5 * ND])
    negd = di("negd", [128, 32])
    ofs = di("ofs", [1, 96], I32)
    wq0 = di("wq0", [D, D]); wk0 = di("wk0", [D, D]); wv0 = di("wv0", [D, D])
    wo0 = di("wo0", [D, D])
    wgvT = di("wgvT", [D, F1])
    woutP = di("woutP", [16 * F2, 128])
    wq1 = di("wq1", [D, D]); wk1 = di("wk1", [D, D]); wv1 = di("wv1", [D, D])
    wo1 = di("wo1", [D, D])
    wgT = di("wgT", [D, F2]); wuT = di("wuT", [D, F2])
    wdP = di("wdP", [16 * F2, 128])
    WQ = [wq0, wq1]; WK = [wk0, wk1]; WV = [wv0, wv1]; WO = [wo0, wo1]

    y_t = nc.dram_tensor("y_t", [D, TPC], F32, kind="ExternalOutput").ap()
    tap_out = {nm: nc.dram_tensor("tap_" + nm, sh, dt, kind="ExternalOutput").ap()
               for nm, sh, dt in taps}

    def dd(name, shape, dt):
        return nc.dram_tensor(name, shape, dt, kind="Internal").ap()

    ag1_in = [dd(f"ag1_in{L}", [2 * D, TPC], BF16) for L in range(2)]
    ag1_out = [dd(f"ag1_out{L}", [8 * D, TPC], BF16) for L in range(2)]
    ag2_in = [dd(f"ag2_in{L}", [4 * HD, S], BF16) for L in range(2)]
    ag2_out = [dd(f"ag2_out{L}", [D, S], BF16) for L in range(2)]
    ar_in = dd("ar_in", [1, 2], F32)
    ar_out = dd("ar_out", [1, 2], F32)
    t16_hbm = dd("t16_hbm", [NT * D * 128], FP16)
    qi_hbm = dd("qi_hbm", [ND * TPC * 128], BF16)
    u_hbm = dd("u_hbm", [64 * TPC * 128], FP16)
    p_hbm = dd("p_hbm", [64 * TPC * 128], BF16)
    x_hbm = dd("x_hbm", [D * TPC], F32)
    tern_hbm = dd("tern_hbm", [16 * F2, 128], BF16)
    row_hbm = dd("row_hbm", [4 * TPC], F32)

    with ExitStack() as ctx:
        E = ctx.enter_context
        X = E(nc.sbuf_tensor("X", [128, ND * TPC], F32))
        CS = E(nc.sbuf_tensor("CS", [128, 12], F32))
        VEC = E(nc.sbuf_tensor("VEC", [128, 5 * ND], F32))
        OFS = E(nc.sbuf_tensor("OFS", [1, 96], I32))
        IND = E(nc.sbuf_tensor("IND", [128, 2], F32))
        ONE1 = E(nc.sbuf_tensor("ONE1", [1, 128], F32))
        ONEF = E(nc.sbuf_tensor("ONEF", [128, 1], F32))
        ONEB = E(nc.sbuf_tensor("ONEB", [128, 2], BF16))
        TAU = E(nc.sbuf_tensor("TAU", [128, 8], F32))
        STT = E(nc.sbuf_tensor("STT", [128, 64], F32))
        ROW = E(nc.sbuf_tensor("ROW", [1, 2 * TPC], F32))
        NEGD = E(nc.sbuf_tensor("NEGD", [128, 32], F32))
        XN = None
        QT = None

        sems = {}
        for s in ("ph", "xld", "wld", "stat", "stA", "stB", "sgA", "sgB",
                  "shA", "shB", "ag", "mm", "dve", "act", "gp", "st", "tp",
                  "tpA", "tpB", "aux", "ev", "pq", "pqA", "pqB", "pr"):
            sems[s] = [E(nc.semaphore("sem_" + s)), 0]
        block = E(nc.Block())
        kb = KB(nc, block, ctx, sems)
        inc, incd, wait, on, n_ = kb.inc, kb.incd, kb.wait, kb.on, kb.n
        _uq = [0]
        def sbname(base):
            _uq[0] += 1
            return f"{base}_{_uq[0]}"

        def tap(name, src_ap):
            if name not in tap_out:
                return
            def f(sync):
                incd("st", sync.dma_start(out=tap_out[name][:, :], in_=src_ap))
                wait(sync, "st")
            on("sync", f)
            kb.barrier()

        def f(sync):
            incd("xld", sync.dma_start(out=CS[:, :], in_=cst[:, :]))
            incd("xld", sync.dma_start(out=VEC[:, :], in_=vecs[:, :]))
            incd("xld", sync.dma_start(out=OFS[:, :], in_=ofs[:, :]))
            incd("xld", sync.dma_start(out=NEGD[:, :], in_=negd[:, :]))
            incd("xld", sync.dma_start(
                out=bass.AP(X, 0, [[ND * TPC, 128], [TPC, ND], [1, TPC]]),
                in_=bass.AP(x_t.tensor, 0, [[TPC, 128], [128 * TPC, ND], [1, TPC]])))
        on("sync", f)

        def f(vector):
            wait(vector, "xld")
            vector.tensor_scalar(IND[:, 1:2], CS[:, 2:3], CS[:, 1:2], None, ALU.add)
            vector.drain()
            vector.tensor_scalar(IND[:, 0:1], IND[:, 1:2], CS[:, 1:2], None, ALU.subtract)
            inc("dve", vector.memset(ONE1[:, :], 1.0))
            inc("dve", vector.memset(ONEF[:, :], 1.0))
            inc("dve", vector.memset(ONEB[:, :], 1.0))
            inc("dve", vector.memset(STT[:, :], 0.0))
            inc("dve", vector.memset(ROW[0:1, :], 0.0))
        on("vector", f)




        def stats_phase2():
            with ExitStack() as c2:
                WS = c2.enter_context(nc.sbuf_tensor(sbname("WS"), [128, 2 * 4096], F32))
                ACC = c2.enter_context(nc.sbuf_tensor(sbname("ACC"), [128, 8], F32))
                PSA = c2.enter_context(nc.psum_tensor(sbname("PSA"), [128, 8], F32))
                pieces = []
                for j in range(8):
                    pieces.append((wgvT, 3, (j // 4) * 128 * F1 + (j % 4) * 4096,
                                   [[F1, 128], [1, 4096]], 0))
                for j in range(4):
                    pieces.append((woutP, 4, j * 32 * 128 * 128,
                                   [[128, 128], [128 * 128, 32], [1, 128]], 1))

                def g(gpsimd):
                    wait(gpsimd, "xld")
                    for i, (src, col, off, dims, _w) in enumerate(pieces):
                        if i >= 2:
                            wait(gpsimd, "dve", n_("dve") - 0)
                red_base = n_("dve")
                for i, (src, col, off, dims, w_id) in enumerate(pieces):
                    def gl(gpsimd, i=i, src=src, dims=dims):
                        if i == 0:
                            wait(gpsimd, "xld")
                        if i >= 2:
                            wait(gpsimd, "dve", red_base + (i - 1))
                        with gpsimd.register() as r1:
                            gpsimd.reg_load(r1, OFS[0:1, 72 + i:73 + i])
                            incd("stA" if i % 2 == 0 else "stB", gpsimd.dma_start(
                                out=WS[:, (i % 2) * 4096:(i % 2 + 1) * 4096],
                                in_=bass.AP(src.tensor, r1, dims)))
                    on("gpsimd", gl)

                    def vr(vector, i=i, w_id=w_id):
                        wait(vector, "stA" if i % 2 == 0 else "stB",
                             16 * (i // 2 + 1))
                        if i == 0:
                            vector.memset(ACC[:, 0:2], 0.0)
                        vector.reduce_sum(ACC[:, 4 + (i % 2):5 + (i % 2)],
                                          WS[:, (i % 2) * 4096:(i % 2 + 1) * 4096],
                                          axis=AX.X, apply_absolute_value=True)
                        vector.drain()
                        inc("dve", vector.tensor_tensor(
                            ACC[:, w_id:w_id + 1], ACC[:, w_id:w_id + 1],
                            ACC[:, 4 + (i % 2):5 + (i % 2)], ALU.add))
                        vector.drain()
                    on("vector", vr)

                def vv(vector):
                    vector.drain()
                    inc("dve", vector.tensor_copy(ACC[:, 2:4], ACC[:, 0:2]))
                on("vector", vv)

                def t(tensor):
                    wait(tensor, "dve")
                    inc("mm", tensor.matmul(PSA[0:1, 0:2], ONEF[:, 0:1], ACC[:, 2:4],
                                            start=True, stop=True))
                on("tensor", t)

                def v3(vector):
                    wait(vector, "mm")
                    inc("dve", vector.tensor_copy(ACC[0:1, 0:2], PSA[0:1, 0:2]))
                on("vector", v3)

                def s(sync):
                    wait(sync, "dve")
                    incd("stat", sync.dma_start(out=ar_in[0:1, :], in_=ACC[0:1, 0:2]))
                on("sync", s)

                def g3(gpsimd):
                    wait(gpsimd, "stat")
                    inc("ag", gpsimd.collective_compute(
                        "AllReduce", ALU.add,
                        replica_groups=[[0, 1, 2, 3, 4, 5, 6, 7]],
                        ins=[ar_in[:, :]], outs=[ar_out[:, :]]))
                on("gpsimd", g3)

                def s2(sync):
                    wait(sync, "ag")
                    incd("stat", sync.dma_start(out=ACC[0:1, 0:2], in_=ar_out[0:1, :]))
                on("sync", s2)

                def v4(vector):
                    wait(vector, "stat")
                    vector.tensor_scalar(ACC[0:1, 0:1], ACC[0:1, 0:1],
                                         1.0 / (D * F1), None, ALU.mult)
                    vector.tensor_scalar(ACC[0:1, 1:2], ACC[0:1, 1:2],
                                         1.0 / (D * F2), None, ALU.mult)
                    vector.drain()
                    vector.tensor_scalar(ACC[0:1, 0:2], ACC[0:1, 0:2],
                                         CS[0:1, 3:4], None, ALU.max)
                    vector.drain()
                    inc("dve", vector.tensor_scalar(ACC[0:1, 2:4], ACC[0:1, 0:2],
                                                    0.5, None, ALU.mult))
                on("vector", v4)

                def t3(tensor):
                    wait(tensor, "dve")
                    inc("mm", tensor.matmul(PSA[:, 4:8], ONE1[0:1, :], ACC[0:1, 0:4],
                                            start=True, stop=True))
                on("tensor", t3)

                def v5(vector):
                    wait(vector, "mm")
                    vector.tensor_copy(TAU[:, 4:6], PSA[:, 4:6])
                    vector.tensor_copy(TAU[:, 0:1], PSA[:, 6:7])
                    vector.tensor_copy(TAU[:, 2:3], PSA[:, 7:8])
                    vector.drain()
                    inc("dve", vector.tensor_scalar(TAU[:, 1:2], TAU[:, 0:1],
                                                    -1.0, None, ALU.mult))
                    inc("dve", vector.tensor_scalar(TAU[:, 3:4], TAU[:, 2:3],
                                                    -1.0, None, ALU.mult))
                on("vector", v5)
            kb.barrier()
        stats_phase2()

        def emit_ln(src, src_bf16, eps_col, fold_R, h_out):
            with ExitStack() as c2:
                CB = c2.enter_context(nc.sbuf_tensor(sbname("lnCB"), [128, 2 * TPC], BF16))
                SQ = c2.enter_context(nc.sbuf_tensor(sbname("lnSQ"), [128, 2 * TPC], BF16))
                PS1 = c2.enter_context(nc.psum_tensor(sbname("lnPS1"), [1, TPC], F32))
                PS2 = c2.enter_context(nc.psum_tensor(sbname("lnPS2"), [1, TPC], F32))
                PSB = c2.enter_context(nc.psum_tensor(sbname("lnPSB"), [128, 2 * TPC], F32))
                HW16 = c2.enter_context(nc.sbuf_tensor(sbname("lnHW16"), [128, 2 * TPC], FP16))
                TMPA = c2.enter_context(nc.sbuf_tensor(sbname("lnTA"), [128, TPC], F32))
                TMPB = c2.enter_context(nc.sbuf_tensor(sbname("lnTB"), [128, TPC], F32))
                LR = c2.enter_context(nc.sbuf_tensor(sbname("lnLR"), [1, 2 * TPC], F32))
                a0, d0, m0 = n_("act"), n_("dve"), n_("mm")
                tpa0, tpb0 = n_("tpA"), n_("tpB")
                for c in range(ND):
                    sl = slice(c * TPC, (c + 1) * TPC)
                    bsl = slice((c % 2) * TPC, (c % 2 + 1) * TPC)
                    if not src_bf16:
                        def vc(vector, c=c, sl=sl, bsl=bsl):
                            if c >= 2:
                                wait(vector, "mm", m0 + 4 * (c - 1))
                            inc("dve", vector.tensor_copy(CB[:, bsl], src[:, sl]))
                        on("vector", vc)
                    def sc(scalar, c=c, sl=sl, bsl=bsl):
                        if c == 0:
                            wait(scalar, "xld")
                        if c >= 2:
                            wait(scalar, "mm", m0 + 4 * (c - 1))
                        inc("act", scalar.activation(SQ[:, bsl], src[:, sl], AF.Square))
                    on("scalar", sc)
                    def tc(tensor, c=c, bsl=bsl):
                        if not src_bf16:
                            wait(tensor, "dve", d0 + (c + 1))
                        wait(tensor, "act", a0 + (c + 1))
                        for h in range(2):
                            hs = slice((c % 2) * TPC + 512 * h, (c % 2) * TPC + 512 * (h + 1))
                            os_ = slice(512 * h, 512 * (h + 1))
                            src_sl = (CB[:, hs] if not src_bf16 else
                                      src[:, c * TPC + 512 * h: c * TPC + 512 * (h + 1)])
                            inc("mm", tensor.matmul(PS1[0:1, os_], ONEB[:, 0:1], src_sl,
                                                    start=(c == 0), stop=(c == ND - 1)))
                            inc("mm", tensor.matmul(PS2[0:1, os_], ONEB[:, 0:1], SQ[:, hs],
                                                    start=(c == 0), stop=(c == ND - 1)))
                    on("tensor", tc)

                def ve(vector):
                    wait(vector, "mm")
                    vector.tensor_copy(ROW[0:1, 0:TPC], PS1[0:1, :])
                    vector.tensor_copy(ROW[0:1, TPC:2 * TPC], PS2[0:1, :])
                    vector.drain()
                    vector.tensor_scalar(ROW[0:1, 0:TPC], ROW[0:1, 0:TPC],
                                         CS[0:1, 5:6], None, ALU.mult)
                    vector.tensor_scalar(ROW[0:1, TPC:2 * TPC], ROW[0:1, TPC:2 * TPC],
                                         CS[0:1, 5:6], None, ALU.mult)
                    vector.drain()
                    vector.tensor_tensor(LR[0:1, 0:TPC], ROW[0:1, 0:TPC],
                                         ROW[0:1, 0:TPC], ALU.mult)
                    vector.drain()
                    vector.tensor_tensor(ROW[0:1, TPC:2 * TPC], ROW[0:1, TPC:2 * TPC],
                                         LR[0:1, 0:TPC], ALU.subtract)
                    vector.drain()
                    inc("dve", vector.tensor_scalar(
                        ROW[0:1, TPC:2 * TPC], ROW[0:1, TPC:2 * TPC],
                        CS[0:1, eps_col:eps_col + 1], None, ALU.add))
                on("vector", ve)

                def se(scalar):
                    wait(scalar, "dve")
                    scalar.activation(LR[0:1, 0:TPC], ROW[0:1, TPC:2 * TPC],
                                      AF.Ln)
                    scalar.drain()
                    inc("act", scalar.activation(
                        ROW[0:1, TPC:2 * TPC], LR[0:1, 0:TPC],
                        AF.Exp, scale=-0.5))
                on("scalar", se)

                def vf(vector):
                    wait(vector, "act")
                    inc("dve", vector.tensor_tensor(
                        LR[0:1, 0:TPC], ROW[0:1, 0:TPC],
                        ROW[0:1, TPC:2 * TPC], ALU.mult))
                on("vector", vf)

                if h_out == 'h':
                    def vq(vector):
                        wait(vector, "act")
                        vector.tensor_tensor(LR[0:1, TPC:2 * TPC],
                                             ROW[0:1, TPC:2 * TPC],
                                             ROW[0:1, TPC:2 * TPC], ALU.mult)
                        vector.drain()
                        inc("dve", vector.tensor_scalar(
                            LR[0:1, TPC:2 * TPC], LR[0:1, TPC:2 * TPC],
                            -1e-5, 1.0 + 1e-5, ALU.mult, ALU.add))
                    on("vector", vq)

                    def sq2(scalar):
                        wait(scalar, "dve")
                        scalar.activation(LR[0:1, TPC:2 * TPC],
                                          LR[0:1, TPC:2 * TPC], AF.Ln)
                        scalar.drain()
                        inc("act", scalar.activation(
                            LR[0:1, TPC:2 * TPC], LR[0:1, TPC:2 * TPC],
                            AF.Exp, scale=-0.5))
                    on("scalar", sq2)

                    def sq3(sync):
                        wait(sync, "act")
                        incd("tp", sync.dma_start(
                            out=bass.AP(row_hbm.tensor, 0, [[1, TPC]]),
                            in_=LR[0:1, TPC:2 * TPC]))
                        wait(sync, "tp")
                        incd("tp", sync.dma_start(
                            out=STT[:, 32:40],
                            in_=bass.AP(row_hbm.tensor, 0, [[1, 128], [128, 8]]),
                            allow_slow_non_contiguous=True))
                        wait(sync, "tp")
                        inc("aux", sync.nop())
                    on("sync", sq3)

                def tb(tensor):
                    wait(tensor, "dve")
                    for h in range(2):
                        inc("mm", tensor.matmul(
                            PSB[:, 512 * h:512 * (h + 1)], ONE1[0:1, :],
                            ROW[0:1, TPC + 512 * h:TPC + 512 * (h + 1)],
                            start=True, stop=True))
                        inc("mm", tensor.matmul(
                            PSB[:, TPC + 512 * h:TPC + 512 * (h + 1)], ONE1[0:1, :],
                            LR[0:1, 512 * h:512 * (h + 1)],
                            start=True, stop=True))
                on("tensor", tb)

                ms0 = n_("mm")
                for c in range(ND):
                    sl = slice(c * TPC, (c + 1) * TPC)
                    def va(vector, c=c, sl=sl):
                        if c == 0:
                            wait(vector, "mm", ms0)
                        vector.drain()
                        vector.tensor_tensor(TMPA[:, :], src[:, sl], PSB[:, 0:TPC],
                                             ALU.mult)
                        vector.drain()
                        if h_out == 'xq':
                            inc("dve", vector.tensor_tensor(TMPB[:, :], TMPA[:, :],
                                                            PSB[:, TPC:2 * TPC],
                                                            ALU.subtract))
                        elif fold_R:
                            vector.tensor_tensor(TMPB[:, :], TMPA[:, :],
                                                 PSB[:, TPC:2 * TPC], ALU.subtract)
                            vector.drain()
                            inc("dve", vector.tensor_tensor(X[:, sl], X[:, sl],
                                                            TMPB[:, :], ALU.add))
                        else:
                            inc("dve", vector.tensor_tensor(XN[:, sl], TMPA[:, :],
                                                            PSB[:, TPC:2 * TPC],
                                                            ALU.subtract))
                        if h_out == 'h':
                            vector.drain()
                            inc("dve", vector.tensor_scalar(
                                TMPA[:, :], TMPB[:, :],
                                VEC[:, 4 * ND + c:4 * ND + c + 1], None, ALU.mult))
                    on("vector", va)
                    if h_out is not None:
                        def vh(vector, c=c):
                            if c >= 2:
                                wait(vector, "tpA" if c % 2 == 0 else "tpB",
                                     tpa0 + 16 * ((c - 2) // 2 + 1) if c % 2 == 0
                                     else tpb0 + 16 * ((c - 2) // 2 + 1))
                            vector.drain()
                            inc("dve", vector.tensor_copy(
                                HW16[:, (c % 2) * TPC:(c % 2 + 1) * TPC],
                                TMPA[:, :] if h_out == 'h' else TMPB[:, :]))
                        on("vector", vh)
                        def sh(sync, c=c):
                            wait(sync, "dve")
                            incd("tpA" if c % 2 == 0 else "tpB", sync.dma_start(
                                out=bass.AP(t16_hbm.tensor, c * 128 * 128,
                                            [[128, 128], [D * 128, NT], [1, 128]]),
                                in_=HW16[:, (c % 2) * TPC:(c % 2 + 1) * TPC]))
                        on("sync", sh)
                if h_out is not None:
                    def sw(sync):
                        wait(sync, "tpA")
                        wait(sync, "tpB")
                        inc("aux", sync.nop())
                    on("sync", sw)
            kb.barrier()

        def emit_proj(wdram, mode, li, bias_col=None, dest=None, dest_off=0):
            with ExitStack() as c2:
                WPN = c2.enter_context(nc.sbuf_tensor(sbname("pjW"), [128, 2 * ND * 512], BF16))
                STG = c2.enter_context(nc.sbuf_tensor(sbname("pjS"), [128, 2 * TPC], BF16))
                PSQa = c2.enter_context(nc.psum_tensor(sbname("pjP"), [128, TPC], F32))
                PSQb = c2.enter_context(nc.psum_tensor(sbname("pjP"), [128, TPC], F32))
                PSQs = (PSQa, PSQb)
                wlsem = ("stA", "stB")
                wl0 = (n_("stA"), n_("stB"))
                sgsem = ("sgA", "sgB")
                sg0 = (n_("sgA"), n_("sgB"))
                mm0, ev0 = n_("mm"), n_("ev")

                for p in range(4):
                    def gw(gpsimd, p=p):
                        if p == 0:
                            wait(gpsimd, "ph")
                        if p >= 2:
                            wait(gpsimd, "mm", mm0 + 4 * (p - 1) *
                                 (32 if mode != 'tm' else 32))
                        incd(wlsem[p % 2], gpsimd.dma_start(
                            out=bass.AP(WPN, (p % 2) * ND * 512,
                                        [[2 * ND * 512, 128], [512, ND], [1, 512]]),
                            in_=bass.AP(wdram.tensor, 512 * p,
                                        [[D, 128], [128 * D, ND], [1, 512]])))
                    on("gpsimd", gw)

                if mode in ('fm', 'fm_sb', 'res'):
                    for p in range(4):
                        for ocr in range(4):
                            oc = 4 * p + ocr
                            def tm(tensor, p=p, ocr=ocr, oc=oc):
                                if ocr == 0:
                                    wait(tensor, wlsem[p % 2], 16 * (wl0[p % 2] // 16 + p // 2 + 1))
                                if oc >= 2:
                                    wait(tensor, "ev", ev0 + oc - 1)
                                for th in range(2):
                                    for dc in range(ND):
                                        tensor.matmul(
                                            PSQs[oc % 2][:, th * 512:(th + 1) * 512],
                                            WPN[:, (p % 2) * ND * 512 + dc * 512 +
                                                ocr * 128: (p % 2) * ND * 512 +
                                                dc * 512 + (ocr + 1) * 128],
                                            XN[:, dc * TPC + th * 512:
                                               dc * TPC + (th + 1) * 512],
                                            start=(dc == 0), stop=(dc == ND - 1))
                                tensor.drain()
                                inc("mm", tensor.nop(), 32)
                            on("tensor", tm)

                            if mode == 'fm_sb':
                                def se_(scalar, oc=oc):
                                    wait(scalar, "mm", mm0 + 32 * (oc + 1))
                                    scalar.drain()
                                    inc("ev", scalar.activation(
                                        QT[:, oc * TPC:(oc + 1) * TPC],
                                        PSQs[oc % 2][:, :],
                                        AF.Identity,
                                        bias=(VEC[:, bias_col + oc:bias_col + oc + 1]
                                              if bias_col is not None else CS[:, 7:8])))
                                on("scalar", se_)
                            elif mode == 'fm':
                                def se_(scalar, oc=oc):
                                    wait(scalar, "mm", mm0 + 32 * (oc + 1))
                                    if oc >= 2:
                                        wait(scalar, sgsem[oc % 2],
                                             sg0[oc % 2] + 16 * ((oc - 2) // 2 + 1))
                                    scalar.drain()
                                    inc("ev", scalar.activation(
                                        STG[:, (oc % 2) * TPC:(oc % 2 + 1) * TPC],
                                        PSQs[oc % 2][:, :],
                                        AF.Identity,
                                        bias=(VEC[:, bias_col + oc:bias_col + oc + 1]
                                              if bias_col is not None else CS[:, 7:8])))
                                on("scalar", se_)
                                def sd(sync, oc=oc):
                                    wait(sync, "ev", ev0 + oc + 1)
                                    incd(sgsem[oc % 2], sync.dma_start(
                                        out=bass.AP(dest.tensor, dest_off + oc * 128 * TPC,
                                                    [[TPC, 128], [1, TPC]]),
                                        in_=STG[:, (oc % 2) * TPC:(oc % 2 + 1) * TPC]))
                                on("sync", sd)
                            else:
                                def ve_(vector, oc=oc, li=li):
                                    wait(vector, "mm", mm0 + 32 * (oc + 1))
                                    vector.drain()
                                    if bias_col is not None:
                                        inc("ev", vector.scalar_tensor_tensor(
                                            X[:, oc * TPC:(oc + 1) * TPC],
                                            PSQs[oc % 2][:, :],
                                            VEC[:, bias_col + oc:bias_col + oc + 1],
                                            X[:, oc * TPC:(oc + 1) * TPC],
                                            ALU.add, ALU.add))
                                    else:
                                        inc("ev", vector.tensor_tensor(
                                            X[:, oc * TPC:(oc + 1) * TPC],
                                            X[:, oc * TPC:(oc + 1) * TPC],
                                            PSQs[oc % 2][:, :],
                                            ALU.add))
                                on("vector", ve_)
                else:
                    for p in range(4):
                        for tt in range(NT):
                            idx = p * NT + tt
                            def tm(tensor, p=p, tt=tt, idx=idx):
                                if tt == 0:
                                    wait(tensor, wlsem[p % 2], 16 * (wl0[p % 2] // 16 + p // 2 + 1))
                                if idx >= 2:
                                    wait(tensor, "ev", ev0 + idx - 1)
                                for dc in range(ND):
                                    tensor.matmul(
                                        PSQs[idx % 2][:, 0:512],
                                        XN[:, dc * TPC + tt * 128:dc * TPC + (tt + 1) * 128],
                                        WPN[:, (p % 2) * ND * 512 + dc * 512:
                                            (p % 2) * ND * 512 + (dc + 1) * 512],
                                        start=(dc == 0), stop=(dc == ND - 1))
                                tensor.drain()
                                inc("mm", tensor.nop(), 16)
                            on("tensor", tm)

                            def ve_(vector, p=p, tt=tt, idx=idx):
                                wait(vector, "mm", mm0 + 16 * (idx + 1))
                                if idx >= 2:
                                    wait(vector, sgsem[idx % 2],
                                         sg0[idx % 2] + 16 * ((idx - 2) // 2 + 1))
                                vector.drain()
                                inc("ev", vector.tensor_copy(
                                    STG[:, (idx % 2) * TPC:(idx % 2) * TPC + 512],
                                    PSQs[idx % 2][:, 0:512]))
                            on("vector", ve_)

                            def sd(sync, p=p, tt=tt, idx=idx):
                                wait(sync, "ev", ev0 + idx + 1)
                                incd(sgsem[idx % 2], sync.dma_start(
                                    out=bass.AP(dest.tensor,
                                                dest_off + tt * 128 * D + p * 512,
                                                [[D, 128], [1, 512]]),
                                    in_=STG[:, (idx % 2) * TPC:(idx % 2) * TPC + 512]))
                            on("sync", sd)
                def fin_w(sync):
                    wait(sync, "sgA")
                    wait(sync, "sgB")
                on("sync", fin_w)
            kb.barrier()

        def emit_attn(li, causal):
            with ExitStack() as c2:
                IOT0 = (c2.enter_context(nc.sbuf_tensor(sbname("atIO"), [128, TPC], F32))
                        if causal else None)

                def g_ag(gpsimd):
                    wait(gpsimd, "ph")
                    if causal:
                        gpsimd.iota(IOT0[:, :], [[1, TPC]], base=0,
                                    channel_multiplier=-1,
                                    allow_small_or_imprecise_dtypes=True)
                    inc("ag", gpsimd.collective_compute(
                        "AllGather", ALU.bypass,
                        replica_groups=[[0, 1, 2, 3], [4, 5, 6, 7]],
                        ins=[ag1_in[li][:, :]], outs=[ag1_out[li][:, :]]))
                on("gpsimd", g_ag)

                KH = c2.enter_context(nc.sbuf_tensor(sbname("atK"), [128, 2 * S], BF16))
                VH = c2.enter_context(nc.sbuf_tensor(sbname("atV"), [128, 2 * S], BF16))
                PRB = c2.enter_context(nc.sbuf_tensor(sbname("atP"), [128, 4 * TPC], BF16))
                MKB = c2.enter_context(nc.sbuf_tensor(sbname("atM"), [128, TPC], BF16))
                RB = c2.enter_context(nc.sbuf_tensor(sbname("atR"), [128, TPC], F32))
                T1 = c2.enter_context(nc.sbuf_tensor(sbname("atT"), [128, TPC], F32))
                ZR = c2.enter_context(nc.sbuf_tensor(sbname("atZ"), [1, TPC], F32))
                ZY = c2.enter_context(nc.sbuf_tensor(sbname("atY"), [1, TPC], F32))
                ZT = T1
                psSs = tuple(c2.enter_context(nc.psum_tensor(sbname("atPS"), [128, TPC], F32))
                             for _ in range(2))
                psO = c2.enter_context(nc.psum_tensor(sbname("atPO"), [128, TPC], F32))
                psZ = c2.enter_context(nc.psum_tensor(sbname("atPZ"), [1, TPC], F32))
                pq0 = (n_("pqA"), n_("pqB"))

                for h in range(H):
                    hs = (h % 2) * S
                    def gl(sync, h=h, hs=hs):
                        if h == 0:
                            wait(sync, "ag")
                        elif h >= 2:
                            wait(sync, "ev", n_("ev") - 1)
                        pqs = "pqA" if h % 2 == 0 else "pqB"
                        for r in range(4):
                            incd(pqs, sync.dma_start(
                                out=KH[:, hs + r * TPC:hs + (r + 1) * TPC],
                                in_=bass.AP(ag1_out[li].tensor,
                                            r * BLK1 + h * 128 * TPC,
                                            [[TPC, 128], [1, TPC]])))
                            incd(pqs, sync.dma_start(
                                out=bass.AP(VH, hs + r * 8 * 128,
                                            [[2 * S, 128], [128, 8], [1, 128]]),
                                in_=bass.AP(ag1_out[li].tensor,
                                            r * BLK1 + V_OFF + h * 128,
                                            [[D, 128], [128 * D, 8], [1, 128]])))
                    on("sync", gl)
                    pq_t = ("pqA" if h % 2 == 0 else "pqB",
                            n_("pqA" if h % 2 == 0 else "pqB"))

                    exp_done = {}
                    msk_done = {}
                    mm_qk = {}
                    pv_done = {}

                    def emit_qk(tensor, h, hs, ck, pq_t):
                        if ck <= 1:
                            wait(tensor, pq_t[0], pq_t[1])
                        if ck >= 2:
                            wait(tensor, "act", exp_done[ck - 2])
                        for th in range(2):
                            inc("mm", tensor.matmul(
                                psSs[ck % 2][:, th * 512:(th + 1) * 512],
                                KH[:, hs + ck * 128:hs + (ck + 1) * 128],
                                QT[:, h * TPC + th * 512:h * TPC + (th + 1) * 512],
                                start=True, stop=True))
                        mm_qk[ck] = n_("mm")

                    def emit_exp(scalar, li, ck):
                        wait(scalar, "mm", mm_qk[ck])
                        if ck >= 4:
                            wait(scalar, "mm", pv_done[ck - 4])
                        inc("act", scalar.activation(
                            PRB[:, (ck % 4) * TPC:(ck % 4 + 1) * TPC],
                            psSs[ck % 2][:, :],
                            AF.Exp, scale=SM_SCALE, bias=CS[:, 8 + li:9 + li]))
                        exp_done[ck] = n_("act")

                    def emit_msk(vector, ck):
                        wait(vector, "act", exp_done[ck])
                        vector.drain()
                        vector.tensor_scalar(
                            MKB[:, :], IOT0[:, :],
                            NEGD[:, ck:ck + 1], None, ALU.is_ge)
                        vector.drain()
                        inc("dve", vector.tensor_tensor(
                            PRB[:, (ck % 4) * TPC:(ck % 4 + 1) * TPC],
                            PRB[:, (ck % 4) * TPC:(ck % 4 + 1) * TPC],
                            MKB[:, :], ALU.mult))
                        msk_done[ck] = n_("dve")

                    def emit_pv(tensor, hs, ck, causal):
                        if causal:
                            wait(tensor, "dve", msk_done[ck])
                        else:
                            wait(tensor, "act", exp_done[ck])
                        for th in range(2):
                            inc("mm", tensor.matmul(
                                psO[:, th * 512:(th + 1) * 512],
                                VH[:, hs + ck * 128:hs + (ck + 1) * 128],
                                PRB[:, (ck % 4) * TPC + th * 512:
                                    (ck % 4) * TPC + (th + 1) * 512],
                                start=(ck == 0), stop=(ck == 31)))
                        for th in range(2):
                            inc("mm", tensor.matmul(
                                psZ[0:1, th * 512:(th + 1) * 512], ONEB[:, 0:1],
                                PRB[:, (ck % 4) * TPC + th * 512:
                                    (ck % 4) * TPC + (th + 1) * 512],
                                start=(ck == 0), stop=(ck == 31)))
                        pv_done[ck] = n_("mm")

                    def t_pre(tensor, h=h, hs=hs, pq_t=pq_t):
                        emit_qk(tensor, h, hs, 0, pq_t)
                        emit_qk(tensor, h, hs, 1, pq_t)
                    on("tensor", t_pre)
                    for ck in range(32):
                        def s_e(scalar, li=li, ck=ck):
                            emit_exp(scalar, li, ck)
                        on("scalar", s_e)
                        if causal:
                            def v_m(vector, ck=ck):
                                emit_msk(vector, ck)
                            on("vector", v_m)
                        def t_b(tensor, h=h, hs=hs, ck=ck, pq_t=pq_t, causal=causal):
                            emit_pv(tensor, hs, ck, causal)
                            if ck + 2 < 32:
                                emit_qk(tensor, h, hs, ck + 2, pq_t)
                        on("tensor", t_b)

                    mm_t = n_("mm")
                    def v_z(vector, mm_t=mm_t):
                        wait(vector, "mm", mm_t)
                        vector.drain()
                        vector.tensor_copy(ZR[0:1, :], psZ[0:1, :])
                        vector.drain()
                        vector.tensor_scalar(
                            ZT[0:1, :].bitcast(I32), ZR[0:1, :].bitcast(I32),
                            23, None, ALU.logical_shift_right)
                        vector.drain()
                        inc("dve", vector.tensor_copy(ZY[0:1, :],
                                                      ZT[0:1, :].bitcast(I32)))
                    on("vector", v_z)

                    def s_seed(scalar):
                        wait(scalar, "dve")
                        scalar.drain()
                        inc("act", scalar.activation(
                            ZY[0:1, :], ZY[0:1, :], AF.Exp,
                            scale=-0.6931471805599453, bias=CS[0:1, 10:11]))
                    on("scalar", s_seed)

                    def v_nt(vector):
                        wait(vector, "act")
                        for _ in range(4):
                            vector.drain()
                            vector.tensor_tensor(ZT[0:1, :], ZR[0:1, :],
                                                 ZY[0:1, :], ALU.mult)
                            vector.drain()
                            vector.tensor_scalar(ZT[0:1, :], ZT[0:1, :], -1.0,
                                                 2.0, ALU.mult, ALU.add)
                            vector.drain()
                            vector.tensor_tensor(ZY[0:1, :], ZY[0:1, :],
                                                 ZT[0:1, :], ALU.mult)
                        vector.drain()
                        inc("dve", vector.nop())
                    on("vector", v_nt)

                    def t_bc(tensor):
                        wait(tensor, "dve")
                        for th in range(2):
                            inc("mm", tensor.matmul(
                                psSs[0][:, th * 512:(th + 1) * 512], ONE1[0:1, :],
                                ZY[0:1, th * 512:(th + 1) * 512],
                                start=True, stop=True))
                    on("tensor", t_bc)

                    def v_ev(vector, li=li, h=h):
                        wait(vector, "mm")
                        vector.drain()
                        vector.tensor_copy(RB[:, :], psSs[0][:, :])
                        vector.drain()
                        vector.tensor_tensor(T1[:, :], psO[:, :], RB[:, :], ALU.mult)
                        vector.drain()
                        if li == 0:
                            inc("ev", vector.tensor_scalar(
                                XN[:, h * TPC:(h + 1) * TPC], T1[:, :],
                                VEC[:, 2 * ND + h:2 * ND + h + 1], None, ALU.add))
                        else:
                            inc("ev", vector.tensor_copy(
                                XN[:, h * TPC:(h + 1) * TPC], T1[:, :]))
                    on("vector", v_ev)

                    def t_gate(tensor):
                        wait(tensor, "ev")
                        inc("mm", tensor.nop())
                    on("tensor", t_gate)
            kb.barrier()
        def emit_hquant(use_rsq, use_eps, rowsum):
            with ExitStack() as c2:
                TMQ = c2.enter_context(nc.sbuf_tensor(sbname("qT"), [128, NT * D], FP16))
                W0 = c2.enter_context(nc.sbuf_tensor(sbname("qW"), [128, D], F32))
                R32 = c2.enter_context(nc.sbuf_tensor(sbname("qR3"), [128, D], F32))
                R64 = c2.enter_context(nc.sbuf_tensor(sbname("qR6"), [128, D], F32))
                QTI = c2.enter_context(nc.sbuf_tensor(sbname("qQ"), [128, 2 * D], BF16))
                tp0 = (n_("tpA"), n_("tpB"))
                def s_tp(sync):
                    wait(sync, "ph")
                    for j in range(NT):
                        incd("tpA" if j % 2 == 0 else "tpB", sync.dma_start_transpose(
                            TMQ[:, j * D:(j + 1) * D],
                            bass.AP(t16_hbm.tensor, j * D * 128, [[128, D], [1, 128]])))
                on("sync", s_tp)

                def v_rm(vector):
                    wait(vector, "tpA")
                    wait(vector, "tpB")
                    for j in range(NT):
                        vector.drain()
                        inc("dve", vector.reduce_max(
                            STT[:, j:j + 1], TMQ[:, j * D:(j + 1) * D],
                            axis=AX.X, apply_absolute_value=True))
                on("vector", v_rm)

                def v_sc(vector):
                    wait(vector, "dve")
                    wait(vector, "aux")
                    vector.drain()
                    if use_rsq:
                        vector.tensor_tensor(STT[:, 40:48], STT[:, 0:8],
                                             STT[:, 32:40], ALU.mult)
                    else:
                        vector.tensor_copy(STT[:, 40:48], STT[:, 0:8])
                    vector.drain()
                    if use_eps:
                        vector.tensor_scalar(STT[:, 40:48], STT[:, 40:48],
                                             CS[:, 3:4], None, ALU.max)
                        vector.drain()
                    inc("dve", vector.nop())
                on("vector", v_sc)

                def s_sc(scalar):
                    wait(scalar, "dve")
                    scalar.drain()
                    scalar.activation(STT[:, 48:56], STT[:, 40:48], AF.Ln)
                    scalar.drain()
                    inc("act", scalar.activation(STT[:, 48:56], STT[:, 48:56],
                                                 AF.Exp, scale=-1.0))
                on("scalar", s_sc)

                def v_sc2(vector):
                    wait(vector, "act")
                    vector.drain()
                    vector.tensor_scalar(STT[:, 48:56], STT[:, 48:56], 127.0,
                                         None, ALU.mult)
                    vector.tensor_scalar(STT[:, 8:16], STT[:, 40:48], 1.0 / 127.0,
                                         None, ALU.mult)
                    vector.drain()
                    if use_rsq:
                        inc("dve", vector.tensor_tensor(
                            STT[:, 16:24], STT[:, 48:56], STT[:, 32:40], ALU.mult))
                    else:
                        inc("dve", vector.tensor_copy(STT[:, 16:24], STT[:, 48:56]))
                on("vector", v_sc2)

                tq0 = (n_("shA"), n_("shB"))
                for j in range(NT):
                    def v_q(vector, j=j):
                        if j == 0:
                            wait(vector, "dve")
                        if j >= 2:
                            wait(vector, "shA" if j % 2 == 0 else "shB",
                                 tq0[j % 2] + 16 * (j // 2))
                        vector.drain()
                        vector.tensor_scalar(W0[:, :], TMQ[:, j * D:(j + 1) * D],
                                             STT[:, 16 + j:17 + j], None, ALU.mult)
                        vector.drain()
                        vector.tensor_scalar(R32[:, :], W0[:, :], CS[:, 0:1], None, ALU.add)
                        vector.drain()
                        vector.tensor_scalar(R32[:, :], R32[:, :], CS[:, 0:1], None, ALU.subtract)
                        vector.tensor_scalar(R64[:, :], W0[:, :], CS[:, 1:2], None, ALU.add)
                        vector.drain()
                        vector.tensor_scalar(R64[:, :], R64[:, :], CS[:, 1:2], None, ALU.subtract)
                        vector.drain()
                        vector.tensor_tensor(R64[:, :], R64[:, :], R32[:, :], ALU.subtract)
                        vector.drain()
                        inc("dve", vector.scalar_tensor_tensor(
                            QTI[:, (j % 2) * D:(j % 2 + 1) * D], R64[:, :],
                            IND[:, 0:1], R32[:, :], ALU.mult, ALU.add))
                        if rowsum:
                            vector.drain()
                            inc("dve", vector.reduce_sum(
                                STT[:, 24 + j:25 + j],
                                QTI[:, (j % 2) * D:(j % 2 + 1) * D], axis=AX.X))
                    on("vector", v_q)

                    def s_sc3(sync, j=j):
                        wait(sync, "dve")
                        incd("shA" if j % 2 == 0 else "shB", sync.dma_start(
                            out=bass.AP(qi_hbm.tensor, j * 128 * 128,
                                        [[128, 128], [TPC * 128, ND], [1, 128]]),
                            in_=bass.AP(QTI, (j % 2) * D,
                                        [[2 * D, 128], [128, ND], [1, 128]])))
                    on("sync", s_sc3)

                def s_fin(sync):
                    wait(sync, "shA")
                    wait(sync, "shB")
                    for dc in range(ND):
                        incd("tp", sync.dma_start_transpose(
                            XN[:, dc * TPC:(dc + 1) * TPC],
                            bass.AP(qi_hbm.tensor, dc * TPC * 128, [[128, TPC], [1, 128]])))
                    wait(sync, "tp")
                    inc("aux", sync.nop())
                on("sync", s_fin)
            kb.barrier()

        def emit_gv():
            PW = 256
            with ExitStack() as c2:
                WF = c2.enter_context(nc.sbuf_tensor(sbname("gW"), [128, 2 * ND * PW], F32))
                WG = c2.enter_context(nc.sbuf_tensor(sbname("gG"), [128, 2 * ND * PW], F32))
                TV = c2.enter_context(nc.sbuf_tensor(sbname("gTV"), [128, 2 * ND * PW], BF16))
                TG = c2.enter_context(nc.sbuf_tensor(sbname("gTG"), [128, 2 * ND * PW], BF16))
                GS = c2.enter_context(nc.sbuf_tensor(sbname("gGS"), [128, 2 * PW], F32))
                UO = c2.enter_context(nc.sbuf_tensor(sbname("gUO"), [128, 2 * PW], FP16))
                SC2 = c2.enter_context(nc.sbuf_tensor(sbname("gS2"), [128, 8], F32))
                psVs = tuple(c2.enter_context(nc.psum_tensor(sbname("gPV"), [128, PW], F32))
                             for _ in range(2))
                psGs = tuple(c2.enter_context(nc.psum_tensor(sbname("gPG"), [128, PW], F32))
                             for _ in range(2))
                wl0 = (n_("shA"), n_("shB"))
                u0 = (n_("sgA"), n_("sgB"))
                evb = n_("ev")
                tern_done = {}

                def v_pre(vector):
                    wait(vector, "ph")
                    vector.drain()
                    inc("dve", vector.tensor_scalar(SC2[:, :], STT[:, 8:16],
                                                    TAU[:, 4:5], None, ALU.mult))
                on("vector", v_pre)

                for p in range(32):
                    def s_w(sync, p=p):
                        if p >= 2:
                            wait(sync, "dve", tern_done[p - 2])
                        incd("shA" if p % 2 == 0 else "shB", sync.dma_start(
                            out=bass.AP(WF, (p % 2) * ND * PW,
                                        [[2 * ND * PW, 128], [PW, ND], [1, PW]]),
                            in_=bass.AP(wgvT.tensor, PW * p,
                                        [[F1, 128], [128 * F1, ND], [1, PW]])))
                        incd("shA" if p % 2 == 0 else "shB", sync.dma_start(
                            out=bass.AP(WG, (p % 2) * ND * PW,
                                        [[2 * ND * PW, 128], [PW, ND], [1, PW]]),
                            in_=bass.AP(wgvT.tensor, F2 + PW * p,
                                        [[F1, 128], [128 * F1, ND], [1, PW]])))
                    on("sync", s_w)

                    def v_t(vector, p=p):
                        wait(vector, "shA" if p % 2 == 0 else "shB",
                             wl0[p % 2] + 32 * (p // 2 + 1))
                        if p >= 2:
                            wait(vector, "mm", n_("mm") - 256)
                        base = (p % 2) * ND * PW
                        sl = slice(base, base + ND * PW)
                        for (src, dst) in ((WF, TV), (WG, TG)):
                            vector.drain()
                            vector.tensor_scalar(dst[:, sl], src[:, sl],
                                                 TAU[:, 1:2], None, ALU.is_le)
                            vector.drain()
                            vector.scalar_tensor_tensor(
                                dst[:, sl], src[:, sl], TAU[:, 0:1], dst[:, sl],
                                ALU.is_ge, ALU.subtract)
                        vector.drain()
                        inc("dve", vector.nop())
                    on("vector", v_t)
                    tern_t = n_("dve")
                    tern_done[p] = tern_t

                    for tt in range(NT):
                        idx = p * NT + tt
                        def t_m(tensor, p=p, tt=tt, idx=idx, tern_t=tern_t):
                            if tt == 0:
                                wait(tensor, "dve", tern_t)
                            if idx >= 2:
                                wait(tensor, "ev", evb + idx - 1)
                            for dc in range(ND):
                                tensor.matmul(
                                    psVs[tt % 2][:, :],
                                    XN[:, dc * TPC + tt * 128:dc * TPC + (tt + 1) * 128],
                                    TV[:, (p % 2) * ND * PW + dc * PW:
                                        (p % 2) * ND * PW + (dc + 1) * PW],
                                    start=(dc == 0), stop=(dc == ND - 1))
                                tensor.matmul(
                                    psGs[tt % 2][:, :],
                                    XN[:, dc * TPC + tt * 128:dc * TPC + (tt + 1) * 128],
                                    TG[:, (p % 2) * ND * PW + dc * PW:
                                        (p % 2) * ND * PW + (dc + 1) * PW],
                                    start=(dc == 0), stop=(dc == ND - 1))
                            tensor.drain()
                            inc("mm", tensor.nop(), 32)
                        on("tensor", t_m)
                        mmv = n_("mm")

                        def s_g(scalar, tt=tt, idx=idx, mmv=mmv):
                            wait(scalar, "mm", mmv)
                            if idx >= 2:
                                wait(scalar, "ev", evb + idx - 1)
                            scalar.drain()
                            inc("act", scalar.activation(
                                GS[:, (idx % 2) * PW:(idx % 2 + 1) * PW],
                                psGs[tt % 2][:, :],
                                AF.Silu, scale=SC2[:, tt:tt + 1]))
                        on("scalar", s_g)
                        actv = n_("act")

                        def v_u(vector, p=p, tt=tt, idx=idx, actv=actv):
                            wait(vector, "act", actv)
                            if idx >= 2:
                                wait(vector, "sgA" if idx % 2 == 0 else "sgB",
                                     u0[idx % 2] + 16 * ((idx - 2) // 2 + 1))
                            vector.drain()
                            vector.scalar_tensor_tensor(
                                UO[:, (idx % 2) * PW:(idx % 2 + 1) * PW],
                                psVs[tt % 2][:, :], SC2[:, tt:tt + 1],
                                GS[:, (idx % 2) * PW:(idx % 2 + 1) * PW],
                                ALU.mult, ALU.mult)
                            vector.drain()
                            vector.reduce_max(STT[:, 48 + tt:49 + tt],
                                              UO[:, (idx % 2) * PW:(idx % 2 + 1) * PW],
                                              axis=AX.X, apply_absolute_value=True)
                            vector.drain()
                            if p == 0:
                                inc("dve", vector.tensor_copy(STT[:, 0 + tt:1 + tt],
                                                              STT[:, 48 + tt:49 + tt]))
                            else:
                                inc("dve", vector.tensor_tensor(
                                    STT[:, 0 + tt:1 + tt], STT[:, 0 + tt:1 + tt],
                                    STT[:, 48 + tt:49 + tt], ALU.max))
                            inc("ev", vector.nop())
                        on("vector", v_u)

                        def s_u(sync, p=p, tt=tt, idx=idx):
                            wait(sync, "ev")
                            incd("sgA" if idx % 2 == 0 else "sgB", sync.dma_start(
                                out=bass.AP(u_hbm.tensor,
                                            2 * p * TPC * 128 + tt * 128 * 128,
                                            [[128, 128], [TPC * 128, 2], [1, 128]]),
                                in_=bass.AP(UO, (idx % 2) * PW,
                                            [[2 * PW, 128], [128, 2], [1, 128]])))
                        on("sync", s_u)

                def s_wd(sync):
                    wait(sync, "sgA")
                    wait(sync, "sgB")
                    inc("aux", sync.nop())
                on("sync", s_wd)
            kb.barrier()

        def emit_ustats(descale_tau_col, out_negate2=False):
            def v_s(vector):
                wait(vector, "ph")
                vector.drain()
                vector.tensor_scalar(STT[:, 40:48], STT[:, 0:8], CS[:, 3:4],
                                     None, ALU.max)
                vector.drain()
                inc("dve", vector.nop())
            on("vector", v_s)

            def s_s(scalar):
                wait(scalar, "dve")
                scalar.drain()
                scalar.activation(STT[:, 48:56], STT[:, 40:48], AF.Ln)
                scalar.drain()
                inc("act", scalar.activation(STT[:, 48:56], STT[:, 48:56],
                                             AF.Exp, scale=-1.0))
            on("scalar", s_s)

            def v_s2(vector):
                wait(vector, "act")
                vector.drain()
                vector.tensor_scalar(STT[:, 48:56], STT[:, 48:56], 127.0, None, ALU.mult)
                vector.drain()
                vector.tensor_copy(STT[:, 16:24], STT[:, 48:56])
                vector.tensor_scalar(STT[:, 56:64], STT[:, 40:48], 1.0 / 127.0,
                                     None, ALU.mult)
                vector.drain()
                inc("dve", vector.tensor_scalar(STT[:, 56:64], STT[:, 56:64],
                                                TAU[:, descale_tau_col:descale_tau_col + 1],
                                                None, ALU.mult))
            on("vector", v_s2)

            def s_r(sync):
                wait(sync, "dve")
                incd("tp", sync.dma_start(
                    out=bass.AP(row_hbm.tensor, 0, [[1, 128], [128, 8]]),
                    in_=STT[:, 16:24], allow_slow_non_contiguous=True))
                incd("tp", sync.dma_start(
                    out=bass.AP(row_hbm.tensor, TPC, [[1, 128], [128, 8]]),
                    in_=STT[:, 56:64], allow_slow_non_contiguous=True))
                wait(sync, "tp")
                incd("tp", sync.dma_start(
                    out=ROW[0:1, 0:TPC],
                    in_=bass.AP(row_hbm.tensor, 0, [[1, TPC]])))
                incd("tp", sync.dma_start(
                    out=ROW[0:1, TPC:2 * TPC],
                    in_=bass.AP(row_hbm.tensor, TPC, [[1, TPC]])))
                wait(sync, "tp")
                inc("aux", sync.nop())
            on("sync", s_r)
            kb.barrier()
        def emit_mmK(li, w_panels, src_hbm, src_fp16, quantize, tern3):
            with ExitStack() as c2:
                BIG = c2.enter_context(nc.sbuf_tensor(sbname("mB"), [128, 64 * 512], BF16))
                CUB = c2.enter_context(nc.sbuf_tensor(sbname("mC"), [128, TPC], F32))
                CUB2 = c2.enter_context(nc.sbuf_tensor(sbname("mC2"), [128, 1024], F32))
                DDB = c2.enter_context(nc.sbuf_tensor(sbname("mD"), [128, TPC], F32))
                UT = c2.enter_context(nc.sbuf_tensor(sbname("mU"), [128, 2 * 1024],
                                                     FP16 if src_fp16 else BF16))
                R32 = c2.enter_context(nc.sbuf_tensor(sbname("mR3"), [128, 1024], F32))
                R64 = c2.enter_context(nc.sbuf_tensor(sbname("mR6"), [128, 1024], F32))
                WQF = c2.enter_context(nc.sbuf_tensor(sbname("mWQ"), [128, 2048], F32))
                SA = c2.enter_context(nc.sbuf_tensor(sbname("mSA"), [128, 512], F32))
                SB_ = c2.enter_context(nc.sbuf_tensor(sbname("mSB"), [128, 512], F32))
                TD = c2.enter_context(nc.sbuf_tensor(sbname("mTD"), [128, 64 * 128], BF16))
                XR = c2.enter_context(nc.sbuf_tensor(sbname("mXR"), [128, 2 * 512], F32))
                T2 = c2.enter_context(nc.sbuf_tensor(sbname("mT2"), [128, 512], F32))
                psO2s = tuple(c2.enter_context(nc.psum_tensor(sbname("mPO"), [128, 512], F32))
                              for _ in range(2))
                psBC = c2.enter_context(nc.psum_tensor(sbname("mPB"), [128, 2 * TPC], F32))

                def s_sp(sync):
                    wait(sync, "ph")
                    incd("tp", sync.dma_start(
                        out=bass.AP(x_hbm.tensor, 0, [[ND * TPC, 128], [1, ND * TPC]]),
                        in_=X[:, :]))
                    wait(sync, "tp")
                    inc("aux", sync.nop())
                on("sync", s_sp)

                def t_bc(tensor):
                    wait(tensor, "ph")
                    for h2 in range(2):
                        inc("mm", tensor.matmul(
                            psBC[:, h2 * 512:(h2 + 1) * 512], ONE1[0:1, :],
                            ROW[0:1, h2 * 512:(h2 + 1) * 512], start=True, stop=True))
                        inc("mm", tensor.matmul(
                            psBC[:, TPC + h2 * 512:TPC + (h2 + 1) * 512], ONE1[0:1, :],
                            ROW[0:1, TPC + h2 * 512:TPC + (h2 + 1) * 512],
                            start=True, stop=True))
                on("tensor", t_bc)

                def v_bc(vector):
                    wait(vector, "mm")
                    vector.drain()
                    vector.tensor_copy(CUB[:, :], psBC[:, 0:TPC])
                    inc("dve", vector.tensor_copy(DDB[:, :], psBC[:, TPC:2 * TPC]))
                on("vector", v_bc)

                for half in range(2):
                    if quantize:
                        def v_c2(vector, half=half):
                            wait(vector, "dve")
                            vector.drain()
                            vector.tensor_copy(CUB2[:, 0:512],
                                               CUB[:, half * 512:(half + 1) * 512])
                            vector.drain()
                            inc("dve", vector.tensor_copy(
                                CUB2[:, 512:1024],
                                CUB[:, half * 512:(half + 1) * 512]))
                        on("vector", v_c2)
                    tp0 = (n_("tpA"), n_("tpB"))
                    for kp in range(32):
                        def s_tp(sync, kp=kp, half=half):
                            if half == 1 and kp == 0:
                                wait(sync, "mm", n_("mm"))
                            if kp >= 2:
                                wait(sync, "dve", n_("dve"))
                            for sub in range(2):
                                kc = 2 * kp + sub
                                incd("tpA" if kp % 2 == 0 else "tpB",
                                     sync.dma_start_transpose(
                                         (UT[:, (kp % 2) * 1024 + sub * 512:
                                             (kp % 2) * 1024 + (sub + 1) * 512]
                                          if quantize else
                                          BIG[:, kc * 512:(kc + 1) * 512]),
                                         bass.AP(src_hbm.tensor,
                                                 kc * TPC * 128 + half * 512 * 128,
                                                 [[128, 512], [1, 128]])))
                        on("sync", s_tp)

                        if quantize:
                            def v_q(vector, kp=kp, half=half, tp0=tp0):
                                wait(vector, "tpA" if kp % 2 == 0 else "tpB",
                                     tp0[kp % 2] + 32 * (kp // 2 + 1))
                                if kp == 0:
                                    wait(vector, "dve")
                                usl = slice((kp % 2) * 1024, (kp % 2 + 1) * 1024)
                                vector.drain()
                                vector.tensor_tensor(R64[:, :], UT[:, usl],
                                                     CUB2[:, :], ALU.mult)
                                vector.drain()
                                vector.tensor_scalar(R32[:, :], R64[:, :], CS[:, 0:1],
                                                     None, ALU.add)
                                vector.drain()
                                vector.tensor_scalar(R32[:, :], R32[:, :], CS[:, 0:1],
                                                     None, ALU.subtract)
                                vector.tensor_scalar(R64[:, :], R64[:, :], CS[:, 1:2],
                                                     None, ALU.add)
                                vector.drain()
                                vector.tensor_scalar(R64[:, :], R64[:, :], CS[:, 1:2],
                                                     None, ALU.subtract)
                                vector.drain()
                                vector.tensor_tensor(R64[:, :], R64[:, :], R32[:, :],
                                                     ALU.subtract)
                                vector.drain()
                                inc("dve", vector.scalar_tensor_tensor(
                                    bass.AP(BIG, 2 * kp * 512,
                                            [[64 * 512, 128], [1, 1024]]),
                                    R64[:, :], IND[:, 0:1], R32[:, :],
                                    ALU.mult, ALU.add))
                            on("vector", v_q)
                    big_t = (n_("dve"), n_("tpA"), n_("tpB"))

                    wq0 = (n_("stA"), n_("stB"))
                    for o2c in range(16):
                        gidx = half * 16 + o2c
                        if tern3 and half == 0:
                            for q in range(4):
                                def s_w(sync, o2c=o2c, q=q):
                                    if o2c + q > 0:
                                        wait(sync, "dve", n_("dve"))
                                    incd("shA", sync.dma_start(
                                        out=bass.AP(WQF, 0,
                                                    [[2048, 128], [128, 16], [1, 128]]),
                                        in_=bass.AP(w_panels.tensor,
                                                    o2c * F2 * 128 + q * 2048 * 128,
                                                    [[128, 128], [128 * 128, 16], [1, 128]])))
                                on("sync", s_w)
                                def v_t(vector, o2c=o2c, q=q):
                                    wait(vector, "shA", n_("shA"))
                                    if q == 0:
                                        wait(vector, "mm", n_("mm"))
                                        wait(vector, "tp", n_("tp"))
                                    for sl4 in range(4):
                                        ssl = slice(sl4 * 512, (sl4 + 1) * 512)
                                        vector.drain()
                                        vector.tensor_scalar(SA[:, :], WQF[:, ssl],
                                                             TAU[:, 2:3], None, ALU.is_ge)
                                        vector.tensor_scalar(SB_[:, :], WQF[:, ssl],
                                                             TAU[:, 3:4], None, ALU.is_le)
                                        vector.drain()
                                        vector.tensor_tensor(
                                            TD[:, q * 2048 + sl4 * 512:
                                               q * 2048 + (sl4 + 1) * 512],
                                            SA[:, :], SB_[:, :], ALU.subtract)
                                    vector.drain()
                                    inc("dve", vector.nop())
                                on("vector", v_t)
                            def s_tc(sync, o2c=o2c):
                                wait(sync, "dve")
                                incd("tp", sync.dma_start(
                                    out=bass.AP(tern_hbm.tensor, o2c * F2 * 128,
                                                [[128, 128], [128 * 128, 64], [1, 128]]),
                                    in_=bass.AP(TD, 0, [[64 * 128, 128], [128, 64], [1, 128]])))
                            on("sync", s_tc)
                            td_t = ("dve", n_("dve"))
                        elif tern3:
                            def s_w(sync, o2c=o2c):
                                wait(sync, "tp", n_("tp"))
                                if o2c > 0:
                                    wait(sync, "mm", n_("mm"))
                                incd("tp", sync.dma_start(
                                    out=bass.AP(TD, 0, [[64 * 128, 128], [128, 64], [1, 128]]),
                                    in_=bass.AP(tern_hbm.tensor, o2c * F2 * 128,
                                                [[128, 128], [128 * 128, 64], [1, 128]])))
                            on("sync", s_w)
                            td_t = ("tp", n_("tp"))
                        else:
                            for q in range(4):
                                def s_w(sync, o2c=o2c, q=q):
                                    if o2c + q > 0:
                                        wait(sync, "dve", n_("dve"))
                                    incd("shA", sync.dma_start(
                                        out=bass.AP(WQF, 0,
                                                    [[2048, 128], [128, 16], [1, 128]]),
                                        in_=bass.AP(w_panels.tensor,
                                                    o2c * F2 * 128 + q * 2048 * 128,
                                                    [[128, 128], [128 * 128, 16], [1, 128]])))
                                on("sync", s_w)
                                def v_t(vector, o2c=o2c, q=q):
                                    wait(vector, "shA", n_("shA"))
                                    if q == 0:
                                        wait(vector, "mm", n_("mm"))
                                    vector.drain()
                                    inc("dve", vector.tensor_scalar(
                                        TD[:, q * 2048:(q + 1) * 2048],
                                        WQF[:, 0:2048], 0.0, None, ALU.is_ge))
                                on("vector", v_t)
                            td_t = ("dve", n_("dve"))

                        def t_m(tensor, half=half, o2c=o2c, gidx=gidx,
                                td_t=td_t, big_t=big_t):
                            wait(tensor, td_t[0], td_t[1])
                            if quantize:
                                wait(tensor, "dve", big_t[0])
                            wait(tensor, "tpA", big_t[1])
                            wait(tensor, "tpB", big_t[2])
                            if gidx >= 2:
                                wait(tensor, "ev", n_("ev"))
                            for kc in range(64):
                                tensor.matmul(
                                    psO2s[gidx % 2][:, :],
                                    TD[:, kc * 128:(kc + 1) * 128],
                                    BIG[:, kc * 512:(kc + 1) * 512],
                                    start=(kc == 0), stop=(kc == 63))
                            tensor.drain()
                            inc("mm", tensor.nop(), 64)
                        on("tensor", t_m)
                        mmv = n_("mm")

                        def s_x(sync, half=half, o2c=o2c, gidx=gidx):
                            wait(sync, "aux")
                            if gidx >= 2:
                                wait(sync, "ev", n_("ev"))
                            incd("sgA" if gidx % 2 == 0 else "sgB", sync.dma_start(
                                out=XR[:, (gidx % 2) * 512:(gidx % 2 + 1) * 512],
                                in_=bass.AP(x_hbm.tensor, o2c * TPC + half * 512,
                                            [[ND * TPC, 128], [1, 512]])))
                        on("sync", s_x)
                        sg_t = (n_("sgA"), n_("sgB"))

                        def v_e(vector, half=half, o2c=o2c, gidx=gidx, mmv=mmv, sg_t=sg_t):
                            wait(vector, "mm", mmv)
                            wait(vector, "sgA" if gidx % 2 == 0 else "sgB",
                                 sg_t[gidx % 2])
                            vector.drain()
                            if tern3:
                                vector.tensor_tensor(
                                    T2[:, :], psO2s[gidx % 2][:, :],
                                    DDB[:, half * 512:(half + 1) * 512], ALU.mult)
                            else:
                                vector.scalar_tensor_tensor(
                                    T2[:, :], psO2s[gidx % 2][:, :],
                                    2.0, DDB[:, half * 512:(half + 1) * 512],
                                    ALU.mult, ALU.subtract)
                            vector.drain()
                            inc("ev", vector.tensor_tensor(
                                X[:, o2c * TPC + half * 512:o2c * TPC + (half + 1) * 512],
                                XR[:, (gidx % 2) * 512:(gidx % 2 + 1) * 512],
                                T2[:, :], ALU.add))
                        on("vector", v_e)
            kb.barrier()

        def emit_gu():
            PW = 256
            with ExitStack() as c2:
                WF = c2.enter_context(nc.sbuf_tensor(sbname("hWF"), [128, 2 * ND * PW], F32))
                WG = c2.enter_context(nc.sbuf_tensor(sbname("hWG"), [128, 2 * ND * PW], F32))
                TV = c2.enter_context(nc.sbuf_tensor(sbname("hTV"), [128, 2 * ND * PW], BF16))
                TG = c2.enter_context(nc.sbuf_tensor(sbname("hTG"), [128, 2 * ND * PW], BF16))
                GS = c2.enter_context(nc.sbuf_tensor(sbname("hGS"), [128, 2 * PW], F32))
                US = c2.enter_context(nc.sbuf_tensor(sbname("hUS"), [128, PW], F32))
                PO = c2.enter_context(nc.sbuf_tensor(sbname("hPO"), [128, 2 * PW], BF16))
                SG2 = c2.enter_context(nc.sbuf_tensor(sbname("hS2"), [128, 8], F32))
                BG2 = c2.enter_context(nc.sbuf_tensor(sbname("hB2"), [128, 8], F32))
                psGs = tuple(c2.enter_context(nc.psum_tensor(sbname("hPG"), [128, PW], F32))
                             for _ in range(2))
                psUs = tuple(c2.enter_context(nc.psum_tensor(sbname("hPU"), [128, PW], F32))
                             for _ in range(2))
                wl0 = (n_("shA"), n_("shB"))
                u0 = (n_("sgA"), n_("sgB"))
                evb = n_("ev")
                tern_done = {}

                def v_pre(vector):
                    wait(vector, "ph")
                    vector.drain()
                    vector.tensor_scalar(SG2[:, :], STT[:, 8:16], 2.0, None, ALU.mult)
                    vector.tensor_tensor(BG2[:, :], STT[:, 24:32], STT[:, 8:16], ALU.mult)
                    vector.drain()
                    vector.tensor_scalar(BG2[:, :], BG2[:, :], -1.0, None, ALU.mult)
                    vector.memset(STT[:, 56:64], 0.0)
                    vector.drain()
                    inc("dve", vector.nop())
                on("vector", v_pre)

                for p in range(32):
                    def s_w(sync, p=p):
                        if p >= 2:
                            wait(sync, "dve", tern_done[p - 2])
                        incd("shA" if p % 2 == 0 else "shB", sync.dma_start(
                            out=bass.AP(WF, (p % 2) * ND * PW,
                                        [[2 * ND * PW, 128], [PW, ND], [1, PW]]),
                            in_=bass.AP(wgT.tensor, PW * p,
                                        [[F2, 128], [128 * F2, ND], [1, PW]])))
                        incd("shA" if p % 2 == 0 else "shB", sync.dma_start(
                            out=bass.AP(WG, (p % 2) * ND * PW,
                                        [[2 * ND * PW, 128], [PW, ND], [1, PW]]),
                            in_=bass.AP(wuT.tensor, PW * p,
                                        [[F2, 128], [128 * F2, ND], [1, PW]])))
                    on("sync", s_w)

                    def v_t(vector, p=p):
                        wait(vector, "shA" if p % 2 == 0 else "shB",
                             wl0[p % 2] + 32 * (p // 2 + 1))
                        if p >= 2:
                            wait(vector, "mm", n_("mm") - 256)
                        base = (p % 2) * ND * PW
                        vector.drain()
                        vector.tensor_scalar(TV[:, base:base + ND * PW],
                                             WF[:, base:base + ND * PW],
                                             0.0, None, ALU.is_ge)
                        vector.tensor_scalar(TG[:, base:base + ND * PW],
                                             WG[:, base:base + ND * PW],
                                             0.0, None, ALU.is_ge)
                        vector.drain()
                        inc("dve", vector.nop())
                    on("vector", v_t)
                    tern_t = n_("dve")
                    tern_done[p] = tern_t

                    for tt in range(NT):
                        idx = p * NT + tt
                        def t_m(tensor, p=p, tt=tt, idx=idx, tern_t=tern_t):
                            if tt == 0:
                                wait(tensor, "dve", tern_t)
                            if idx >= 2:
                                wait(tensor, "ev", evb + idx - 1)
                            for dc in range(ND):
                                tensor.matmul(
                                    psGs[tt % 2][:, :],
                                    XN[:, dc * TPC + tt * 128:dc * TPC + (tt + 1) * 128],
                                    TV[:, (p % 2) * ND * PW + dc * PW:
                                        (p % 2) * ND * PW + (dc + 1) * PW],
                                    start=(dc == 0), stop=(dc == ND - 1))
                                tensor.matmul(
                                    psUs[tt % 2][:, :],
                                    XN[:, dc * TPC + tt * 128:dc * TPC + (tt + 1) * 128],
                                    TG[:, (p % 2) * ND * PW + dc * PW:
                                        (p % 2) * ND * PW + (dc + 1) * PW],
                                    start=(dc == 0), stop=(dc == ND - 1))
                            tensor.drain()
                            inc("mm", tensor.nop(), 32)
                        on("tensor", t_m)
                        mmv = n_("mm")

                        def s_g(scalar, tt=tt, idx=idx, mmv=mmv):
                            wait(scalar, "mm", mmv)
                            if idx >= 2:
                                wait(scalar, "ev", evb + idx - 1)
                            scalar.drain()
                            inc("act", scalar.activation(
                                GS[:, (idx % 2) * PW:(idx % 2 + 1) * PW],
                                psGs[tt % 2][:, :],
                                AF.Silu, scale=SG2[:, tt:tt + 1], bias=BG2[:, tt:tt + 1]))
                        on("scalar", s_g)
                        actv = n_("act")

                        def v_u(vector, p=p, tt=tt, idx=idx, actv=actv):
                            wait(vector, "act", actv)
                            if idx >= 2:
                                wait(vector, "sgA" if idx % 2 == 0 else "sgB",
                                     u0[idx % 2] + 16 * ((idx - 2) // 2 + 1))
                            vector.drain()
                            vector.tensor_scalar(
                                US[:, :], psUs[tt % 2][:, :],
                                SG2[:, tt:tt + 1], BG2[:, tt:tt + 1], ALU.mult, ALU.add)
                            vector.drain()
                            vector.tensor_tensor(
                                PO[:, (idx % 2) * PW:(idx % 2 + 1) * PW],
                                GS[:, (idx % 2) * PW:(idx % 2 + 1) * PW],
                                US[:, :], ALU.mult)
                            vector.drain()
                            vector.reduce_sum(STT[:, 48 + tt:49 + tt],
                                              PO[:, (idx % 2) * PW:(idx % 2 + 1) * PW],
                                              axis=AX.X)
                            vector.drain()
                            inc("dve", vector.tensor_tensor(
                                STT[:, 56 + tt:57 + tt], STT[:, 56 + tt:57 + tt],
                                STT[:, 48 + tt:49 + tt], ALU.add))
                            inc("ev", vector.nop())
                        on("vector", v_u)

                        def s_u(sync, p=p, tt=tt, idx=idx):
                            wait(sync, "ev")
                            incd("sgA" if idx % 2 == 0 else "sgB", sync.dma_start(
                                out=bass.AP(p_hbm.tensor,
                                            2 * p * TPC * 128 + tt * 128 * 128,
                                            [[128, 128], [TPC * 128, 2], [1, 128]]),
                                in_=bass.AP(PO, (idx % 2) * PW,
                                            [[2 * PW, 128], [128, 2], [1, 128]])))
                        on("sync", s_u)

                def s_wd(sync):
                    wait(sync, "sgA")
                    wait(sync, "sgB")
                    inc("aux", sync.nop())
                on("sync", s_wd)

                def s_r(sync):
                    wait(sync, "dve")
                    incd("tp", sync.dma_start(
                        out=bass.AP(row_hbm.tensor, TPC, [[1, 128], [128, 8]]),
                        in_=STT[:, 56:64], allow_slow_non_contiguous=True))
                    wait(sync, "tp")
                    incd("tp", sync.dma_start(
                        out=ROW[0:1, TPC:2 * TPC],
                        in_=bass.AP(row_hbm.tensor, TPC, [[1, TPC]])))
                    wait(sync, "tp")
                    inc("aux", sync.nop())
                on("sync", s_r)
            kb.barrier()

        with ExitStack() as cL0:
            XN = cL0.enter_context(nc.sbuf_tensor(sbname("XNa"), [128, ND * TPC], BF16))
            emit_ln(X, False, 3, fold_R=False, h_out=None)
            tap("xn1", XN[:, :])
            if STAGE >= 2:
                with ExitStack() as cQ0:
                    QT = cQ0.enter_context(
                        nc.sbuf_tensor(sbname("QTa"), [128, ND * TPC], BF16))
                    emit_proj(WQ[0], 'fm_sb', 0, bias_col=0)
                    emit_proj(WK[0], 'fm', 0, bias_col=ND, dest=ag1_in[0], dest_off=0)
                    emit_proj(WV[0], 'tm', 0, dest=ag1_in[0], dest_off=V_OFF)
                    if STAGE >= 3:
                        emit_attn(0, causal=False)
                        tap("attnT0", XN[:, :])
            if STAGE >= 4:
                emit_proj(WO[0], 'res', 0, bias_col=3 * ND)
                tap("x1", X[:, :])
            if STAGE >= 5:
                emit_ln(X, False, 3, fold_R=True, h_out='h')
                emit_hquant(use_rsq=True, use_eps=True, rowsum=False)
                tap("qi0", XN[:, :])
                tap("stt", STT[:, :])
            if STAGE >= 6:
                emit_gv()
                emit_ustats(descale_tau_col=5)
        if STAGE >= 7:
            emit_mmK(0, woutP, u_hbm, src_fp16=True, quantize=True, tern3=True)
            tap("x2", X[:, :])
        if STAGE >= 8:
            with ExitStack() as cL1:
                XN = cL1.enter_context(nc.sbuf_tensor(sbname("XNb"), [128, ND * TPC], BF16))
                emit_ln(X, False, 3, fold_R=False, h_out=None)
                with ExitStack() as cQ1:
                    QT = cQ1.enter_context(
                        nc.sbuf_tensor(sbname("QTb"), [128, ND * TPC], BF16))
                    emit_proj(WQ[1], 'fm_sb', 1, bias_col=None)
                    tap("qt1", QT[:, :])
                    emit_proj(WK[1], 'fm', 1, bias_col=None, dest=ag1_in[1], dest_off=0)
                    emit_proj(WV[1], 'tm', 1, dest=ag1_in[1], dest_off=V_OFF)
                    emit_attn(1, causal=True)
                emit_proj(WO[1], 'res', 1, bias_col=None)
                tap("x3", X[:, :])
                if STAGE >= 9:
                    emit_ln(X, False, 3, fold_R=False, h_out=None)
                    emit_ln(XN, True, 4, fold_R=False, h_out='xq')
                    emit_hquant(use_rsq=False, use_eps=False, rowsum=True)
                    emit_gu()
        if STAGE >= 9:
            emit_mmK(1, wdP, p_hbm, src_fp16=False, quantize=False, tern3=False)

        kb.barrier()

        def fin(sync):
            incd("st", sync.dma_start(
                out=bass.AP(y_t.tensor, 0, [[TPC, 128], [128 * TPC, ND], [1, TPC]]),
                in_=bass.AP(X, 0, [[ND * TPC, 128], [TPC, ND], [1, TPC]])))
            wait(sync, "st")
        on("sync", fin)
    return nc


def shard_inputs(inputs):
    x = np.asarray(inputs["L0_attn_hidden_state"], np.float32)
    T = lambda a: np.ascontiguousarray(np.asarray(a, np.float32).T)
    C = lambda a: np.ascontiguousarray(np.asarray(a, np.float32))
    wp = np.asarray(inputs["L1_attn_w_project"], np.float32)
    wout_T = T(inputs["L0_ffn_w_out"])
    wd_T = T(inputs["L1_ffn_W_d"])
    woutP = np.ascontiguousarray(wout_T.reshape(F2, 16, 128).transpose(1, 0, 2)
                                 ).reshape(16 * F2, 128)
    wdP = np.ascontiguousarray(wd_T.reshape(F2, 16, 128).transpose(1, 0, 2)
                               ).reshape(16 * F2, 128)
    cstv = np.zeros((128, 12), np.float32)
    cstv[:, 0] = C32; cstv[:, 1] = C64; cstv[:, 2] = 0.75
    cstv[:, 3] = 1e-5; cstv[:, 4] = 1e-8; cstv[:, 5] = 1.0 / 2048.0
    cstv[:, 6] = 127.0
    cstv[:, 8] = -8.0
    cstv[:, 9] = -48.0
    cstv[:, 10] = np.float32(127 * np.log(2.0) + np.log(2.0 / 3.0))

    def colmaj(v):
        return np.ascontiguousarray(np.asarray(v, np.float32).reshape(ND, 128).T)
    shared = dict(
        cst=cstv,
        wq0=C(inputs["L0_attn_Wq"]), wk0=C(inputs["L0_attn_Wk"]),
        wv0=C(inputs["L0_attn_Wv"]), wo0=C(inputs["L0_attn_Wo"]),
        wgvT=T(inputs["L0_ffn_w_gv"]), woutP=woutP,
        wq1=T(wp[0:D]), wk1=T(wp[D:2 * D]), wv1=T(wp[2 * D:3 * D]),
        wo1=T(inputs["L1_attn_w_out"]),
        wgT=T(inputs["L1_ffn_W_g"]), wuT=T(inputs["L1_ffn_W_u"]), wdP=wdP)
    bq = colmaj(inputs["L0_attn_bq"]); bk = colmaj(inputs["L0_attn_bk"])
    bo = colmaj(inputs["L0_attn_bo"]); wn = colmaj(inputs["L0_ffn_w_norm"])
    bv = colmaj(inputs["L0_attn_bv"])
    vecs_shared = np.concatenate([bq, bk, bv, bo, wn], axis=1)
    maps = []
    for c in range(8):
        g, l = divmod(c, 4)
        vecsv = vecs_shared
        negdv = np.zeros((128, 32), np.float32)
        for ck in range(32):
            negdv[:, ck] = 128 * ck - 1024 * l
        ofsv = np.zeros((1, 96), np.int64)
        for hh in range(4):
            for r in range(4):
                col = 8 + (hh * 4 + r) * 3
                base = l * 4 * HD * TPC + r * BLK1 + hh * 128 * TPC
                ofsv[0, col] = base
                ofsv[0, col + 1] = base + D * TPC
                ofsv[0, col + 2] = l * 4 * HD + r * BLK1 + V_OFF + hh * 128
        for dc in range(ND):
            ofsv[0, 56 + dc] = l * TPC + dc * 128 * S
        for i in range(8):
            ofsv[0, 72 + i] = c * 256 * F1 + (i // 4) * 128 * F1 + (i % 4) * 4096
        for i in range(4):
            ofsv[0, 80 + i] = c * 16384 * 128 + i * 32 * 128 * 128
        m = dict(shared)
        m["vecs"] = vecsv
        m["negd"] = negdv
        m["x_t"] = np.ascontiguousarray(x[g, l * TPC:(l + 1) * TPC, :].T)
        m["ofs"] = ofsv.astype(np.int32)
        maps.append(m)
    return maps


def kernel(**inputs):
    maps = shard_inputs(inputs)
    nc = build_program(taps=TAPS)
    res = run_bass_kernel_spmd(nc, maps, core_ids=list(range(8)))
    out = np.zeros((B, S, D), np.float32)
    for c in range(8):
        g, l = divmod(c, 4)
        out[g, l * TPC:(l + 1) * TPC, :] = np.asarray(res.results[c]["y_t"]).T
    return out
